# version 17
# baseline (speedup 1.0000x reference)
import numpy as np
from contextlib import ExitStack

import concourse.bass as bass
import concourse.mybir as mybir
from concourse.bass_utils import run_bass_kernel_spmd

F32 = mybir.dt.float32
F16 = mybir.dt.float16
F32R = mybir.dt.float32r
AF = mybir.ActivationFunctionType
ALU = mybir.AluOpType

P = 128
NDC = 8
D = 1024
TT = 256
KAP = 512
CH = 4096
NT_A = KAP // TT + CH // TT + 1
ROWS = KAP + CH + TT
EPS = 1e-5
LN2C = float(np.log(2.0))
C52 = float(2.0 ** 52)
C23 = float(2.0 ** 23)


def build_nc(nb=16, na=NT_A, ksa=9, kpr=0, kdbg="", kdbt=2):
    nc = bass.Bass("TRN2", target_bir_lowering=False, dynamic_dma_scratch_size=8192)

    dx = nc.dram_tensor("x", [D, ROWS], F32R, kind="ExternalInput").ap()
    dwt = nc.dram_tensor("wt", [P, NDC, D], F32, kind="ExternalInput").ap()
    dbwt = nc.dram_tensor("bwt", [P, NDC, D], F32, kind="ExternalInput").ap()
    drt = nc.dram_tensor("rt", [P, NDC, 8], F32, kind="ExternalInput").ap()
    dla = nc.dram_tensor("la", [P, NDC, 8], F32, kind="ExternalInput").ap()
    dlb = nc.dram_tensor("lb", [64, NDC, P], F32, kind="ExternalInput").ap()
    dcw = nc.dram_tensor("cw", [P, NDC, 5], F32, kind="ExternalInput").ap()
    ddec = nc.dram_tensor("dec", [P, NDC], F32, kind="ExternalInput").ap()
    dba = nc.dram_tensor("ba", [P, NDC], F32, kind="ExternalInput").ap()
    dcb = nc.dram_tensor("cb", [P, NDC], F32, kind="ExternalInput").ap()
    dbb = nc.dram_tensor("bb", [P, NDC], F32, kind="ExternalInput").ap()
    drb = nc.dram_tensor("rb", [8, 1], F32, kind="ExternalInput").ap()
    dinvd = nc.dram_tensor("invd", [P, P], F32R, kind="ExternalInput").ap()
    dinvdf = nc.dram_tensor("invdf", [P, P], F32, kind="ExternalInput").ap()
    donesz = nc.dram_tensor("onesz", [8, P], F32, kind="ExternalInput").ap()
    dbrep = nc.dram_tensor("brep", [8, 64], F32, kind="ExternalInput").ap()
    drrep = nc.dram_tensor("rrep", [8, 64], F32, kind="ExternalInput").ap()
    dout = nc.dram_tensor("out", [D, CH], F32, kind="ExternalOutput").ap()
    ddbg = nc.dram_tensor("dbg", [P, NDC, TT + 4], F32, kind="ExternalOutput").ap()

    es = ExitStack()

    def sb(name, shape, dt=F32):
        return es.enter_context(nc.sbuf_tensor(name, shape, dt))

    def psf(name):
        return es.enter_context(nc.psum_tensor(name, [P, 512], F32))

    with es:
        wh = sb("wh", [P, NDC, NDC, P], F16)
        wl = sb("wl", [P, NDC, NDC, P], F16)
        bh = sb("bh", [P, NDC, NDC, P], F16)
        bl = sb("bl", [P, NDC, NDC, P], F16)
        sph = sb("sph", [P, NDC, 40], F16)
        spl = sb("spl", [P, NDC, 40], F16)
        lbh = sb("lbh", [64, NDC, P], F16)
        lbl = sb("lbl", [64, NDC, P], F16)
        cwp = sb("cwp", [P, NDC, 5])
        s_a = sb("s_a", [P, NDC])
        s_oma = sb("s_oma", [P, NDC])
        s_ioma = sb("s_ioma", [P, NDC])
        biasmix = sb("biasmix", [P, NDC])
        s_cb = sb("s_cb", [P, NDC])
        s_bb = sb("s_bb", [P, NDC])
        s_rb = sb("s_rb", [8, 1])
        s_invd = sb("s_invd", [P, P], F32R)
        s_onesz = sb("s_onesz", [8, P])
        s_brep = sb("s_brep", [8, 64])
        s_rrep = sb("s_rrep", [8, 64])
        epsb = sb("epsb", [P, 1])
        b151 = sb("b151", [8, 1])
        bm135 = sb("bm135", [8, 1])
        bufx = [sb(f"bufx{i}", [P, NDC, TT], F32R) for i in range(3)]
        xpad = [sb(f"xpad{i}", [P, NDC, TT + 4]) for i in range(2)]
        lstash = sb("lstash", [P, NDC, 2])
        sS = sb("sS", [P, NDC, TT])
        carry = sb("carry", [P, NDC, 1])
        shi = sb("shi0", [P, NDC, TT], F16)
        slo = sb("slo0", [P, NDC, TT], F16)
        sqb = sb("sqb", [P, NDC, TT], F32R)
        statr = sb("statr", [P, NDC, TT], F32R)
        convb = sb("convb", [P, NDC, TT])
        convb2 = sb("convb2", [P, NDC, TT])
        x2b = [sb("x2b0", [P, NDC, TT]), sb("x2b1", [P, NDC, TT])]
        hfb = sb("hfb", [P, NDC, TT])
        hh = sb("hh", [P, NDC, TT], F16)
        hl = sb("hl", [P, NDC, TT], F16)
        tmA = sb("tmA", [P, TT])
        tmB = sb("tmB", [P, TT])
        mcA = sb("mcA", [P, TT])
        mcB = sb("mcB", [P, TT])
        e8b = sb("e8b", [8, TT])
        lnzb = sb("lnzb", [8, TT])
        gb = [sb(f"gb{i}", [8, TT]) for i in range(3)]
        ind8 = sb("ind8", [8, 1])
        omi8 = sb("omi8", [8, 1])
        ln2t = sb("ln2t", [8, 1])
        nln2t = sb("nln2t", [8, 1])
        rln2t = sb("rln2t", [8, 1])
        outerb = sb("outerb", [64, TT])
        oh = sb("oh", [64, TT], F16)
        ol = sb("ol", [64, TT], F16)
        wsc = [convb[:, :, 0:P], convb[:, :, P:2 * P]]
        s_invdf = x2b[0][:, 0, 0:P].rearrange("p t -> p t")
        s_dec = x2b[0][:, 1, 0:NDC]
        s_cw = x2b[0][:, 2, 0:40].rearrange("p (dc k) -> p dc k", k=5)
        pstat = [psf("pstat0"), psf("pstat1")]
        pmm = [psf(f"pmm{i}") for i in range(4)]
        ppk = psf("ppk")
        pzb = psf("pzb")

        sems = {k: es.enter_context(nc.semaphore(f"sem_{k}"))
                for k in ["v", "s", "t", "g", "dc", "dwA", "dwB",
                          "dx0", "dx1", "dx2", "dst", "o0", "o1"]}
        block = es.enter_context(nc.Block())

        def bc8(ap2d):
            return ap2d.unsqueeze(1).broadcast_to([P, NDC, TT])

        class Env:
            def __init__(self, active, eng):
                self.active = active
                self.eng = eng
                self.cnt = {k: 0 for k in sems}
                self.cp = {}
                self.last = None

            def emit(self, engine, fn):
                if self.active == engine:
                    self.last = fn(self.eng)

            def chk(self, engine, label, dma=False, via=None):
                self.cnt[engine] += 16 if dma else 1
                assert label not in self.cp, label
                self.cp[label] = (engine, self.cnt[engine])
                owner = via if via is not None else engine
                if self.active == owner:
                    self.last.then_inc(sems[engine], 16 if dma else 1)

            def wait(self, label, who=None):
                if who is not None and self.active != who:
                    return
                src, val = self.cp[label]
                if src == self.active:
                    return
                self.eng.wait_ge(sems[src], val)

            def nop(self, engine):
                if self.active == engine:
                    self.eng.engine_nop()

            def selfsync(self, engine):
                if self.active == engine:
                    self.eng.drain()

        def prep(env):
            def dma(dst, src, label, sem="dc"):
                env.emit("d", lambda e: e.dma_start(out=dst, in_=src))
                env.chk(sem, label, dma=True, via="d")

            dma(s_invd[:], dinvd, "d_invd")
            dma(s_invdf, dinvdf, "d_invdf")
            dma(s_dec, ddec, "d_dec")
            dma(s_cw, dcw, "d_cw")
            dma(biasmix[:], dba, "d_ba")
            dma(s_cb[:], dcb, "d_cb")
            dma(s_bb[:], dbb, "d_bb")
            dma(s_rb[:], drb, "d_rb")
            dma(s_onesz[:], donesz, "d_onesz")
            dma(s_brep[:], dbrep, "d_brep")
            dma(s_rrep[:], drrep, "d_rrep")
            env.cp["d_consts"] = ("dc", env.cnt["dc"])

            env.wait("d_consts", who="s")
            env.emit("s", lambda e: e.activation(s_a[:], s_dec, AF.Sigmoid))
            env.chk("s", "a_done")

            env.emit("v", lambda e: e.memset(ind8[:], 1.0))
            env.emit("v", lambda e: e.drain())
            env.emit("v", lambda e: e.tensor_single_scalar(ind8[:], ind8[:], float(2.0 ** -40), ALU.add))
            env.emit("v", lambda e: e.drain())
            env.emit("v", lambda e: e.tensor_single_scalar(ind8[:], ind8[:], 1.0, ALU.subtract))
            env.emit("v", lambda e: e.drain())
            env.emit("v", lambda e: e.tensor_single_scalar(ind8[:], ind8[:], float(2.0 ** 40), ALU.mult))
            env.emit("v", lambda e: e.drain())
            env.emit("v", lambda e: e.tensor_scalar(omi8[:], ind8[:], -1.0, 1.0, ALU.mult, ALU.add))
            env.emit("v", lambda e: e.memset(epsb[:], EPS))
            env.emit("v", lambda e: e.memset(carry[:], 0.0))
            env.emit("v", lambda e: e.memset(ln2t[:], 2.0))
            env.emit("v", lambda e: e.drain())
            env.wait("a_done", who="v")
            env.emit("v", lambda e: e.tensor_scalar(s_oma[:], s_a[:], -1.0, 1.0, ALU.mult, ALU.add))
            env.selfsync("v")
            env.emit("v", lambda e: e.reciprocal(s_ioma[:], s_oma[:]))
            env.selfsync("v")
            env.wait("d_consts", who="v")
            for k in range(5):
                env.emit("v", lambda e, k=k: e.tensor_tensor(cwp[:, :, k], s_cw[:, :, k], s_ioma[:], ALU.mult))
            env.wait("d_consts", who="v")
            env.wait("d_consts", who="v")
            env.emit("v", lambda e: e.tensor_tensor(biasmix[:], biasmix[:], s_cb[:], ALU.add))
            env.chk("v", "vconsts")
            env.wait("vconsts", who="s")
            env.emit("s", lambda e: e.activation(ln2t[:], ln2t[:], AF.Ln))
            env.chk("s", "ln2done")
            env.wait("ln2done", who="v")
            env.emit("v", lambda e: e.tensor_single_scalar(b151[:], ln2t[:], 151.0, ALU.mult))
            env.emit("v", lambda e: e.tensor_single_scalar(bm135[:], ln2t[:], -135.0, ALU.mult))
            env.emit("v", lambda e: e.tensor_single_scalar(nln2t[:], ln2t[:], -1.0, ALU.mult))
            env.emit("v", lambda e: e.drain())
            env.emit("v", lambda e: e.reciprocal(rln2t[:], ln2t[:]))
            env.chk("v", "ln2fam")

            for wi, (src, dsth, dstl, center) in enumerate(
                    [] if kpr else [(dwt, wh, wl, False), (dbwt, bh, bl, True)]):
                for oc in range(NDC):
                    scr = wsc[oc % 2]
                    pwi, poc = (wi, oc - 2) if oc >= 2 else (wi - 1, NDC - 2 + oc)
                    if pwi >= 0:
                        env.wait(f"w{pwi}l_{poc}", who="d")
                    dma(scr, src[:, :, oc * P:(oc + 1) * P], f"d_w{wi}_{oc}", sem=('dwA', 'dwB')[oc % 2])
                    if center:
                        env.wait(f"d_w{wi}_{oc}", who="t")
                        if oc >= 1:
                            env.wait(f"w{wi}c_{oc - 1}", who="t")
                        env.wait("d_consts", who="t")
                        for kc in range(NDC):
                            env.emit("t", lambda e, kc=kc, scr=scr: e.matmul(
                                pstat[0][:, 0:P], s_invdf,
                                scr[:, kc, :],
                                start=(kc == 0), stop=(kc == NDC - 1)))
                        env.chk("t", f"w{wi}m_{oc}")
                        env.wait(f"w{wi}m_{oc}", who="v")
                        for kc in range(NDC):
                            env.emit("v", lambda e, kc=kc, scr=scr: e.tensor_tensor(
                                scr[:, kc, :], scr[:, kc, :],
                                pstat[0][:, 0:P], ALU.subtract))
                        env.selfsync("v")
                        env.chk("v", f"w{wi}c_{oc}")
                        env.wait(f"w{wi}c_{oc}", who="s")
                    else:
                        env.wait(f"d_w{wi}_{oc}", who="s")
                    env.emit("s", lambda e, oc=oc, scr=scr, dsth=dsth: e.copy(dsth[:, :, oc, :], scr))
                    env.chk("s", f"w{wi}h_{oc}")
                    env.wait(f"w{wi}h_{oc}", who="v")
                    env.emit("v", lambda e, oc=oc, scr=scr, dsth=dsth, dstl=dstl: e.tensor_tensor(
                        dstl[:, :, oc, :], scr, dsth[:, :, oc, :], ALU.subtract))
                    env.chk("v", f"w{wi}l_{oc}")

            if kpr:
                env.cp["prep_s"] = env.cp["a_done"]
                env.cp["prep_t"] = ("t", 0)
                env.emit("v", lambda e: e.engine_nop()); env.chk("v", "prep_v")
                env.emit("g", lambda e: e.engine_nop()); env.chk("g", "prep_g")
                for k in ["v", "s", "t", "g"]:
                    env.wait(f"prep_{k}")
                dma(bufx[0][:], dx[:, 0:TT].rearrange("(dc p) t -> p dc t", p=P), "d_x0", sem="dx0")
                return
            env.wait("w1l_6", who="d")
            dma(wsc[0][:, :, 0:8], drt, "d_rt", sem="dwA")
            env.wait("w1l_7", who="d")
            dma(wsc[0][:, :, 8:16], dla, "d_la", sem="dwB")
            env.wait("d_rt", who="t")
            env.wait("d_la", who="t")
            env.wait("w1c_7", who="t")
            for kc in range(NDC):
                env.emit("t", lambda e, kc=kc: e.matmul(
                    pstat[0][:, 0:16], s_invdf, wsc[0][:, kc, 0:16],
                    start=(kc == 0), stop=(kc == NDC - 1)))
            env.chk("t", "rtm")
            env.wait("rtm", who="v")
            for kc in range(NDC):
                env.emit("v", lambda e, kc=kc: e.tensor_tensor(
                    wsc[0][:, kc, 0:16], wsc[0][:, kc, 0:16], pstat[0][:, 0:16], ALU.subtract))
            env.chk("v", "rtc")
            env.emit("v", lambda e: e.memset(sph[:], 0.0))
            env.emit("v", lambda e: e.memset(spl[:], 0.0))
            env.chk("v", "spz")
            env.wait("rtc", who="s")
            env.wait("spz", who="s")
            env.emit("s", lambda e: e.copy(sph[:, :, 0:8], wsc[0][:, :, 0:8]))
            env.emit("s", lambda e: e.copy(sph[:, :, 32:40], wsc[0][:, :, 8:16]))
            env.chk("s", "rth")
            env.wait("rth", who="v")
            env.emit("v", lambda e: e.tensor_tensor(spl[:, :, 0:8], wsc[0][:, :, 0:8], sph[:, :, 0:8], ALU.subtract))
            env.emit("v", lambda e: e.tensor_tensor(spl[:, :, 32:40], wsc[0][:, :, 8:16], sph[:, :, 32:40], ALU.subtract))
            env.chk("v", "rtl")

            env.wait("rtl", who="d")
            lsc = convb[0:64, :, P:2 * P]
            dma(lsc, dlb, "d_lb", sem="dwB")
            env.wait("d_lb", who="v")
            env.emit("v", lambda e: e.tensor_single_scalar(lsc, lsc, 2.0, ALU.mult))
            env.selfsync("v")
            env.chk("v", "lb2")
            env.wait("lb2", who="s")
            env.emit("s", lambda e: e.copy(lbh[:], lsc))
            env.chk("s", "lbh")
            env.wait("lbh", who="v")
            env.emit("v", lambda e: e.tensor_tensor(lbl[:], lsc, lbh[:], ALU.subtract))
            env.chk("v", "lbl")

            dma(bufx[0][:], dx[:, 0:TT].rearrange("(dc p) t -> p dc t", p=P), "d_x0", sem="dx0")

            env.emit("v", lambda e: e.engine_nop()); env.chk("v", "prep_v")
            env.cp["prep_s"] = env.cp["lbh"]
            env.cp["prep_t"] = env.cp["rtm"]
            env.emit("g", lambda e: e.engine_nop()); env.chk("g", "prep_g")
            for k in ["v", "s", "t", "g"]:
                env.wait(f"prep_{k}")

        def stageA(env, j):
            halo = (j == NT_A - 1)
            xb = bufx[j % 3]
            xp = xpad[j % 2]
            pst = pstat[0]
            if j + 1 < NT_A:
                if j >= 2:
                    env.wait(f"stat1_{j - 2}", who="d")
                    env.wait(f"xp_{j - 2}", who="d")
                    if 2 <= j - 2 <= NT_A - 2 and (j - 2) - 2 < nb:
                        env.wait(f"x2_{j - 2}", who="d")
                env.emit("d", lambda e, j=j: e.dma_start(
                    out=bufx[(j + 1) % 3][:],
                    in_=dx[:, (j + 1) * TT:(j + 2) * TT].rearrange("(dc p) t -> p dc t", p=P)))
                env.chk(f"dx{(j + 1) % 3}", f"d_x{j + 1}", dma=True, via="d")
            env.wait(f"d_x{j}", who="s")
            if j >= 1:
                env.wait(f"xp_{j - 1}", who="s")
            if 2 <= j - 3 <= NT_A - 2 and (j - 3) - 2 < nb:
                env.wait(f"stat3_{j - 3}", who="s")
            elif j >= 4 and has_b(j - 2):
                env.wait(f"stat2_{j - 2}", who="s")
            env.emit("s", lambda e: e.activation(sqb[:], xb[:].bitcast(F32), AF.Square))
            env.chk("s", f"sqx_{j}")
            if ksa < 2:
                return
            env.wait(f"d_x{j}", who="t")
            env.wait(f"sqx_{j}", who="t")
            if j >= 4 and (j - 2) - 2 < nb:
                env.wait(f"tm2_{j - 2}", who="t")
                env.wait(f"t2_{j - 2}", who="t")
            elif j >= 2:
                env.wait(f"xp_{j - 2}", who="t")
            for kc in range(NDC):
                env.emit("t", lambda e, kc=kc: e.matmul(
                    pst[:, 0:TT], s_invd[:], xb[:, kc, :],
                    start=(kc == 0), stop=(kc == NDC - 1)))
            for kc in range(NDC):
                env.emit("t", lambda e, kc=kc: e.matmul(
                    pst[:, TT:2 * TT], s_invd[:], sqb[:, kc, :],
                    start=(kc == 0), stop=(kc == NDC - 1)))
            env.chk("t", f"stat1_{j}")
            if ksa < 3:
                return
            env.wait(f"stat1_{j}", who="s")
            env.emit("s", lambda e: e.activation(tmA[:], pst[:, 0:TT], AF.Square))
            env.chk("s", f"tm1_{j}")
            env.wait(f"tm1_{j}", who="v")
            env.emit("v", lambda e: e.tensor_tensor(tmA[:], pst[:, TT:2 * TT], tmA[:], ALU.subtract))
            env.chk("v", f"wc1_{j}")
            env.wait(f"wc1_{j}", who="s")
            env.emit("s", lambda e: e.activation(tmA[:], tmA[:], AF.Ln, bias=epsb[:]))
            env.selfsync("s")
            env.emit("s", lambda e: e.activation(tmA[:], tmA[:], AF.Exp, scale=-0.5))
            env.chk("s", f"r1_{j}")
            if ksa < 4:
                return
            env.emit("v", lambda e: e.tensor_copy(mcA[:], pst[:, 0:TT]))
            env.selfsync("v")
            env.emit("v", lambda e: e.tensor_tensor(xp[:, :, 2:2 + TT], xb[:].bitcast(F32), bc8(mcA[:]), ALU.subtract))
            env.chk("v", f"u1_{j}")
            env.wait(f"r1_{j}", who="v")
            env.selfsync("v")
            for dc in range(NDC):
                env.emit("v", lambda e, dc=dc: e.scalar_tensor_tensor(
                    xp[:, dc, 2:2 + TT], xp[:, dc, 2:2 + TT], s_oma[:, dc:dc + 1], tmA[:],
                    ALU.mult, ALU.mult))
            env.selfsync("v")
            env.chk("v", f"xp_{j}")
            if j == kdbt:
                dbgdump(env, "xp_2", xp[:], TT + 4)
                dbgdump(env, "m_2", mcA[:], -1, wlabel=f"u1_{j}")
                dbgdump(env, "r_2", tmA[:], -1, wlabel=f"r1_{j}")
                dbgdump(env, "sq_2", sqb[:].bitcast(F32), TT, wlabel=f"sqx_{j}")
                dbgdump(env, "xb_2", xb[:].bitcast(F32), TT, wlabel=f"d_x{j}")
            if j == 1:
                env.emit("v", lambda e: e.tensor_copy(lstash[:], xp[:, :, TT:TT + 2]))
            if halo or (ksa < 6 and ksa != 45):
                return
            if j - 1 >= 2:
                env.wait(f"shi_{j - 1}", who="v")
                env.wait(f"slo_{j - 1}", who="v")
            for dc in range(NDC):
                init = 0.0 if j == 0 else carry[:, dc, 0:1]
                env.emit("v", lambda e, dc=dc, init=init: e.tensor_tensor_scan(
                    sS[:, dc, :], s_a[:, dc:dc + 1].broadcast_to([P, TT]),
                    xp[:, dc, 2:2 + TT], init, ALU.mult, ALU.add))
            env.selfsync("v")
            env.emit("v", lambda e: e.tensor_copy(carry[:], sS[:, :, TT - 1:TT]))
            env.chk("v", f"scan_{j}")
            if j == kdbt:
                dbgdump(env, "scan_2", sS[:], TT)

        def stageA2(env, j):
            if j < 2 or j == NT_A - 1 or ksa < 6:
                return
            env.wait(f"scan_{j}", who="s")
            if has_mix(j - 1):
                env.wait(f"wmm_{j - 1}_7", who="s")
            env.emit("s", lambda e: e.copy(shi[:], sS[:]))
            env.chk("s", f"shi_{j}")
            env.wait(f"shi_{j}", who="v")
            if has_mix(j - 1):
                env.wait(f"wmm_{j - 1}_7", who="v")
            env.emit("v", lambda e: e.tensor_tensor(slo[:], sS[:], shi[:], ALU.subtract))
            env.chk("v", f"slo_{j}")

        def has_mix(m):
            return m == 2 or (3 <= m <= NT_A - 2 and (m - 3) < nb)

        def has_b(q):
            return q >= 2 and q - 2 < nb

        def cvb(m):
            return convb2 if m % 2 == 0 else convb

        def mix_chunk(env, m, oc):
            bank = pmm[oc % 2]
            if oc == 0:
                env.wait(f"shi_{m}", who="t")
                env.wait(f"slo_{m}", who="t")
            if oc >= 2:
                env.wait(f"mevac_{m}_{oc - 2}", who="t")
            elif has_mix(m - 1):
                env.wait(f"mevac_{m - 1}_{6 + oc}", who="t")
            for kc in range(NDC):
                env.emit("t", lambda e, kc=kc, oc=oc, bank=bank: e.matmul(
                    bank[:, 0:TT], wh[:, kc, oc, :], shi[:, kc, :], start=(kc == 0), stop=False))
                env.emit("t", lambda e, kc=kc, oc=oc, bank=bank: e.matmul(
                    bank[:, 0:TT], wh[:, kc, oc, :], slo[:, kc, :], start=False, stop=False))
            for kc in range(NDC):
                env.emit("t", lambda e, kc=kc, oc=oc, bank=bank: e.matmul(
                    bank[:, 0:TT], wl[:, kc, oc, :], shi[:, kc, :], start=False,
                    stop=(kc == NDC - 1)))
            env.chk("t", f"wmm_{m}_{oc}")
            if m == kdbt and oc == 2:
                dbgdump(env, "shi_2", shi[:].bitcast(F32), 128, wlabel=f"wmm_{m}_{oc}")
                dbgdump(env, "slo_2", slo[:].bitcast(F32), 128, wlabel=f"wmm_{m}_{oc}")
            env.wait(f"wmm_{m}_{oc}", who="s")
            if oc == 0 and m >= 4 and (m - 2) - 2 < nb:
                env.wait(f"x2_{m - 2}", who="s")
            env.emit("s", lambda e, oc=oc, bank=bank, m=m: e.activation(
                cvb(m)[:, oc, :], bank[:, 0:TT], AF.Identity, bias=biasmix[:, oc:oc + 1]))
            env.chk("s", f"mevac_{m}_{oc}")
            if m == kdbt and oc == 7:
                dbgdump(env, "ymm_2", cvb(m)[:], TT, wlabel=f"mevac_{m}_{oc}")

        def stageB_conv(env, j):
            xp = xpad[j % 2]
            xpn = xpad[(j + 1) % 2]
            cv = cvb(j)
            env.emit("v", lambda e: e.tensor_copy(xp[:, :, 0:2], lstash[:]))
            env.emit("v", lambda e: e.tensor_copy(xp[:, :, 2 + TT:4 + TT], xpn[:, :, 2:4]))
            env.selfsync("v")
            env.emit("v", lambda e: e.tensor_copy(lstash[:], xp[:, :, TT:TT + 2]))
            env.selfsync("v")
            env.wait(f"mevac_{j}_7", who="v")
            for k in range(5):
                if k:
                    env.selfsync("v")
                for dc in range(NDC):
                    env.emit("v", lambda e, dc=dc, k=k: e.scalar_tensor_tensor(
                        cv[:, dc, :], xp[:, dc, k:k + TT], cwp[:, dc, k:k + 1],
                        cv[:, dc, :], ALU.mult, ALU.add))
            env.selfsync("v")
            env.chk("v", f"conv_{j}")

        def stageB_head(env, j):
            f = j - 2
            m = j + 1
            emb = has_mix(m)
            xb = bufx[j % 3]
            cv = cvb(j)
            xv = x2b[j % 2]

            if j == kdbt:
                dbgdump(env, "y_2", cv[:], TT, wlabel=f"conv_{j}")

            env.wait(f"conv_{j}", who="s")
            if has_b(j - 2):
                env.wait(f"stat4_{j - 2}", who="s")
            else:
                env.wait(f"stat1_{j + 1}", who="s")
            env.emit("s", lambda e: e.activation(sqb[:], cv[:], AF.Square))
            env.chk("s", f"sqy_{j}")
            env.wait(f"conv_{j}", who="d")
            if has_b(j - 2):
                env.wait(f"stat4_{j - 2}", who="d")
            env.emit("d", lambda e: e.dma_start(out=statr[:], in_=cv[:].bitcast(F32R)))
            env.chk("dst", f"dcy_{j}", dma=True, via="d")

            if emb:
                mix_chunk(env, m, 0)
                mix_chunk(env, m, 1)
            pst = pstat[0]
            env.wait(f"sqy_{j}", who="t")
            env.wait(f"dcy_{j}", who="t")
            env.wait(f"xp_{j + 1}", who="t")
            env.wait(f"tm1_{j + 1}", who="t")
            for kc in range(NDC):
                env.emit("t", lambda e, kc=kc: e.matmul(
                    pst[:, 0:TT], s_invd[:], statr[:, kc, :],
                    start=(kc == 0), stop=(kc == NDC - 1)))
            for kc in range(NDC):
                env.emit("t", lambda e, kc=kc: e.matmul(
                    pst[:, TT:2 * TT], s_invd[:], sqb[:, kc, :],
                    start=(kc == 0), stop=(kc == NDC - 1)))
            env.chk("t", f"stat2_{j}")
            env.wait(f"stat2_{j}", who="s")
            if has_b(j - 2):
                env.wait(f"t4_{j - 2}", who="s")
            env.emit("s", lambda e: e.activation(tmB[:], pst[:, 0:TT], AF.Square))
            env.chk("s", f"tm2_{j}")
            env.wait(f"tm2_{j}", who="v")
            env.emit("v", lambda e: e.tensor_tensor(tmB[:], pst[:, TT:2 * TT], tmB[:], ALU.subtract))
            env.chk("v", f"wc2_{j}")
            env.wait(f"wc2_{j}", who="s")
            env.emit("s", lambda e: e.activation(tmB[:], tmB[:], AF.Ln, bias=epsb[:]))
            env.selfsync("s")
            env.emit("s", lambda e: e.activation(tmB[:], tmB[:], AF.Exp, scale=-0.5))
            env.chk("s", f"r2_{j}")
            env.emit("v", lambda e: e.tensor_copy(mcB[:], pst[:, 0:TT]))
            env.selfsync("v")
            env.emit("v", lambda e: e.tensor_tensor(cv[:], cv[:], bc8(mcB[:]), ALU.subtract))
            env.wait(f"r2_{j}", who="v")
            env.selfsync("v")
            env.emit("v", lambda e: e.tensor_tensor(cv[:], cv[:], bc8(tmB[:]), ALU.mult))
            env.chk("v", f"t2_{j}")
            env.wait(f"t2_{j}", who="g")
            if has_b(j - 2):
                env.wait(f"hp_{j - 2}", who="g")
                env.wait(f"dcx2_{j - 2}", who="g")
            env.emit("g", lambda e: e.tensor_tensor(xv[:], cv[:], xb[:].bitcast(F32), ALU.add))
            env.chk("g", f"x2_{j}")

            if j == kdbt:
                dbgdump(env, "x2_2", xv[:], TT)

            if emb:
                mix_chunk(env, m, 2)

        def stageB_head2(env, j):
            f = j - 2
            m = j + 2
            emb = has_mix(m)
            xv = x2b[j % 2]

            env.wait(f"x2_{j}", who="s")
            if has_b(j + 1):
                env.wait(f"stat2_{j + 1}", who="s")
            elif has_b(j - 1):
                env.wait(f"stat4_{j - 1}", who="s")
            env.emit("s", lambda e: e.activation(sqb[:], xv[:], AF.Square))
            env.chk("s", f"sqx2_{j}")
            env.wait(f"x2_{j}", who="d")
            if has_b(j + 1):
                env.wait(f"stat2_{j + 1}", who="d")
            elif has_b(j - 1):
                env.wait(f"stat4_{j - 1}", who="d")
            env.emit("d", lambda e: e.dma_start(out=statr[:], in_=xv[:].bitcast(F32R)))
            env.chk("dst", f"dcx2_{j}", dma=True, via="d")
            pst3 = pstat[1]
            env.wait(f"sqx2_{j}", who="t")
            env.wait(f"dcx2_{j}", who="t")
            if f > 0:
                env.wait(f"tm4_{j - 1}", who="t")
                env.wait(f"t4_{j - 1}", who="t")
            else:
                env.wait(f"xp_{j}", who="t")
                env.wait(f"tm1_{j}", who="t")
            for kc in range(NDC):
                env.emit("t", lambda e, kc=kc: e.matmul(
                    pst3[:, 0:TT], s_invd[:], statr[:, kc, :],
                    start=(kc == 0), stop=(kc == NDC - 1)))
            for kc in range(NDC):
                env.emit("t", lambda e, kc=kc: e.matmul(
                    pst3[:, TT:2 * TT], s_invd[:], sqb[:, kc, :],
                    start=(kc == 0), stop=(kc == NDC - 1)))
            env.chk("t", f"stat3_{j}")
            env.wait(f"stat3_{j}", who="s")
            if has_b(j + 1):
                env.wait(f"t2_{j + 1}", who="s")
            env.emit("s", lambda e: e.activation(tmB[:], pst3[:, 0:TT], AF.Square))
            env.chk("s", f"tm3_{j}")
            env.wait(f"tm3_{j}", who="v")
            env.emit("v", lambda e: e.tensor_tensor(tmB[:], pst3[:, TT:2 * TT], tmB[:], ALU.subtract))
            env.chk("v", f"wc3_{j}")
            env.wait(f"wc3_{j}", who="s")
            env.emit("s", lambda e: e.activation(tmB[:], tmB[:], AF.Ln, bias=epsb[:]))
            env.selfsync("s")
            env.emit("s", lambda e: e.activation(tmB[:], tmB[:], AF.Exp, scale=-0.5))
            env.chk("s", f"r3_{j}")
            env.wait(f"r3_{j}", who="v")
            if f > 0:
                env.wait(f"dout_{j - 1}", who="v")
                env.wait(f"dcf_{j - 1}", who="v")
                env.wait(f"sqf_{j - 1}", who="v")
            env.emit("v", lambda e: e.tensor_tensor(hfb[:], xv[:], bc8(tmB[:]), ALU.mult))
            env.chk("v", f"hp_{j}")
            env.wait(f"hp_{j}", who="s")
            if f > 0:
                env.wait(f"ffn_{j - 1}_7", who="s")
            env.emit("s", lambda e: e.copy(hh[:], hfb[:]))
            env.chk("s", f"hh_{j}")
            env.wait(f"hh_{j}", who="g")
            if f > 0:
                env.wait(f"ffn_{j - 1}_7", who="g")
            env.emit("g", lambda e: e.tensor_tensor(hl[:], hfb[:], hh[:], ALU.subtract))
            env.chk("g", f"hl_{j}")

            if emb:
                mix_chunk(env, m, 3)
                mix_chunk(env, m, 4)

            env.wait(f"hh_{j}", who="t")
            env.wait(f"hl_{j}", who="t")
            if f > 0:
                env.wait(f"outer_{j - 1}", who="t")
            for kc in range(NDC):
                env.emit("t", lambda e, kc=kc: e.matmul(
                    ppk[0:40, 0:TT], sph[:, kc, :], hh[:, kc, :], start=(kc == 0), stop=False))
                env.emit("t", lambda e, kc=kc: e.matmul(
                    ppk[0:40, 0:TT], sph[:, kc, :], hl[:, kc, :], start=False, stop=False))
            for kc in range(NDC):
                env.emit("t", lambda e, kc=kc: e.matmul(
                    ppk[0:40, 0:TT], spl[:, kc, :], hh[:, kc, :], start=False,
                    stop=(kc == NDC - 1)))
            env.chk("t", f"pack_{j}")

            env.wait(f"pack_{j}", who="s")
            if f > 0:
                env.wait(f"gq8_{j - 1}", who="s")
            env.emit("s", lambda e: e.activation(e8b[:], ppk[0:8, 0:TT], AF.Exp, bias=s_rb[:]))
            env.chk("s", f"e8_{j}")
            if emb:
                mix_chunk(env, m, 5)
            env.wait(f"e8_{j}", who="t")
            env.emit("t", lambda e: e.matmul(pzb[:, 0:TT], s_onesz[:], e8b[:], start=True, stop=True))
            env.chk("t", f"zbc_{j}")
            env.wait(f"zbc_{j}", who="s")
            if f > 0:
                env.wait(f"gr_{j - 1}", who="s")
            env.emit("s", lambda e: e.activation(lnzb[:], pzb[0:8, 0:TT], AF.Ln))
            env.chk("s", f"lnz_{j}")
            env.wait(f"lnz_{j}", who="v")
            if f > 0:
                env.wait(f"gq8_{j - 1}", who="v")
                env.wait(f"reps_{j - 1}", who="v")
            env.emit("v", lambda e: e.scalar_tensor_tensor(
                gb[0][:], ppk[0:8, 0:TT], s_rb[:], lnzb[:], ALU.add, ALU.subtract))
            env.selfsync("v")
            env.emit("v", lambda e: e.tensor_scalar(
                gb[1][:], gb[0][:], rln2t[:], 127.5, ALU.mult, ALU.add))
            env.chk("v", f"ge2p_{j}")

            def rint_blend(src_t, dst):
                env.selfsync("v")
                env.emit("v", lambda e: e.tensor_single_scalar(e8b[:], src_t[:], C52, ALU.add))
                env.emit("v", lambda e: e.tensor_single_scalar(lnzb[:], src_t[:], C23, ALU.add))
                env.selfsync("v")
                env.emit("v", lambda e: e.tensor_single_scalar(e8b[:], e8b[:], C52, ALU.subtract))
                env.emit("v", lambda e: e.tensor_single_scalar(lnzb[:], lnzb[:], C23, ALU.subtract))
                env.selfsync("v")
                env.emit("v", lambda e: e.tensor_scalar(e8b[:], e8b[:], ind8[:], None, ALU.mult))
                env.emit("v", lambda e: e.tensor_scalar(lnzb[:], lnzb[:], omi8[:], None, ALU.mult))
                env.selfsync("v")
                env.emit("v", lambda e: e.tensor_tensor(dst[:], e8b[:], lnzb[:], ALU.add))

            env.wait(f"zbc_{j}", who="v")
            rint_blend(gb[1], gb[2])
            env.selfsync("v")
            env.emit("v", lambda e: e.scalar_tensor_tensor(
                gb[1][:], gb[2][:], nln2t[:], gb[0][:], ALU.mult, ALU.add))
            env.chk("v", f"gs_{j}")
            env.wait(f"gs_{j}", who="s")
            env.emit("s", lambda e: e.activation(gb[0][:], gb[1][:], AF.Exp, bias=b151[:]))
            env.chk("s", f"gv32_{j}")
            if emb:
                mix_chunk(env, m, 6)
            env.wait(f"gv32_{j}", who="v")
            rint_blend(gb[0], gb[1])
            env.selfsync("v")
            env.emit("v", lambda e: e.tensor_single_scalar(gb[1][:], gb[1][:], float(2.0 ** -16), ALU.mult))
            env.selfsync("v")
            rint_blend(gb[1], gb[0])
            env.chk("v", f"gq8_{j}")
            env.wait(f"gq8_{j}", who="s")
            env.emit("s", lambda e: e.activation(lnzb[:], gb[2][:], AF.Exp, scale=ln2t[:], bias=bm135[:]))
            env.chk("s", f"gu_{j}")
            if emb:
                mix_chunk(env, m, 7)
            env.wait(f"gu_{j}", who="v")
            env.selfsync("v")
            env.emit("v", lambda e: e.tensor_tensor(gb[1][:], gb[0][:], lnzb[:], ALU.mult))
            env.chk("v", f"gr_{j}")
            env.emit("v", lambda e: e.tensor_copy(gb[2][:], ppk[32:40, 0:TT]))
            env.chk("v", f"mide_{j}")

            env.wait(f"gr_{j}", who="t")
            env.wait(f"mide_{j}", who="t")
            env.emit("t", lambda e: e.matmul(ppk[0:64, TT:2 * TT], s_brep[:], gb[1][:], start=True, stop=True))
            env.emit("t", lambda e: e.matmul(pzb[0:64, TT:2 * TT], s_rrep[:], gb[2][:], start=True, stop=True))
            env.chk("t", f"reps_{j}")
            env.wait(f"reps_{j}", who="v")
            if f > 0:
                env.wait(f"ol_{j - 1}", who="v")
            env.emit("v", lambda e: e.tensor_copy(outerb[:], ppk[0:64, TT:2 * TT]))
            env.selfsync("v")
            env.emit("v", lambda e: e.tensor_tensor(outerb[:], outerb[:], pzb[0:64, TT:2 * TT], ALU.mult))
            env.chk("v", f"outer_{j}")
            env.wait(f"outer_{j}", who="s")
            if f > 0:
                env.wait(f"ffn_{j - 1}_7", who="s")
            env.emit("s", lambda e: e.copy(oh[:], outerb[:]))
            env.chk("s", f"oh_{j}")
            env.wait(f"oh_{j}", who="g")
            env.emit("g", lambda e: e.tensor_tensor(ol[:], outerb[:], oh[:], ALU.subtract))
            env.chk("g", f"ol_{j}")


        def stageB_tail(env, j):
            f = j - 2
            for oc in range(NDC):
                bank = pmm[2 + oc % 2]
                if oc >= 2:
                    env.wait(f"ffnevac_{j}_{oc - 2}", who="t")
                elif f > 0:
                    env.wait(f"ffnevac_{j - 1}_{6 + oc}", who="t")
                for kc in range(NDC):
                    env.emit("t", lambda e, kc=kc, oc=oc, bank=bank: e.matmul(
                        bank[:, 0:TT], bh[:, kc, oc, :], hh[:, kc, :], start=(kc == 0), stop=False))
                    env.emit("t", lambda e, kc=kc, oc=oc, bank=bank: e.matmul(
                        bank[:, 0:TT], bh[:, kc, oc, :], hl[:, kc, :], start=False, stop=False))
                for kc in range(NDC):
                    env.emit("t", lambda e, kc=kc, oc=oc, bank=bank: e.matmul(
                        bank[:, 0:TT], bl[:, kc, oc, :], hh[:, kc, :], start=False, stop=False))
                if oc == 0:
                    env.wait(f"oh_{j}", who="t")
                    env.wait(f"ol_{j}", who="t")
                env.emit("t", lambda e, oc=oc, bank=bank: e.matmul(
                    bank[:, 0:TT], lbh[:, oc, :], oh[:], start=False, stop=False))
                env.emit("t", lambda e, oc=oc, bank=bank: e.matmul(
                    bank[:, 0:TT], lbh[:, oc, :], ol[:], start=False, stop=False))
                env.emit("t", lambda e, oc=oc, bank=bank: e.matmul(
                    bank[:, 0:TT], lbl[:, oc, :], oh[:], start=False, stop=True))
                env.chk("t", f"ffn_{j}_{oc}")
                env.wait(f"ffn_{j}_{oc}", who="s")
                env.emit("s", lambda e, oc=oc, bank=bank: e.activation(
                    hfb[:, oc, :], bank[:, 0:TT], AF.Identity, bias=s_bb[:, oc:oc + 1]))
                env.chk("s", f"ffnevac_{j}_{oc}")

            if j == kdbt:
                dbgdump(env, "f_2", hfb[:], TT, wlabel=f"ffnevac_{j}_{NDC-1}")
            env.wait(f"stat3_{j}", who="s")
            if j + 3 < min(na, NT_A):
                env.wait(f"stat1_{j + 3}", who="s")
            env.selfsync("s")
            env.emit("s", lambda e: e.activation(sqb[:], hfb[:], AF.Square))
            env.chk("s", f"sqf_{j}")
            env.wait(f"ffnevac_{j}_{NDC - 1}", who="d")
            env.wait(f"stat3_{j}", who="d")
            env.emit("d", lambda e: e.dma_start(out=statr[:], in_=hfb[:].bitcast(F32R)))
            env.chk("dst", f"dcf_{j}", dma=True, via="d")
            pst4 = pstat[1]
            env.wait(f"sqf_{j}", who="t")
            env.wait(f"dcf_{j}", who="t")
            env.wait(f"wc3_{j}", who="t")
            env.wait(f"tm3_{j}", who="t")
            for kc in range(NDC):
                env.emit("t", lambda e, kc=kc: e.matmul(
                    pst4[:, 0:TT], s_invd[:], statr[:, kc, :],
                    start=(kc == 0), stop=(kc == NDC - 1)))
            for kc in range(NDC):
                env.emit("t", lambda e, kc=kc: e.matmul(
                    pst4[:, TT:2 * TT], s_invd[:], sqb[:, kc, :],
                    start=(kc == 0), stop=(kc == NDC - 1)))
            env.chk("t", f"stat4_{j}")
            env.wait(f"stat4_{j}", who="s")
            env.emit("s", lambda e: e.activation(tmB[:], pst4[:, 0:TT], AF.Square))
            env.chk("s", f"tm4_{j}")

        def stageB_ln4(env, j):
            f = j - 2
            pst4 = pstat[1]
            env.wait(f"tm4_{j}", who="v")
            env.emit("v", lambda e: e.tensor_tensor(tmB[:], pst4[:, TT:2 * TT], tmB[:], ALU.subtract))
            env.chk("v", f"wc4_{j}")
            env.wait(f"wc4_{j}", who="s")
            env.emit("s", lambda e: e.activation(tmB[:], tmB[:], AF.Ln, bias=epsb[:]))
            env.selfsync("s")
            env.emit("s", lambda e: e.activation(tmB[:], tmB[:], AF.Exp, scale=-0.5))
            env.chk("s", f"r4_{j}")
            env.emit("v", lambda e: e.tensor_copy(mcB[:], pst4[:, 0:TT]))
            env.selfsync("v")
            env.emit("v", lambda e: e.tensor_tensor(hfb[:], hfb[:], bc8(mcB[:]), ALU.subtract))
            env.wait(f"r4_{j}", who="v")
            env.selfsync("v")
            env.emit("v", lambda e: e.tensor_tensor(hfb[:], hfb[:], bc8(tmB[:]), ALU.mult))
            env.chk("v", f"t4_{j}")
            env.wait(f"t4_{j}", who="g")
            env.emit("g", lambda e, j=j: e.tensor_tensor(hfb[:], hfb[:], x2b[j % 2][:], ALU.add))
            env.chk("g", f"outres_{j}")
        def stageB_dout(env, j):
            f = j - 2
            if j == 15:
                dbgdump(env, "wh_2", wh[:, :, 2, :].bitcast(F32), 64, wlabel=f"outres_{j}")
            env.wait(f"outres_{j}", who="d")
            env.emit("d", lambda e, f=f: e.dma_start(
                out=dout[:, f * TT:(f + 1) * TT].rearrange("(dc p) t -> p dc t", p=P),
                in_=hfb[:]))
            env.chk(f"o{j % 2}", f"dout_{j}", dma=True, via="d")

        def dbgdump(env, label, buf, n3, wlabel=None):
            if kdbg != label:
                return
            env.wait(wlabel or label, who="d")
            dst = ddbg[:, 0, 0:TT] if n3 == -1 else ddbg[:, :, 0:n3]
            env.emit("d", lambda e: e.dma_start(out=dst, in_=buf))
            env.chk("dc", "dbgd", dma=True, via="d")

        def script(env):
            prep(env)
            last_j = min(na, NT_A) - 1
            for j in range(min(na, NT_A)):
                stageA(env, j)
                if has_b(j - 3):
                    stageB_tail(env, j - 3)
                stageA2(env, j)
                if j == 2 and ksa >= 6:
                    for oc in range(NDC):
                        mix_chunk(env, 2, oc)
                if has_b(j - 1):
                    stageB_conv(env, j - 1)
                if has_b(j - 3):
                    stageB_ln4(env, j - 3)
                    stageB_dout(env, j - 3)
                if has_b(j - 1):
                    stageB_head(env, j - 1)
                if j == 3 and ksa >= 6 and has_mix(3):
                    for oc in range(3, NDC):
                        mix_chunk(env, 3, oc)
                if has_b(j - 2):
                    stageB_head2(env, j - 2)
            for q in (last_j - 2, last_j - 1):
                if has_b(q):
                    if q == last_j - 1:
                        stageB_head2(env, q)
                    stageB_tail(env, q)
                    stageB_ln4(env, q)
                    stageB_dout(env, q)
            if env.active == "d":
                env.eng.wait_ge(sems["o0"], env.cnt["o0"])
                env.eng.wait_ge(sems["o1"], env.cnt["o1"])

        @block.sync
        def _(sync):
            script(Env("d", sync))

        @block.tensor
        def _(tensor):
            script(Env("t", tensor))

        @block.vector
        def _(vector):
            script(Env("v", vector))

        @block.scalar
        def _(scalar):
            script(Env("s", scalar))

        @block.gpsimd
        def _(gpsimd):
            script(Env("g", gpsimd))

    return nc


_CACHED = {}


def kernel(L0_attn_x, L0_attn_decay_logit, L0_attn_W, L0_attn_b, L0_attn_conv_w,
           L0_attn_conv_b, L0_ffn_base_w, L0_ffn_base_b, L0_ffn_lora_A,
           L0_ffn_lora_B, L0_ffn_router_w, L0_ffn_router_b):
    x = np.ascontiguousarray(np.asarray(L0_attn_x, dtype=np.float32))
    B, T, Dm = x.shape

    def fm(w):
        w = np.asarray(w, dtype=np.float32)
        return np.ascontiguousarray(w.reshape(NDC, P, -1).transpose(1, 0, 2))

    wt = fm(np.asarray(L0_attn_W, dtype=np.float32).T)
    bwt = fm(np.asarray(L0_ffn_base_w, dtype=np.float32).T)
    rt = fm(np.asarray(L0_ffn_router_w, dtype=np.float32).T)
    la = fm(np.asarray(L0_ffn_lora_A, dtype=np.float32))
    lb = np.ascontiguousarray(
        np.asarray(L0_ffn_lora_B, dtype=np.float32).reshape(64, NDC, P))
    cw = fm(np.asarray(L0_attn_conv_w, dtype=np.float32)[:, 0, :])

    def pv(v):
        return np.ascontiguousarray(np.asarray(v, dtype=np.float32).reshape(NDC, P).T)

    dec = pv(L0_attn_decay_logit)
    ba = pv(L0_attn_b)
    cb = pv(L0_attn_conv_b)
    bb = pv(L0_ffn_base_b)
    rb = np.asarray(L0_ffn_router_b, dtype=np.float32).reshape(8, 1)
    invd = np.full((P, P), 1.0 / D, dtype=np.float32)
    onesz = np.ones((8, P), dtype=np.float32)
    brep = np.zeros((8, 64), dtype=np.float32)
    rrep = np.zeros((8, 64), dtype=np.float32)
    for e in range(8):
        for r in range(8):
            brep[e, e * 8 + r] = 1.0
            rrep[r, e * 8 + r] = 1.0

    in_maps = []
    for c in range(8):
        b, q = divmod(c, 4)
        t0 = q * CH
        xin = np.zeros((ROWS, D), dtype=np.float32)
        lo = max(0, t0 - KAP)
        hi = min(T, t0 + CH + TT)
        xin[KAP - (t0 - lo): KAP + (hi - t0)] = x[b, lo:hi]
        in_maps.append({
            "x": np.ascontiguousarray(xin.T), "wt": wt, "bwt": bwt, "rt": rt,
            "la": la, "lb": lb, "cw": cw, "dec": dec, "ba": ba, "cb": cb,
            "bb": bb, "rb": rb, "invd": invd, "invdf": invd, "onesz": onesz,
            "brep": brep, "rrep": rrep,
        })

    import os as _os
    if "nc" not in _CACHED:
        _CACHED["nc"] = build_nc(nb=int(_os.environ.get("KNB", "16")), na=int(_os.environ.get("KNA", str(NT_A))), ksa=int(_os.environ.get("KSA", "9")), kpr=int(_os.environ.get("KPR", "0")), kdbg=_os.environ.get("KDBG", ""), kdbt=int(_os.environ.get("KDBT", "2")))
    res = run_bass_kernel_spmd(_CACHED["nc"], in_maps, core_ids=list(range(8)))

    out = np.empty((B, T, Dm), dtype=np.float32)
    for c in range(8):
        b, q = divmod(c, 4)
        out[b, q * CH:(q + 1) * CH] = np.asarray(res.results[c]["out"]).T
    return out


# version 18
# speedup vs baseline: 1.1049x; 1.1049x over previous
import numpy as np
from contextlib import ExitStack

import concourse.bass as bass
import concourse.mybir as mybir
from concourse.bass_utils import run_bass_kernel_spmd

F32 = mybir.dt.float32
F16 = mybir.dt.float16
F32R = mybir.dt.float32r
AF = mybir.ActivationFunctionType
ALU = mybir.AluOpType

P = 128
NDC = 8
D = 1024
TT = 256
KAP = 512
CH = 4096
NT_A = KAP // TT + CH // TT + 1
ROWS = KAP + CH + TT
EPS = 1e-5
LN2C = float(np.log(2.0))
C52 = float(2.0 ** 52)
C23 = float(2.0 ** 23)


def build_nc(nb=16, na=NT_A, ksa=9, kpr=0, kdbg="", kdbt=2):
    nc = bass.Bass("TRN2", target_bir_lowering=False, dynamic_dma_scratch_size=8192)

    dx = nc.dram_tensor("x", [D, ROWS], F32R, kind="ExternalInput").ap()
    dwt = nc.dram_tensor("wt", [P, NDC, D], F32, kind="ExternalInput").ap()
    dbwt = nc.dram_tensor("bwt", [P, NDC, D], F32, kind="ExternalInput").ap()
    drt = nc.dram_tensor("rt", [P, NDC, 8], F32, kind="ExternalInput").ap()
    dla = nc.dram_tensor("la", [P, NDC, 8], F32, kind="ExternalInput").ap()
    dlb = nc.dram_tensor("lb", [64, NDC, P], F32, kind="ExternalInput").ap()
    dcw = nc.dram_tensor("cw", [P, NDC, 5], F32, kind="ExternalInput").ap()
    ddec = nc.dram_tensor("dec", [P, NDC], F32, kind="ExternalInput").ap()
    dba = nc.dram_tensor("ba", [P, NDC], F32, kind="ExternalInput").ap()
    dcb = nc.dram_tensor("cb", [P, NDC], F32, kind="ExternalInput").ap()
    dbb = nc.dram_tensor("bb", [P, NDC], F32, kind="ExternalInput").ap()
    drb = nc.dram_tensor("rb", [8, 1], F32, kind="ExternalInput").ap()
    dinvd = nc.dram_tensor("invd", [P, P], F32R, kind="ExternalInput").ap()
    dinvdf = nc.dram_tensor("invdf", [P, P], F32, kind="ExternalInput").ap()
    donesz = nc.dram_tensor("onesz", [8, P], F32, kind="ExternalInput").ap()
    dbrep = nc.dram_tensor("brep", [8, 64], F32, kind="ExternalInput").ap()
    drrep = nc.dram_tensor("rrep", [8, 64], F32, kind="ExternalInput").ap()
    dout = nc.dram_tensor("out", [D, CH], F32, kind="ExternalOutput").ap()
    ddbg = nc.dram_tensor("dbg", [P, NDC, TT + 4], F32, kind="ExternalOutput").ap()

    es = ExitStack()

    def sb(name, shape, dt=F32):
        return es.enter_context(nc.sbuf_tensor(name, shape, dt))

    def psf(name):
        return es.enter_context(nc.psum_tensor(name, [P, 512], F32))

    with es:
        wh = sb("wh", [P, NDC, NDC, P], F16)
        wl = sb("wl", [P, NDC, NDC, P], F16)
        bh = sb("bh", [P, NDC, NDC, P], F16)
        bl = sb("bl", [P, NDC, NDC, P], F16)
        sph = sb("sph", [P, NDC, 40], F16)
        spl = sb("spl", [P, NDC, 40], F16)
        lbh = sb("lbh", [64, NDC, P], F16)
        lbl = sb("lbl", [64, NDC, P], F16)
        cwp = sb("cwp", [P, NDC, 5])
        s_a = sb("s_a", [P, NDC])
        s_oma = sb("s_oma", [P, NDC])
        s_ioma = sb("s_ioma", [P, NDC])
        biasmix = sb("biasmix", [P, NDC])
        s_cb = sb("s_cb", [P, NDC])
        s_bb = sb("s_bb", [P, NDC])
        s_rb = sb("s_rb", [8, 1])
        s_invd = sb("s_invd", [P, P], F32R)
        s_onesz = sb("s_onesz", [8, P])
        s_brep = sb("s_brep", [8, 64])
        s_rrep = sb("s_rrep", [8, 64])
        epsb = sb("epsb", [P, 1])
        b151 = sb("b151", [8, 1])
        bm135 = sb("bm135", [8, 1])
        bufx = [sb(f"bufx{i}", [P, NDC, TT], F32R) for i in range(3)]
        xpad = [sb(f"xpad{i}", [P, NDC, TT + 4]) for i in range(2)]
        lstash = sb("lstash", [P, NDC, 2])
        sS = sb("sS", [P, NDC, TT])
        carry = sb("carry", [P, NDC, 1])
        shi = sb("shi0", [P, NDC, TT], F16)
        slo = sb("slo0", [P, NDC, TT], F16)
        sqb = sb("sqb", [P, NDC, TT], F32R)
        statr = sb("statr", [P, NDC, TT], F32R)
        convb = sb("convb", [P, NDC, TT])
        convb2 = sb("convb2", [P, NDC, TT])
        x2b = [sb("x2b0", [P, NDC, TT]), sb("x2b1", [P, NDC, TT])]
        hfb = sb("hfb", [P, NDC, TT])
        hh = sb("hh", [P, NDC, TT], F16)
        hl = sb("hl", [P, NDC, TT], F16)
        tmA = sb("tmA", [P, TT])
        tmB = sb("tmB", [P, TT])
        mcA = sb("mcA", [P, TT])
        mcB = sb("mcB", [P, TT])
        e8b = sb("e8b", [8, TT])
        lnzb = sb("lnzb", [8, TT])
        gb = [sb(f"gb{i}", [8, TT]) for i in range(3)]
        ind8 = sb("ind8", [8, 1])
        omi8 = sb("omi8", [8, 1])
        ln2t = sb("ln2t", [8, 1])
        nln2t = sb("nln2t", [8, 1])
        rln2t = sb("rln2t", [8, 1])
        outerb = sb("outerb", [64, TT])
        oh = sb("oh", [64, TT], F16)
        ol = sb("ol", [64, TT], F16)
        wsc = [convb[:, :, 0:P], convb[:, :, P:2 * P]]
        s_invdf = x2b[0][:, 0, 0:P].rearrange("p t -> p t")
        s_dec = x2b[0][:, 1, 0:NDC]
        s_cw = x2b[0][:, 2, 0:40].rearrange("p (dc k) -> p dc k", k=5)
        pstat = [psf("pstat0"), psf("pstat1")]
        pmm = [psf(f"pmm{i}") for i in range(4)]
        ppk = psf("ppk")
        pzb = psf("pzb")

        sems = {k: es.enter_context(nc.semaphore(f"sem_{k}"))
                for k in ["v", "s", "t", "g", "dc", "dwA", "dwB",
                          "dx0", "dx1", "dx2", "dst", "o0", "o1"]}
        block = es.enter_context(nc.Block())

        def bc8(ap2d):
            return ap2d.unsqueeze(1).broadcast_to([P, NDC, TT])

        class Env:
            def __init__(self, active, eng):
                self.active = active
                self.eng = eng
                self.cnt = {k: 0 for k in sems}
                self.cp = {}
                self.last = None

            def emit(self, engine, fn):
                if self.active == engine:
                    self.last = fn(self.eng)

            def chk(self, engine, label, dma=False, via=None):
                self.cnt[engine] += 16 if dma else 1
                assert label not in self.cp, label
                self.cp[label] = (engine, self.cnt[engine])
                owner = via if via is not None else engine
                if self.active == owner:
                    self.last.then_inc(sems[engine], 16 if dma else 1)

            def wait(self, label, who=None):
                if who is not None and self.active != who:
                    return
                src, val = self.cp[label]
                if src == self.active:
                    return
                self.eng.wait_ge(sems[src], val)

            def nop(self, engine):
                if self.active == engine:
                    self.eng.engine_nop()

            def selfsync(self, engine):
                if self.active == engine:
                    self.eng.drain()

        def prep(env):
            def dma(dst, src, label, sem="dc"):
                env.emit("d", lambda e: e.dma_start(out=dst, in_=src))
                env.chk(sem, label, dma=True, via="d")

            dma(s_invd[:], dinvd, "d_invd")
            dma(s_invdf, dinvdf, "d_invdf")
            dma(s_dec, ddec, "d_dec")
            dma(s_cw, dcw, "d_cw")
            dma(biasmix[:], dba, "d_ba")
            dma(s_cb[:], dcb, "d_cb")
            dma(s_bb[:], dbb, "d_bb")
            dma(s_rb[:], drb, "d_rb")
            dma(s_onesz[:], donesz, "d_onesz")
            dma(s_brep[:], dbrep, "d_brep")
            dma(s_rrep[:], drrep, "d_rrep")
            env.cp["d_consts"] = ("dc", env.cnt["dc"])

            env.wait("d_consts", who="s")
            env.emit("s", lambda e: e.activation(s_a[:], s_dec, AF.Sigmoid))
            env.chk("s", "a_done")

            env.emit("v", lambda e: e.memset(ind8[:], 1.0))
            env.emit("v", lambda e: e.drain())
            env.emit("v", lambda e: e.tensor_single_scalar(ind8[:], ind8[:], float(2.0 ** -40), ALU.add))
            env.emit("v", lambda e: e.drain())
            env.emit("v", lambda e: e.tensor_single_scalar(ind8[:], ind8[:], 1.0, ALU.subtract))
            env.emit("v", lambda e: e.drain())
            env.emit("v", lambda e: e.tensor_single_scalar(ind8[:], ind8[:], float(2.0 ** 40), ALU.mult))
            env.emit("v", lambda e: e.drain())
            env.emit("v", lambda e: e.tensor_scalar(omi8[:], ind8[:], -1.0, 1.0, ALU.mult, ALU.add))
            env.emit("v", lambda e: e.memset(epsb[:], EPS))
            env.emit("v", lambda e: e.memset(carry[:], 0.0))
            env.emit("v", lambda e: e.memset(ln2t[:], 2.0))
            env.emit("v", lambda e: e.drain())
            env.wait("a_done", who="v")
            env.emit("v", lambda e: e.tensor_scalar(s_oma[:], s_a[:], -1.0, 1.0, ALU.mult, ALU.add))
            env.selfsync("v")
            env.emit("v", lambda e: e.reciprocal(s_ioma[:], s_oma[:]))
            env.selfsync("v")
            env.wait("d_consts", who="v")
            for k in range(5):
                env.emit("v", lambda e, k=k: e.tensor_tensor(cwp[:, :, k], s_cw[:, :, k], s_ioma[:], ALU.mult))
            env.wait("d_consts", who="v")
            env.wait("d_consts", who="v")
            env.emit("v", lambda e: e.tensor_tensor(biasmix[:], biasmix[:], s_cb[:], ALU.add))
            env.chk("v", "vconsts")
            env.wait("vconsts", who="s")
            env.emit("s", lambda e: e.activation(ln2t[:], ln2t[:], AF.Ln))
            env.chk("s", "ln2done")
            env.wait("ln2done", who="v")
            env.emit("v", lambda e: e.tensor_single_scalar(b151[:], ln2t[:], 151.0, ALU.mult))
            env.emit("v", lambda e: e.tensor_single_scalar(bm135[:], ln2t[:], -135.0, ALU.mult))
            env.emit("v", lambda e: e.tensor_single_scalar(nln2t[:], ln2t[:], -1.0, ALU.mult))
            env.emit("v", lambda e: e.drain())
            env.emit("v", lambda e: e.reciprocal(rln2t[:], ln2t[:]))
            env.chk("v", "ln2fam")

            for wi, (src, dsth, dstl, center) in enumerate(
                    [] if kpr else [(dwt, wh, wl, False), (dbwt, bh, bl, True)]):
                for oc in range(NDC):
                    scr = wsc[oc % 2]
                    pwi, poc = (wi, oc - 2) if oc >= 2 else (wi - 1, NDC - 2 + oc)
                    if pwi >= 0:
                        env.wait(f"w{pwi}l_{poc}", who="d")
                    dma(scr, src[:, :, oc * P:(oc + 1) * P], f"d_w{wi}_{oc}", sem=('dwA', 'dwB')[oc % 2])
                    if center:
                        env.wait(f"d_w{wi}_{oc}", who="t")
                        if oc >= 1:
                            env.wait(f"w{wi}c_{oc - 1}", who="t")
                        env.wait("d_consts", who="t")
                        for kc in range(NDC):
                            env.emit("t", lambda e, kc=kc, scr=scr: e.matmul(
                                pstat[0][:, 0:P], s_invdf,
                                scr[:, kc, :],
                                start=(kc == 0), stop=(kc == NDC - 1)))
                        env.chk("t", f"w{wi}m_{oc}")
                        env.wait(f"w{wi}m_{oc}", who="v")
                        for kc in range(NDC):
                            env.emit("v", lambda e, kc=kc, scr=scr: e.tensor_tensor(
                                scr[:, kc, :], scr[:, kc, :],
                                pstat[0][:, 0:P], ALU.subtract))
                        env.selfsync("v")
                        env.chk("v", f"w{wi}c_{oc}")
                        env.wait(f"w{wi}c_{oc}", who="s")
                    else:
                        env.wait(f"d_w{wi}_{oc}", who="s")
                    env.emit("s", lambda e, oc=oc, scr=scr, dsth=dsth: e.copy(dsth[:, :, oc, :], scr))
                    env.chk("s", f"w{wi}h_{oc}")
                    env.wait(f"w{wi}h_{oc}", who="v")
                    env.emit("v", lambda e, oc=oc, scr=scr, dsth=dsth, dstl=dstl: e.tensor_tensor(
                        dstl[:, :, oc, :], scr, dsth[:, :, oc, :], ALU.subtract))
                    env.chk("v", f"w{wi}l_{oc}")

            if kpr:
                env.cp["prep_s"] = env.cp["a_done"]
                env.cp["prep_t"] = ("t", 0)
                env.emit("v", lambda e: e.engine_nop()); env.chk("v", "prep_v")
                env.emit("g", lambda e: e.engine_nop()); env.chk("g", "prep_g")
                for k in ["v", "s", "t", "g"]:
                    env.wait(f"prep_{k}")
                dma(bufx[0][:], dx[:, 0:TT].rearrange("(dc p) t -> p dc t", p=P), "d_x0", sem="dx0")
                return
            env.wait("w1l_6", who="d")
            dma(wsc[0][:, :, 0:8], drt, "d_rt", sem="dwA")
            env.wait("w1l_7", who="d")
            dma(wsc[0][:, :, 8:16], dla, "d_la", sem="dwB")
            env.wait("d_rt", who="t")
            env.wait("d_la", who="t")
            env.wait("w1c_7", who="t")
            for kc in range(NDC):
                env.emit("t", lambda e, kc=kc: e.matmul(
                    pstat[0][:, 0:16], s_invdf, wsc[0][:, kc, 0:16],
                    start=(kc == 0), stop=(kc == NDC - 1)))
            env.chk("t", "rtm")
            env.wait("rtm", who="v")
            for kc in range(NDC):
                env.emit("v", lambda e, kc=kc: e.tensor_tensor(
                    wsc[0][:, kc, 0:16], wsc[0][:, kc, 0:16], pstat[0][:, 0:16], ALU.subtract))
            env.chk("v", "rtc")
            env.emit("v", lambda e: e.memset(sph[:], 0.0))
            env.emit("v", lambda e: e.memset(spl[:], 0.0))
            env.chk("v", "spz")
            env.wait("rtc", who="s")
            env.wait("spz", who="s")
            env.emit("s", lambda e: e.copy(sph[:, :, 0:8], wsc[0][:, :, 0:8]))
            env.emit("s", lambda e: e.copy(sph[:, :, 32:40], wsc[0][:, :, 8:16]))
            env.chk("s", "rth")
            env.wait("rth", who="v")
            env.emit("v", lambda e: e.tensor_tensor(spl[:, :, 0:8], wsc[0][:, :, 0:8], sph[:, :, 0:8], ALU.subtract))
            env.emit("v", lambda e: e.tensor_tensor(spl[:, :, 32:40], wsc[0][:, :, 8:16], sph[:, :, 32:40], ALU.subtract))
            env.chk("v", "rtl")

            env.wait("rtl", who="d")
            lsc = convb[0:64, :, P:2 * P]
            dma(lsc, dlb, "d_lb", sem="dwB")
            env.wait("d_lb", who="v")
            env.emit("v", lambda e: e.tensor_single_scalar(lsc, lsc, 2.0, ALU.mult))
            env.selfsync("v")
            env.chk("v", "lb2")
            env.wait("lb2", who="s")
            env.emit("s", lambda e: e.copy(lbh[:], lsc))
            env.chk("s", "lbh")
            env.wait("lbh", who="v")
            env.emit("v", lambda e: e.tensor_tensor(lbl[:], lsc, lbh[:], ALU.subtract))
            env.chk("v", "lbl")

            dma(bufx[0][:], dx[:, 0:TT].rearrange("(dc p) t -> p dc t", p=P), "d_x0", sem="dx0")

            env.emit("v", lambda e: e.engine_nop()); env.chk("v", "prep_v")
            env.cp["prep_s"] = env.cp["lbh"]
            env.cp["prep_t"] = env.cp["rtm"]
            env.emit("g", lambda e: e.engine_nop()); env.chk("g", "prep_g")
            for k in ["v", "s", "t", "g"]:
                env.wait(f"prep_{k}")

        def stageA(env, j):
            halo = (j == NT_A - 1)
            xb = bufx[j % 3]
            xp = xpad[j % 2]
            pst = pstat[0]
            if j + 1 < NT_A:
                if j >= 2:
                    env.wait(f"stat1_{j - 2}", who="d")
                    env.wait(f"xp_{j - 2}", who="d")
                    if 2 <= j - 2 <= NT_A - 2 and (j - 2) - 2 < nb:
                        env.wait(f"x2_{j - 2}", who="d")
                env.emit("d", lambda e, j=j: e.dma_start(
                    out=bufx[(j + 1) % 3][:],
                    in_=dx[:, (j + 1) * TT:(j + 2) * TT].rearrange("(dc p) t -> p dc t", p=P)))
                env.chk(f"dx{(j + 1) % 3}", f"d_x{j + 1}", dma=True, via="d")
            env.wait(f"d_x{j}", who="s")
            if j >= 1:
                env.wait(f"xp_{j - 1}", who="s")
            if j >= 4 and has_b(j - 2):
                env.wait(f"stat2_{j - 2}", who="s")
            env.emit("s", lambda e: e.activation(sqb[:], xb[:].bitcast(F32), AF.Square))
            env.chk("s", f"sqx_{j}")
            if ksa < 2:
                return
            env.wait(f"d_x{j}", who="t")
            env.wait(f"sqx_{j}", who="t")
            if j >= 4 and (j - 2) - 2 < nb:
                env.wait(f"tm2_{j - 2}", who="t")
                env.wait(f"t2_{j - 2}", who="t")
            elif j >= 2:
                env.wait(f"xp_{j - 2}", who="t")
            for kc in range(NDC):
                env.emit("t", lambda e, kc=kc: e.matmul(
                    pst[:, 0:TT], s_invd[:], xb[:, kc, :],
                    start=(kc == 0), stop=(kc == NDC - 1)))
            for kc in range(NDC):
                env.emit("t", lambda e, kc=kc: e.matmul(
                    pst[:, TT:2 * TT], s_invd[:], sqb[:, kc, :],
                    start=(kc == 0), stop=(kc == NDC - 1)))
            env.chk("t", f"stat1_{j}")
            if ksa < 3:
                return
            env.wait(f"stat1_{j}", who="s")
            env.emit("s", lambda e: e.activation(tmA[:], pst[:, 0:TT], AF.Square))
            env.chk("s", f"tm1_{j}")
            env.wait(f"tm1_{j}", who="v")
            env.emit("v", lambda e: e.tensor_tensor(tmA[:], pst[:, TT:2 * TT], tmA[:], ALU.subtract))
            env.chk("v", f"wc1_{j}")
            env.wait(f"wc1_{j}", who="s")
            env.emit("s", lambda e: e.activation(tmA[:], tmA[:], AF.Ln, bias=epsb[:]))
            env.selfsync("s")
            env.emit("s", lambda e: e.activation(tmA[:], tmA[:], AF.Exp, scale=-0.5))
            env.chk("s", f"r1_{j}")
            if ksa < 4:
                return
            env.emit("v", lambda e: e.tensor_copy(mcA[:], pst[:, 0:TT]))
            env.selfsync("v")
            env.emit("v", lambda e: e.tensor_tensor(xp[:, :, 2:2 + TT], xb[:].bitcast(F32), bc8(mcA[:]), ALU.subtract))
            env.chk("v", f"u1_{j}")
            env.wait(f"r1_{j}", who="v")
            env.selfsync("v")
            for dc in range(NDC):
                env.emit("v", lambda e, dc=dc: e.scalar_tensor_tensor(
                    xp[:, dc, 2:2 + TT], xp[:, dc, 2:2 + TT], s_oma[:, dc:dc + 1], tmA[:],
                    ALU.mult, ALU.mult))
            env.selfsync("v")
            env.chk("v", f"xp_{j}")
            if j == kdbt:
                dbgdump(env, "xp_2", xp[:], TT + 4)
                dbgdump(env, "m_2", mcA[:], -1, wlabel=f"u1_{j}")
                dbgdump(env, "r_2", tmA[:], -1, wlabel=f"r1_{j}")
                dbgdump(env, "sq_2", sqb[:].bitcast(F32), TT, wlabel=f"sqx_{j}")
                dbgdump(env, "xb_2", xb[:].bitcast(F32), TT, wlabel=f"d_x{j}")
            if j == 1:
                env.emit("v", lambda e: e.tensor_copy(lstash[:], xp[:, :, TT:TT + 2]))
            if halo or (ksa < 6 and ksa != 45):
                return
            if j - 1 >= 2:
                env.wait(f"shi_{j - 1}", who="v")
                env.wait(f"slo_{j - 1}", who="v")
            for dc in range(NDC):
                init = 0.0 if j == 0 else carry[:, dc, 0:1]
                env.emit("v", lambda e, dc=dc, init=init: e.tensor_tensor_scan(
                    sS[:, dc, :], s_a[:, dc:dc + 1].broadcast_to([P, TT]),
                    xp[:, dc, 2:2 + TT], init, ALU.mult, ALU.add))
            env.selfsync("v")
            env.emit("v", lambda e: e.tensor_copy(carry[:], sS[:, :, TT - 1:TT]))
            env.chk("v", f"scan_{j}")
            if j == kdbt:
                dbgdump(env, "scan_2", sS[:], TT)

        def stageA2(env, j):
            if j < 2 or j == NT_A - 1 or ksa < 6:
                return
            env.wait(f"scan_{j}", who="s")
            if has_mix(j - 1):
                env.wait(f"wmm_{j - 1}_7", who="s")
            env.emit("s", lambda e: e.copy(shi[:], sS[:]))
            env.chk("s", f"shi_{j}")
        def stageA2b(env, j):
            if j < 2 or j == NT_A - 1 or ksa < 6:
                return
            env.wait(f"shi_{j}", who="v")
            if has_mix(j - 1):
                env.wait(f"wmm_{j - 1}_7", who="v")
            env.emit("v", lambda e: e.tensor_tensor(slo[:], sS[:], shi[:], ALU.subtract))
            env.chk("v", f"slo_{j}")

        def has_mix(m):
            return m == 2 or (3 <= m <= NT_A - 2 and (m - 3) < nb)

        def has_b(q):
            return q >= 2 and q - 2 < nb

        def cvb(m):
            return convb2 if m % 2 == 0 else convb

        def mix_chunk(env, m, oc):
            bank = pmm[oc % 2]
            if oc == 0:
                env.wait(f"shi_{m}", who="t")
                env.wait(f"slo_{m}", who="t")
            if oc >= 2:
                env.wait(f"mevac_{m}_{oc - 2}", who="t")
            elif has_mix(m - 1):
                env.wait(f"mevac_{m - 1}_{6 + oc}", who="t")
            for kc in range(NDC):
                env.emit("t", lambda e, kc=kc, oc=oc, bank=bank: e.matmul(
                    bank[:, 0:TT], wh[:, kc, oc, :], shi[:, kc, :], start=(kc == 0), stop=False))
                env.emit("t", lambda e, kc=kc, oc=oc, bank=bank: e.matmul(
                    bank[:, 0:TT], wh[:, kc, oc, :], slo[:, kc, :], start=False, stop=False))
            for kc in range(NDC):
                env.emit("t", lambda e, kc=kc, oc=oc, bank=bank: e.matmul(
                    bank[:, 0:TT], wl[:, kc, oc, :], shi[:, kc, :], start=False,
                    stop=(kc == NDC - 1)))
            env.chk("t", f"wmm_{m}_{oc}")
            if m == kdbt and oc == 2:
                dbgdump(env, "shi_2", shi[:].bitcast(F32), 128, wlabel=f"wmm_{m}_{oc}")
                dbgdump(env, "slo_2", slo[:].bitcast(F32), 128, wlabel=f"wmm_{m}_{oc}")
            env.wait(f"wmm_{m}_{oc}", who="s")
            if oc == 0 and m >= 4 and (m - 2) - 2 < nb:
                env.wait(f"x2_{m - 2}", who="s")
            env.emit("s", lambda e, oc=oc, bank=bank, m=m: e.activation(
                cvb(m)[:, oc, :], bank[:, 0:TT], AF.Identity, bias=biasmix[:, oc:oc + 1]))
            env.chk("s", f"mevac_{m}_{oc}")
            if m == kdbt and oc == 7:
                dbgdump(env, "ymm_2", cvb(m)[:], TT, wlabel=f"mevac_{m}_{oc}")

        def stageB_conv(env, j):
            xp = xpad[j % 2]
            xpn = xpad[(j + 1) % 2]
            cv = cvb(j)
            env.emit("v", lambda e: e.tensor_copy(xp[:, :, 0:2], lstash[:]))
            env.emit("v", lambda e: e.tensor_copy(xp[:, :, 2 + TT:4 + TT], xpn[:, :, 2:4]))
            env.selfsync("v")
            env.emit("v", lambda e: e.tensor_copy(lstash[:], xp[:, :, TT:TT + 2]))
            env.selfsync("v")
            env.wait(f"mevac_{j}_7", who="v")
            for k in range(5):
                if k:
                    env.selfsync("v")
                for dc in range(NDC):
                    env.emit("v", lambda e, dc=dc, k=k: e.scalar_tensor_tensor(
                        cv[:, dc, :], xp[:, dc, k:k + TT], cwp[:, dc, k:k + 1],
                        cv[:, dc, :], ALU.mult, ALU.add))
            env.selfsync("v")
            env.chk("v", f"conv_{j}")

        def stageB_head(env, j):
            f = j - 2
            m = j + 1
            emb = has_mix(m)
            xb = bufx[j % 3]
            cv = cvb(j)
            xv = x2b[j % 2]

            if j == kdbt:
                dbgdump(env, "y_2", cv[:], TT, wlabel=f"conv_{j}")

            env.wait(f"conv_{j}", who="s")
            if has_b(j - 1):
                env.wait(f"stat3_{j - 1}", who="s")
            else:
                env.wait(f"stat1_{j + 1}", who="s")
            env.emit("s", lambda e: e.activation(sqb[:], cv[:], AF.Square))
            env.chk("s", f"sqy_{j}")
            env.wait(f"conv_{j}", who="d")
            if has_b(j - 1):
                env.wait(f"stat3_{j - 1}", who="d")
            env.emit("d", lambda e: e.dma_start(out=statr[:], in_=cv[:].bitcast(F32R)))
            env.chk("dst", f"dcy_{j}", dma=True, via="d")
            pst = pstat[0]
            env.wait(f"sqy_{j}", who="t")
            env.wait(f"dcy_{j}", who="t")
            env.wait(f"xp_{j + 1}", who="t")
            env.wait(f"tm1_{j + 1}", who="t")
            for kc in range(NDC):
                env.emit("t", lambda e, kc=kc: e.matmul(
                    pst[:, 0:TT], s_invd[:], statr[:, kc, :],
                    start=(kc == 0), stop=(kc == NDC - 1)))
            for kc in range(NDC):
                env.emit("t", lambda e, kc=kc: e.matmul(
                    pst[:, TT:2 * TT], s_invd[:], sqb[:, kc, :],
                    start=(kc == 0), stop=(kc == NDC - 1)))
            env.chk("t", f"stat2_{j}")
            env.wait(f"stat2_{j}", who="s")
            if has_b(j - 1):
                env.wait(f"hp_{j - 1}", who="s")
            env.emit("s", lambda e: e.activation(tmB[:], pst[:, 0:TT], AF.Square))
            env.chk("s", f"tm2_{j}")
            env.wait(f"tm2_{j}", who="v")
            env.emit("v", lambda e: e.tensor_tensor(tmB[:], pst[:, TT:2 * TT], tmB[:], ALU.subtract))
            env.chk("v", f"wc2_{j}")
            env.wait(f"wc2_{j}", who="s")
            env.emit("s", lambda e: e.activation(tmB[:], tmB[:], AF.Ln, bias=epsb[:]))
            env.selfsync("s")
            env.emit("s", lambda e: e.activation(tmB[:], tmB[:], AF.Exp, scale=-0.5))
            env.chk("s", f"r2_{j}")
            env.emit("v", lambda e: e.tensor_copy(mcB[:], pst[:, 0:TT]))
            env.selfsync("v")
            env.emit("v", lambda e: e.tensor_tensor(cv[:], cv[:], bc8(mcB[:]), ALU.subtract))
            env.wait(f"r2_{j}", who="v")
            env.selfsync("v")
            env.emit("v", lambda e: e.tensor_tensor(cv[:], cv[:], bc8(tmB[:]), ALU.mult))
            env.chk("v", f"t2_{j}")
            if emb:
                mix_chunk(env, m, 0)
                mix_chunk(env, m, 1)
            env.wait(f"t2_{j}", who="g")
            if has_b(j - 2):
                env.wait(f"hp_{j - 2}", who="g")
                env.wait(f"dcx2_{j - 2}", who="g")
            env.emit("g", lambda e: e.tensor_tensor(xv[:], cv[:], xb[:].bitcast(F32), ALU.add))
            env.chk("g", f"x2_{j}")

            if j == kdbt:
                dbgdump(env, "x2_2", xv[:], TT)

            if emb:
                mix_chunk(env, m, 2)

        def stageB_head2a(env, j):
            f = j - 2
            xv = x2b[j % 2]

            env.wait(f"x2_{j}", who="s")
            if has_b(j - 1):
                env.wait(f"stat4_{j - 1}", who="s")
            else:
                env.wait(f"stat1_{j + 2}", who="s")
            env.emit("s", lambda e: e.activation(sqb[:], xv[:], AF.Square))
            env.chk("s", f"sqx2_{j}")
            env.wait(f"x2_{j}", who="d")
            if has_b(j - 1):
                env.wait(f"stat4_{j - 1}", who="d")
            env.emit("d", lambda e: e.dma_start(out=statr[:], in_=xv[:].bitcast(F32R)))
            env.chk("dst", f"dcx2_{j}", dma=True, via="d")
            pst3 = pstat[1]
            env.wait(f"sqx2_{j}", who="t")
            env.wait(f"dcx2_{j}", who="t")
            if f > 0:
                env.wait(f"tm4_{j - 1}", who="t")
                env.wait(f"t4_{j - 1}", who="t")
            else:
                env.wait(f"xp_{j}", who="t")
                env.wait(f"tm1_{j}", who="t")
            for kc in range(NDC):
                env.emit("t", lambda e, kc=kc: e.matmul(
                    pst3[:, 0:TT], s_invd[:], statr[:, kc, :],
                    start=(kc == 0), stop=(kc == NDC - 1)))
            for kc in range(NDC):
                env.emit("t", lambda e, kc=kc: e.matmul(
                    pst3[:, TT:2 * TT], s_invd[:], sqb[:, kc, :],
                    start=(kc == 0), stop=(kc == NDC - 1)))
            env.chk("t", f"stat3_{j}")
            env.wait(f"stat3_{j}", who="s")
            env.emit("s", lambda e: e.activation(tmB[:], pst3[:, 0:TT], AF.Square))
            env.chk("s", f"tm3_{j}")
            env.wait(f"tm3_{j}", who="v")
            env.emit("v", lambda e: e.tensor_tensor(tmB[:], pst3[:, TT:2 * TT], tmB[:], ALU.subtract))
            env.chk("v", f"wc3_{j}")
            env.wait(f"wc3_{j}", who="s")
            env.emit("s", lambda e: e.activation(tmB[:], tmB[:], AF.Ln, bias=epsb[:]))
            env.selfsync("s")
            env.emit("s", lambda e: e.activation(tmB[:], tmB[:], AF.Exp, scale=-0.5))
            env.chk("s", f"r3_{j}")
            env.wait(f"r3_{j}", who="v")
            if f > 0:
                env.wait(f"dout_{j - 1}", who="v")
                env.wait(f"dcf_{j - 1}", who="v")
                env.wait(f"sqf_{j - 1}", who="v")
            env.emit("v", lambda e: e.tensor_tensor(hfb[:], xv[:], bc8(tmB[:]), ALU.mult))
            env.chk("v", f"hp_{j}")
            env.wait(f"hp_{j}", who="s")
            if f > 0:
                env.wait(f"ffn_{j - 1}_7", who="s")
            env.emit("s", lambda e: e.copy(hh[:], hfb[:]))
            env.chk("s", f"hh_{j}")
            env.wait(f"hh_{j}", who="g")
            if f > 0:
                env.wait(f"ffn_{j - 1}_7", who="g")
            env.emit("g", lambda e: e.tensor_tensor(hl[:], hfb[:], hh[:], ALU.subtract))
            env.chk("g", f"hl_{j}")

        def stageB_head2b(env, j):
            f = j - 2
            m = j + 2
            emb = has_mix(m)

            if emb:
                mix_chunk(env, m, 3)
                mix_chunk(env, m, 4)

            env.wait(f"hh_{j}", who="t")
            env.wait(f"hl_{j}", who="t")
            if f > 0:
                env.wait(f"outer_{j - 1}", who="t")
            for kc in range(NDC):
                env.emit("t", lambda e, kc=kc: e.matmul(
                    ppk[0:40, 0:TT], sph[:, kc, :], hh[:, kc, :], start=(kc == 0), stop=False))
                env.emit("t", lambda e, kc=kc: e.matmul(
                    ppk[0:40, 0:TT], sph[:, kc, :], hl[:, kc, :], start=False, stop=False))
            for kc in range(NDC):
                env.emit("t", lambda e, kc=kc: e.matmul(
                    ppk[0:40, 0:TT], spl[:, kc, :], hh[:, kc, :], start=False,
                    stop=(kc == NDC - 1)))
            env.chk("t", f"pack_{j}")

            env.wait(f"pack_{j}", who="s")
            if f > 0:
                env.wait(f"gq8_{j - 1}", who="s")
            env.emit("s", lambda e: e.activation(e8b[:], ppk[0:8, 0:TT], AF.Exp, bias=s_rb[:]))
            env.chk("s", f"e8_{j}")
            if emb:
                mix_chunk(env, m, 5)
            env.wait(f"e8_{j}", who="t")
            env.emit("t", lambda e: e.matmul(pzb[:, 0:TT], s_onesz[:], e8b[:], start=True, stop=True))
            env.chk("t", f"zbc_{j}")
            env.wait(f"zbc_{j}", who="s")
            if f > 0:
                env.wait(f"gr_{j - 1}", who="s")
            env.emit("s", lambda e: e.activation(lnzb[:], pzb[0:8, 0:TT], AF.Ln))
            env.chk("s", f"lnz_{j}")
            env.wait(f"lnz_{j}", who="v")
            if f > 0:
                env.wait(f"gq8_{j - 1}", who="v")
                env.wait(f"reps_{j - 1}", who="v")
            env.emit("v", lambda e: e.scalar_tensor_tensor(
                gb[0][:], ppk[0:8, 0:TT], s_rb[:], lnzb[:], ALU.add, ALU.subtract))
            env.selfsync("v")
            env.emit("v", lambda e: e.tensor_scalar(
                gb[1][:], gb[0][:], rln2t[:], 127.5, ALU.mult, ALU.add))
            env.chk("v", f"ge2p_{j}")

            def rint_blend(src_t, dst):
                env.selfsync("v")
                env.emit("v", lambda e: e.tensor_single_scalar(e8b[:], src_t[:], C52, ALU.add))
                env.emit("v", lambda e: e.tensor_single_scalar(lnzb[:], src_t[:], C23, ALU.add))
                env.selfsync("v")
                env.emit("v", lambda e: e.tensor_single_scalar(e8b[:], e8b[:], C52, ALU.subtract))
                env.emit("v", lambda e: e.tensor_single_scalar(lnzb[:], lnzb[:], C23, ALU.subtract))
                env.selfsync("v")
                env.emit("v", lambda e: e.tensor_scalar(e8b[:], e8b[:], ind8[:], None, ALU.mult))
                env.emit("v", lambda e: e.tensor_scalar(lnzb[:], lnzb[:], omi8[:], None, ALU.mult))
                env.selfsync("v")
                env.emit("v", lambda e: e.tensor_tensor(dst[:], e8b[:], lnzb[:], ALU.add))

            env.wait(f"zbc_{j}", who="v")
            rint_blend(gb[1], gb[2])
            env.selfsync("v")
            env.emit("v", lambda e: e.scalar_tensor_tensor(
                gb[1][:], gb[2][:], nln2t[:], gb[0][:], ALU.mult, ALU.add))
            env.chk("v", f"gs_{j}")
            env.wait(f"gs_{j}", who="s")
            env.emit("s", lambda e: e.activation(gb[0][:], gb[1][:], AF.Exp, bias=b151[:]))
            env.chk("s", f"gv32_{j}")
            if emb:
                mix_chunk(env, m, 6)
            env.wait(f"gv32_{j}", who="v")
            rint_blend(gb[0], gb[1])
            env.selfsync("v")
            env.emit("v", lambda e: e.tensor_single_scalar(gb[1][:], gb[1][:], float(2.0 ** -16), ALU.mult))
            env.selfsync("v")
            rint_blend(gb[1], gb[0])
            env.chk("v", f"gq8_{j}")
            env.wait(f"gq8_{j}", who="s")
            env.emit("s", lambda e: e.activation(lnzb[:], gb[2][:], AF.Exp, scale=ln2t[:], bias=bm135[:]))
            env.chk("s", f"gu_{j}")
            if emb:
                mix_chunk(env, m, 7)
            env.wait(f"gu_{j}", who="v")
            env.selfsync("v")
            env.emit("v", lambda e: e.tensor_tensor(gb[1][:], gb[0][:], lnzb[:], ALU.mult))
            env.chk("v", f"gr_{j}")
            env.emit("v", lambda e: e.tensor_copy(gb[2][:], ppk[32:40, 0:TT]))
            env.chk("v", f"mide_{j}")

            env.wait(f"gr_{j}", who="t")
            env.wait(f"mide_{j}", who="t")
            env.emit("t", lambda e: e.matmul(ppk[0:64, TT:2 * TT], s_brep[:], gb[1][:], start=True, stop=True))
            env.emit("t", lambda e: e.matmul(pzb[0:64, TT:2 * TT], s_rrep[:], gb[2][:], start=True, stop=True))
            env.chk("t", f"reps_{j}")
            env.wait(f"reps_{j}", who="v")
            if f > 0:
                env.wait(f"ol_{j - 1}", who="v")
            env.emit("v", lambda e: e.tensor_copy(outerb[:], ppk[0:64, TT:2 * TT]))
            env.selfsync("v")
            env.emit("v", lambda e: e.tensor_tensor(outerb[:], outerb[:], pzb[0:64, TT:2 * TT], ALU.mult))
            env.chk("v", f"outer_{j}")
            env.wait(f"outer_{j}", who="s")
            if f > 0:
                env.wait(f"ffn_{j - 1}_7", who="s")
            env.emit("s", lambda e: e.copy(oh[:], outerb[:]))
            env.chk("s", f"oh_{j}")
            env.wait(f"oh_{j}", who="g")
            env.emit("g", lambda e: e.tensor_tensor(ol[:], outerb[:], oh[:], ALU.subtract))
            env.chk("g", f"ol_{j}")


        def stageB_tail(env, j):
            f = j - 2
            for oc in range(NDC):
                bank = pmm[2 + oc % 2]
                if oc >= 2:
                    env.wait(f"ffnevac_{j}_{oc - 2}", who="t")
                elif f > 0:
                    env.wait(f"ffnevac_{j - 1}_{6 + oc}", who="t")
                for kc in range(NDC):
                    env.emit("t", lambda e, kc=kc, oc=oc, bank=bank: e.matmul(
                        bank[:, 0:TT], bh[:, kc, oc, :], hh[:, kc, :], start=(kc == 0), stop=False))
                    env.emit("t", lambda e, kc=kc, oc=oc, bank=bank: e.matmul(
                        bank[:, 0:TT], bh[:, kc, oc, :], hl[:, kc, :], start=False, stop=False))
                for kc in range(NDC):
                    env.emit("t", lambda e, kc=kc, oc=oc, bank=bank: e.matmul(
                        bank[:, 0:TT], bl[:, kc, oc, :], hh[:, kc, :], start=False, stop=False))
                if oc == 0:
                    env.wait(f"oh_{j}", who="t")
                    env.wait(f"ol_{j}", who="t")
                env.emit("t", lambda e, oc=oc, bank=bank: e.matmul(
                    bank[:, 0:TT], lbh[:, oc, :], oh[:], start=False, stop=False))
                env.emit("t", lambda e, oc=oc, bank=bank: e.matmul(
                    bank[:, 0:TT], lbh[:, oc, :], ol[:], start=False, stop=False))
                env.emit("t", lambda e, oc=oc, bank=bank: e.matmul(
                    bank[:, 0:TT], lbl[:, oc, :], oh[:], start=False, stop=True))
                env.chk("t", f"ffn_{j}_{oc}")
                env.wait(f"ffn_{j}_{oc}", who="s")
                env.emit("s", lambda e, oc=oc, bank=bank: e.activation(
                    hfb[:, oc, :], bank[:, 0:TT], AF.Identity, bias=s_bb[:, oc:oc + 1]))
                env.chk("s", f"ffnevac_{j}_{oc}")

            if j == kdbt:
                dbgdump(env, "f_2", hfb[:], TT, wlabel=f"ffnevac_{j}_{NDC-1}")
            env.wait(f"stat3_{j}", who="s")
            if j + 3 < min(na, NT_A):
                env.wait(f"stat1_{j + 3}", who="s")
            env.selfsync("s")
            env.emit("s", lambda e: e.activation(sqb[:], hfb[:], AF.Square))
            env.chk("s", f"sqf_{j}")
            env.wait(f"ffnevac_{j}_{NDC - 1}", who="d")
            if has_b(j + 1):
                env.wait(f"stat2_{j + 1}", who="d")
            else:
                env.wait(f"stat3_{j}", who="d")
            env.emit("d", lambda e: e.dma_start(out=statr[:], in_=hfb[:].bitcast(F32R)))
            env.chk("dst", f"dcf_{j}", dma=True, via="d")
            pst4 = pstat[1]
            env.wait(f"sqf_{j}", who="t")
            env.wait(f"dcf_{j}", who="t")
            env.wait(f"wc3_{j}", who="t")
            env.wait(f"tm3_{j}", who="t")
            for kc in range(NDC):
                env.emit("t", lambda e, kc=kc: e.matmul(
                    pst4[:, 0:TT], s_invd[:], statr[:, kc, :],
                    start=(kc == 0), stop=(kc == NDC - 1)))
            for kc in range(NDC):
                env.emit("t", lambda e, kc=kc: e.matmul(
                    pst4[:, TT:2 * TT], s_invd[:], sqb[:, kc, :],
                    start=(kc == 0), stop=(kc == NDC - 1)))
            env.chk("t", f"stat4_{j}")
            env.wait(f"stat4_{j}", who="s")
            env.emit("s", lambda e: e.activation(tmB[:], pst4[:, 0:TT], AF.Square))
            env.chk("s", f"tm4_{j}")

        def stageB_ln4(env, j):
            f = j - 2
            pst4 = pstat[1]
            env.wait(f"tm4_{j}", who="v")
            env.emit("v", lambda e: e.tensor_tensor(tmB[:], pst4[:, TT:2 * TT], tmB[:], ALU.subtract))
            env.chk("v", f"wc4_{j}")
            env.wait(f"wc4_{j}", who="s")
            env.emit("s", lambda e: e.activation(tmB[:], tmB[:], AF.Ln, bias=epsb[:]))
            env.selfsync("s")
            env.emit("s", lambda e: e.activation(tmB[:], tmB[:], AF.Exp, scale=-0.5))
            env.chk("s", f"r4_{j}")
            env.emit("v", lambda e: e.tensor_copy(mcB[:], pst4[:, 0:TT]))
            env.selfsync("v")
            env.emit("v", lambda e: e.tensor_tensor(hfb[:], hfb[:], bc8(mcB[:]), ALU.subtract))
            env.wait(f"r4_{j}", who="v")
            env.selfsync("v")
            env.emit("v", lambda e: e.tensor_tensor(hfb[:], hfb[:], bc8(tmB[:]), ALU.mult))
            env.chk("v", f"t4_{j}")
            env.wait(f"t4_{j}", who="g")
            env.emit("g", lambda e, j=j: e.tensor_tensor(hfb[:], hfb[:], x2b[j % 2][:], ALU.add))
            env.chk("g", f"outres_{j}")
        def stageB_dout(env, j):
            f = j - 2
            if j == 15:
                dbgdump(env, "wh_2", wh[:, :, 2, :].bitcast(F32), 64, wlabel=f"outres_{j}")
            env.wait(f"outres_{j}", who="d")
            env.emit("d", lambda e, f=f: e.dma_start(
                out=dout[:, f * TT:(f + 1) * TT].rearrange("(dc p) t -> p dc t", p=P),
                in_=hfb[:]))
            env.chk(f"o{j % 2}", f"dout_{j}", dma=True, via="d")

        def dbgdump(env, label, buf, n3, wlabel=None):
            if kdbg != label:
                return
            env.wait(wlabel or label, who="d")
            dst = ddbg[:, 0, 0:TT] if n3 == -1 else ddbg[:, :, 0:n3]
            env.emit("d", lambda e: e.dma_start(out=dst, in_=buf))
            env.chk("dc", "dbgd", dma=True, via="d")

        def script(env):
            prep(env)
            last_j = min(na, NT_A) - 1
            for j in range(min(na, NT_A)):
                stageA(env, j)
                if has_b(j - 3):
                    stageB_tail(env, j - 3)
                stageA2(env, j)
                if has_b(j - 1):
                    stageB_conv(env, j - 1)
                if has_b(j - 3):
                    stageB_ln4(env, j - 3)
                    stageB_dout(env, j - 3)
                stageA2b(env, j)
                if j == 2 and ksa >= 6:
                    for oc in range(NDC):
                        mix_chunk(env, 2, oc)
                if has_b(j - 2):
                    stageB_head2a(env, j - 2)
                if has_b(j - 1):
                    stageB_head(env, j - 1)
                if j == 3 and ksa >= 6 and has_mix(3):
                    for oc in range(3, NDC):
                        mix_chunk(env, 3, oc)
                if has_b(j - 2):
                    stageB_head2b(env, j - 2)
            for q in (last_j - 2, last_j - 1):
                if has_b(q):
                    if q == last_j - 1:
                        stageB_head2a(env, q)
                        stageB_head2b(env, q)
                    stageB_tail(env, q)
                    stageB_ln4(env, q)
                    stageB_dout(env, q)
            if env.active == "d":
                env.eng.wait_ge(sems["o0"], env.cnt["o0"])
                env.eng.wait_ge(sems["o1"], env.cnt["o1"])

        @block.sync
        def _(sync):
            script(Env("d", sync))

        @block.tensor
        def _(tensor):
            script(Env("t", tensor))

        @block.vector
        def _(vector):
            script(Env("v", vector))

        @block.scalar
        def _(scalar):
            script(Env("s", scalar))

        @block.gpsimd
        def _(gpsimd):
            script(Env("g", gpsimd))

    return nc


_CACHED = {}


def kernel(L0_attn_x, L0_attn_decay_logit, L0_attn_W, L0_attn_b, L0_attn_conv_w,
           L0_attn_conv_b, L0_ffn_base_w, L0_ffn_base_b, L0_ffn_lora_A,
           L0_ffn_lora_B, L0_ffn_router_w, L0_ffn_router_b):
    x = np.ascontiguousarray(np.asarray(L0_attn_x, dtype=np.float32))
    B, T, Dm = x.shape

    def fm(w):
        w = np.asarray(w, dtype=np.float32)
        return np.ascontiguousarray(w.reshape(NDC, P, -1).transpose(1, 0, 2))

    wt = fm(np.asarray(L0_attn_W, dtype=np.float32).T)
    bwt = fm(np.asarray(L0_ffn_base_w, dtype=np.float32).T)
    rt = fm(np.asarray(L0_ffn_router_w, dtype=np.float32).T)
    la = fm(np.asarray(L0_ffn_lora_A, dtype=np.float32))
    lb = np.ascontiguousarray(
        np.asarray(L0_ffn_lora_B, dtype=np.float32).reshape(64, NDC, P))
    cw = fm(np.asarray(L0_attn_conv_w, dtype=np.float32)[:, 0, :])

    def pv(v):
        return np.ascontiguousarray(np.asarray(v, dtype=np.float32).reshape(NDC, P).T)

    dec = pv(L0_attn_decay_logit)
    ba = pv(L0_attn_b)
    cb = pv(L0_attn_conv_b)
    bb = pv(L0_ffn_base_b)
    rb = np.asarray(L0_ffn_router_b, dtype=np.float32).reshape(8, 1)
    invd = np.full((P, P), 1.0 / D, dtype=np.float32)
    onesz = np.ones((8, P), dtype=np.float32)
    brep = np.zeros((8, 64), dtype=np.float32)
    rrep = np.zeros((8, 64), dtype=np.float32)
    for e in range(8):
        for r in range(8):
            brep[e, e * 8 + r] = 1.0
            rrep[r, e * 8 + r] = 1.0

    in_maps = []
    for c in range(8):
        b, q = divmod(c, 4)
        t0 = q * CH
        xin = np.zeros((ROWS, D), dtype=np.float32)
        lo = max(0, t0 - KAP)
        hi = min(T, t0 + CH + TT)
        xin[KAP - (t0 - lo): KAP + (hi - t0)] = x[b, lo:hi]
        in_maps.append({
            "x": np.ascontiguousarray(xin.T), "wt": wt, "bwt": bwt, "rt": rt,
            "la": la, "lb": lb, "cw": cw, "dec": dec, "ba": ba, "cb": cb,
            "bb": bb, "rb": rb, "invd": invd, "invdf": invd, "onesz": onesz,
            "brep": brep, "rrep": rrep,
        })

    import os as _os
    if "nc" not in _CACHED:
        _CACHED["nc"] = build_nc(nb=int(_os.environ.get("KNB", "16")), na=int(_os.environ.get("KNA", str(NT_A))), ksa=int(_os.environ.get("KSA", "9")), kpr=int(_os.environ.get("KPR", "0")), kdbg=_os.environ.get("KDBG", ""), kdbt=int(_os.environ.get("KDBT", "2")))
    res = run_bass_kernel_spmd(_CACHED["nc"], in_maps, core_ids=list(range(8)))

    out = np.empty((B, T, Dm), dtype=np.float32)
    for c in range(8):
        b, q = divmod(c, 4)
        out[b, q * CH:(q + 1) * CH] = np.asarray(res.results[c]["out"]).T
    return out


# version 19
# speedup vs baseline: 1.2346x; 1.1174x over previous
import numpy as np
from contextlib import ExitStack

import concourse.bass as bass
import concourse.mybir as mybir
from concourse.bass_utils import run_bass_kernel_spmd

F32 = mybir.dt.float32
F16 = mybir.dt.float16
F32R = mybir.dt.float32r
AF = mybir.ActivationFunctionType
ALU = mybir.AluOpType

P = 128
NDC = 8
D = 1024
TT = 256
KAP = 512
CH = 4096
NT_A = KAP // TT + CH // TT + 1
ROWS = KAP + CH + TT
EPS = 1e-5
LN2C = float(np.log(2.0))
C52 = float(2.0 ** 52)
C23 = float(2.0 ** 23)


def build_nc(nb=16, na=NT_A, ksa=9, kpr=0, kdbg="", kdbt=2):
    nc = bass.Bass("TRN2", target_bir_lowering=False, dynamic_dma_scratch_size=8192)

    dx = nc.dram_tensor("x", [D, ROWS], F32R, kind="ExternalInput").ap()
    dwt = nc.dram_tensor("wt", [P, NDC, D], F32, kind="ExternalInput").ap()
    dbwt = nc.dram_tensor("bwt", [P, NDC, D], F32, kind="ExternalInput").ap()
    drt = nc.dram_tensor("rt", [P, NDC, 8], F32, kind="ExternalInput").ap()
    dla = nc.dram_tensor("la", [P, NDC, 8], F32, kind="ExternalInput").ap()
    dlb = nc.dram_tensor("lb", [64, NDC, P], F32, kind="ExternalInput").ap()
    dcw = nc.dram_tensor("cw", [P, NDC, 5], F32, kind="ExternalInput").ap()
    ddec = nc.dram_tensor("dec", [P, NDC], F32, kind="ExternalInput").ap()
    dba = nc.dram_tensor("ba", [P, NDC], F32, kind="ExternalInput").ap()
    dcb = nc.dram_tensor("cb", [P, NDC], F32, kind="ExternalInput").ap()
    dbb = nc.dram_tensor("bb", [P, NDC], F32, kind="ExternalInput").ap()
    drb = nc.dram_tensor("rb", [8, 1], F32, kind="ExternalInput").ap()
    dinvd = nc.dram_tensor("invd", [P, P], F32R, kind="ExternalInput").ap()
    dinvdf = nc.dram_tensor("invdf", [P, P], F32, kind="ExternalInput").ap()
    donesz = nc.dram_tensor("onesz", [8, P], F32, kind="ExternalInput").ap()
    dbrep = nc.dram_tensor("brep", [8, 64], F32, kind="ExternalInput").ap()
    drrep = nc.dram_tensor("rrep", [8, 64], F32, kind="ExternalInput").ap()
    dout = nc.dram_tensor("out", [D, CH], F32, kind="ExternalOutput").ap()
    ddbg = nc.dram_tensor("dbg", [P, NDC, TT + 4], F32, kind="ExternalOutput").ap()

    es = ExitStack()

    def sb(name, shape, dt=F32):
        return es.enter_context(nc.sbuf_tensor(name, shape, dt))

    def psf(name):
        return es.enter_context(nc.psum_tensor(name, [P, 512], F32))

    with es:
        wh = sb("wh", [P, NDC, NDC, P], F16)
        wl = sb("wl", [P, NDC, NDC, P], F16)
        bh = sb("bh", [P, NDC, NDC, P], F16)
        bl = sb("bl", [P, NDC, NDC, P], F16)
        sph = sb("sph", [P, NDC, 40], F16)
        spl = sb("spl", [P, NDC, 40], F16)
        lbh = sb("lbh", [64, NDC, P], F16)
        lbl = sb("lbl", [64, NDC, P], F16)
        cwp = sb("cwp", [P, NDC, 5])
        s_a = sb("s_a", [P, NDC])
        s_oma = sb("s_oma", [P, NDC])
        s_ioma = sb("s_ioma", [P, NDC])
        biasmix = sb("biasmix", [P, NDC])
        s_cb = sb("s_cb", [P, NDC])
        s_bb = sb("s_bb", [P, NDC])
        s_rb = sb("s_rb", [8, 1])
        s_invd = sb("s_invd", [P, P], F32R)
        s_onesz = sb("s_onesz", [8, P])
        s_brep = sb("s_brep", [8, 64])
        s_rrep = sb("s_rrep", [8, 64])
        epsb = sb("epsb", [P, 1])
        b151 = sb("b151", [8, 1])
        bm135 = sb("bm135", [8, 1])
        bufx = [sb(f"bufx{i}", [P, NDC, TT], F32R) for i in range(3)]
        xpad = [sb(f"xpad{i}", [P, NDC, TT + 4]) for i in range(2)]
        lstash = sb("lstash", [P, NDC, 2])
        sS = sb("sS", [P, NDC, TT])
        carry = sb("carry", [P, NDC, 1])
        shi = sb("shi0", [P, NDC, TT], F16)
        slo = sb("slo0", [P, NDC, TT], F16)
        sqb = sb("sqb", [P, NDC, TT], F32R)
        statr = sb("statr", [P, NDC, TT], F32R)
        convb = sb("convb", [P, NDC, TT])
        convb2 = sb("convb2", [P, NDC, TT])
        x2b = [sb("x2b0", [P, NDC, TT]), sb("x2b1", [P, NDC, TT])]
        hfb = sb("hfb", [P, NDC, TT])
        hpb = sb("hpb", [P, NDC, TT])
        hh = sb("hh", [P, NDC, TT], F16)
        hl = sb("hl", [P, NDC, TT], F16)
        tmA = sb("tmA", [P, TT])
        tmB = sb("tmB", [P, TT])
        mcA = sb("mcA", [P, TT])
        mcB = sb("mcB", [P, TT])
        e8b = sb("e8b", [8, TT])
        lnzb = sb("lnzb", [8, TT])
        gb = [sb(f"gb{i}", [8, TT]) for i in range(3)]
        ind8 = sb("ind8", [8, 1])
        omi8 = sb("omi8", [8, 1])
        ln2t = sb("ln2t", [8, 1])
        nln2t = sb("nln2t", [8, 1])
        rln2t = sb("rln2t", [8, 1])
        outerb = sb("outerb", [64, TT])
        oh = sb("oh", [64, TT], F16)
        ol = sb("ol", [64, TT], F16)
        wsc = [convb[:, :, 0:P], convb[:, :, P:2 * P]]
        s_invdf = x2b[0][:, 0, 0:P].rearrange("p t -> p t")
        s_dec = x2b[0][:, 1, 0:NDC]
        s_cw = x2b[0][:, 2, 0:40].rearrange("p (dc k) -> p dc k", k=5)
        pstat = [psf("pstat0"), psf("pstat1")]
        pmm = [psf(f"pmm{i}") for i in range(4)]
        ppk = psf("ppk")
        pzb = psf("pzb")

        sems = {k: es.enter_context(nc.semaphore(f"sem_{k}"))
                for k in ["v", "s", "t", "g", "dc", "dwA", "dwB",
                          "dx0", "dx1", "dx2", "dst", "o0", "o1"]}
        block = es.enter_context(nc.Block())

        def bc8(ap2d):
            return ap2d.unsqueeze(1).broadcast_to([P, NDC, TT])

        class Env:
            def __init__(self, active, eng):
                self.active = active
                self.eng = eng
                self.cnt = {k: 0 for k in sems}
                self.cp = {}
                self.last = None

            def emit(self, engine, fn):
                if self.active == engine:
                    self.last = fn(self.eng)

            def chk(self, engine, label, dma=False, via=None):
                self.cnt[engine] += 16 if dma else 1
                assert label not in self.cp, label
                self.cp[label] = (engine, self.cnt[engine])
                owner = via if via is not None else engine
                if self.active == owner:
                    self.last.then_inc(sems[engine], 16 if dma else 1)

            def wait(self, label, who=None):
                if who is not None and self.active != who:
                    return
                src, val = self.cp[label]
                if src == self.active:
                    return
                self.eng.wait_ge(sems[src], val)

            def nop(self, engine):
                if self.active == engine:
                    self.eng.engine_nop()

            def selfsync(self, engine):
                if self.active == engine:
                    self.eng.drain()

        def prep(env):
            def dma(dst, src, label, sem="dc"):
                env.emit("d", lambda e: e.dma_start(out=dst, in_=src))
                env.chk(sem, label, dma=True, via="d")

            dma(s_invd[:], dinvd, "d_invd")
            dma(s_invdf, dinvdf, "d_invdf")
            dma(s_dec, ddec, "d_dec")
            dma(s_cw, dcw, "d_cw")
            dma(biasmix[:], dba, "d_ba")
            dma(s_cb[:], dcb, "d_cb")
            dma(s_bb[:], dbb, "d_bb")
            dma(s_rb[:], drb, "d_rb")
            dma(s_onesz[:], donesz, "d_onesz")
            dma(s_brep[:], dbrep, "d_brep")
            dma(s_rrep[:], drrep, "d_rrep")
            env.cp["d_consts"] = ("dc", env.cnt["dc"])

            env.wait("d_consts", who="s")
            env.emit("s", lambda e: e.activation(s_a[:], s_dec, AF.Sigmoid))
            env.chk("s", "a_done")

            env.emit("v", lambda e: e.memset(ind8[:], 1.0))
            env.emit("v", lambda e: e.drain())
            env.emit("v", lambda e: e.tensor_single_scalar(ind8[:], ind8[:], float(2.0 ** -40), ALU.add))
            env.emit("v", lambda e: e.drain())
            env.emit("v", lambda e: e.tensor_single_scalar(ind8[:], ind8[:], 1.0, ALU.subtract))
            env.emit("v", lambda e: e.drain())
            env.emit("v", lambda e: e.tensor_single_scalar(ind8[:], ind8[:], float(2.0 ** 40), ALU.mult))
            env.emit("v", lambda e: e.drain())
            env.emit("v", lambda e: e.tensor_scalar(omi8[:], ind8[:], -1.0, 1.0, ALU.mult, ALU.add))
            env.emit("v", lambda e: e.memset(epsb[:], EPS))
            env.emit("v", lambda e: e.memset(carry[:], 0.0))
            env.emit("v", lambda e: e.memset(ln2t[:], 2.0))
            env.emit("v", lambda e: e.drain())
            env.wait("a_done", who="v")
            env.emit("v", lambda e: e.tensor_scalar(s_oma[:], s_a[:], -1.0, 1.0, ALU.mult, ALU.add))
            env.selfsync("v")
            env.emit("v", lambda e: e.reciprocal(s_ioma[:], s_oma[:]))
            env.selfsync("v")
            env.wait("d_consts", who="v")
            for k in range(5):
                env.emit("v", lambda e, k=k: e.tensor_tensor(cwp[:, :, k], s_cw[:, :, k], s_ioma[:], ALU.mult))
            env.wait("d_consts", who="v")
            env.wait("d_consts", who="v")
            env.emit("v", lambda e: e.tensor_tensor(biasmix[:], biasmix[:], s_cb[:], ALU.add))
            env.chk("v", "vconsts")
            env.wait("vconsts", who="s")
            env.emit("s", lambda e: e.activation(ln2t[:], ln2t[:], AF.Ln))
            env.chk("s", "ln2done")
            env.wait("ln2done", who="v")
            env.emit("v", lambda e: e.tensor_single_scalar(b151[:], ln2t[:], 151.0, ALU.mult))
            env.emit("v", lambda e: e.tensor_single_scalar(bm135[:], ln2t[:], -135.0, ALU.mult))
            env.emit("v", lambda e: e.tensor_single_scalar(nln2t[:], ln2t[:], -1.0, ALU.mult))
            env.emit("v", lambda e: e.drain())
            env.emit("v", lambda e: e.reciprocal(rln2t[:], ln2t[:]))
            env.chk("v", "ln2fam")

            for wi, (src, dsth, dstl, center) in enumerate(
                    [] if kpr else [(dwt, wh, wl, False), (dbwt, bh, bl, True)]):
                for oc in range(NDC):
                    scr = wsc[oc % 2]
                    pwi, poc = (wi, oc - 2) if oc >= 2 else (wi - 1, NDC - 2 + oc)
                    if pwi >= 0:
                        env.wait(f"w{pwi}l_{poc}", who="d")
                    dma(scr, src[:, :, oc * P:(oc + 1) * P], f"d_w{wi}_{oc}", sem=('dwA', 'dwB')[oc % 2])
                    if center:
                        env.wait(f"d_w{wi}_{oc}", who="t")
                        if oc >= 1:
                            env.wait(f"w{wi}c_{oc - 1}", who="t")
                        env.wait("d_consts", who="t")
                        for kc in range(NDC):
                            env.emit("t", lambda e, kc=kc, scr=scr: e.matmul(
                                pstat[0][:, 0:P], s_invdf,
                                scr[:, kc, :],
                                start=(kc == 0), stop=(kc == NDC - 1)))
                        env.chk("t", f"w{wi}m_{oc}")
                        env.wait(f"w{wi}m_{oc}", who="v")
                        for kc in range(NDC):
                            env.emit("v", lambda e, kc=kc, scr=scr: e.tensor_tensor(
                                scr[:, kc, :], scr[:, kc, :],
                                pstat[0][:, 0:P], ALU.subtract))
                        env.selfsync("v")
                        env.chk("v", f"w{wi}c_{oc}")
                        env.wait(f"w{wi}c_{oc}", who="s")
                    else:
                        env.wait(f"d_w{wi}_{oc}", who="s")
                    env.emit("s", lambda e, oc=oc, scr=scr, dsth=dsth: e.copy(dsth[:, :, oc, :], scr))
                    env.chk("s", f"w{wi}h_{oc}")
                    env.wait(f"w{wi}h_{oc}", who="v")
                    env.emit("v", lambda e, oc=oc, scr=scr, dsth=dsth, dstl=dstl: e.tensor_tensor(
                        dstl[:, :, oc, :], scr, dsth[:, :, oc, :], ALU.subtract))
                    env.chk("v", f"w{wi}l_{oc}")

            if kpr:
                env.cp["prep_s"] = env.cp["a_done"]
                env.cp["prep_t"] = ("t", 0)
                env.emit("v", lambda e: e.engine_nop()); env.chk("v", "prep_v")
                env.emit("g", lambda e: e.engine_nop()); env.chk("g", "prep_g")
                for k in ["v", "s", "t", "g"]:
                    env.wait(f"prep_{k}")
                dma(bufx[0][:], dx[:, 0:TT].rearrange("(dc p) t -> p dc t", p=P), "d_x0", sem="dx0")
                return
            env.wait("w1l_6", who="d")
            dma(wsc[0][:, :, 0:8], drt, "d_rt", sem="dwA")
            env.wait("w1l_7", who="d")
            dma(wsc[0][:, :, 8:16], dla, "d_la", sem="dwB")
            env.wait("d_rt", who="t")
            env.wait("d_la", who="t")
            env.wait("w1c_7", who="t")
            for kc in range(NDC):
                env.emit("t", lambda e, kc=kc: e.matmul(
                    pstat[0][:, 0:16], s_invdf, wsc[0][:, kc, 0:16],
                    start=(kc == 0), stop=(kc == NDC - 1)))
            env.chk("t", "rtm")
            env.wait("rtm", who="v")
            for kc in range(NDC):
                env.emit("v", lambda e, kc=kc: e.tensor_tensor(
                    wsc[0][:, kc, 0:16], wsc[0][:, kc, 0:16], pstat[0][:, 0:16], ALU.subtract))
            env.chk("v", "rtc")
            env.emit("v", lambda e: e.memset(sph[:], 0.0))
            env.emit("v", lambda e: e.memset(spl[:], 0.0))
            env.chk("v", "spz")
            env.wait("rtc", who="s")
            env.wait("spz", who="s")
            env.emit("s", lambda e: e.copy(sph[:, :, 0:8], wsc[0][:, :, 0:8]))
            env.emit("s", lambda e: e.copy(sph[:, :, 32:40], wsc[0][:, :, 8:16]))
            env.chk("s", "rth")
            env.wait("rth", who="v")
            env.emit("v", lambda e: e.tensor_tensor(spl[:, :, 0:8], wsc[0][:, :, 0:8], sph[:, :, 0:8], ALU.subtract))
            env.emit("v", lambda e: e.tensor_tensor(spl[:, :, 32:40], wsc[0][:, :, 8:16], sph[:, :, 32:40], ALU.subtract))
            env.chk("v", "rtl")

            env.wait("rtl", who="d")
            lsc = convb[0:64, :, P:2 * P]
            dma(lsc, dlb, "d_lb", sem="dwB")
            env.wait("d_lb", who="v")
            env.emit("v", lambda e: e.tensor_single_scalar(lsc, lsc, 2.0, ALU.mult))
            env.selfsync("v")
            env.chk("v", "lb2")
            env.wait("lb2", who="s")
            env.emit("s", lambda e: e.copy(lbh[:], lsc))
            env.chk("s", "lbh")
            env.wait("lbh", who="v")
            env.emit("v", lambda e: e.tensor_tensor(lbl[:], lsc, lbh[:], ALU.subtract))
            env.chk("v", "lbl")

            dma(bufx[0][:], dx[:, 0:TT].rearrange("(dc p) t -> p dc t", p=P), "d_x0", sem="dx0")

            env.emit("v", lambda e: e.engine_nop()); env.chk("v", "prep_v")
            env.cp["prep_s"] = env.cp["lbh"]
            env.cp["prep_t"] = env.cp["rtm"]
            env.emit("g", lambda e: e.engine_nop()); env.chk("g", "prep_g")
            for k in ["v", "s", "t", "g"]:
                env.wait(f"prep_{k}")

        def stageA(env, j):
            halo = (j == NT_A - 1)
            xb = bufx[j % 3]
            xp = xpad[j % 2]
            pst = pstat[0]
            if j + 1 < NT_A:
                if j >= 2:
                    env.wait(f"stat1_{j - 2}", who="d")
                    env.wait(f"xp_{j - 2}", who="d")
                    if 2 <= j - 2 <= NT_A - 2 and (j - 2) - 2 < nb:
                        env.wait(f"x2_{j - 2}", who="d")
                env.emit("d", lambda e, j=j: e.dma_start(
                    out=bufx[(j + 1) % 3][:],
                    in_=dx[:, (j + 1) * TT:(j + 2) * TT].rearrange("(dc p) t -> p dc t", p=P)))
                env.chk(f"dx{(j + 1) % 3}", f"d_x{j + 1}", dma=True, via="d")
            env.wait(f"d_x{j}", who="s")
            if j >= 1:
                env.wait(f"xp_{j - 1}", who="s")
            if j >= 4 and has_b(j - 2):
                env.wait(f"stat2_{j - 2}", who="s")
            env.emit("s", lambda e: e.activation(sqb[:], xb[:].bitcast(F32), AF.Square))
            env.chk("s", f"sqx_{j}")
            if ksa < 2:
                return
            env.wait(f"d_x{j}", who="t")
            env.wait(f"sqx_{j}", who="t")
            if j >= 4 and (j - 2) - 2 < nb:
                env.wait(f"tm2_{j - 2}", who="t")
                env.wait(f"t2_{j - 2}", who="t")
            elif j >= 2:
                env.wait(f"xp_{j - 2}", who="t")
            for kc in range(NDC):
                env.emit("t", lambda e, kc=kc: e.matmul(
                    pst[:, 0:TT], s_invd[:], xb[:, kc, :],
                    start=(kc == 0), stop=(kc == NDC - 1)))
            for kc in range(NDC):
                env.emit("t", lambda e, kc=kc: e.matmul(
                    pst[:, TT:2 * TT], s_invd[:], sqb[:, kc, :],
                    start=(kc == 0), stop=(kc == NDC - 1)))
            env.chk("t", f"stat1_{j}")
            if ksa < 3:
                return
            env.wait(f"stat1_{j}", who="s")
            env.emit("s", lambda e: e.activation(tmA[:], pst[:, 0:TT], AF.Square))
            env.chk("s", f"tm1_{j}")
            env.wait(f"tm1_{j}", who="v")
            env.emit("v", lambda e: e.tensor_tensor(tmA[:], pst[:, TT:2 * TT], tmA[:], ALU.subtract))
            env.chk("v", f"wc1_{j}")
            env.wait(f"wc1_{j}", who="s")
            env.emit("s", lambda e: e.activation(tmA[:], tmA[:], AF.Ln, bias=epsb[:]))
            env.selfsync("s")
            env.emit("s", lambda e: e.activation(tmA[:], tmA[:], AF.Exp, scale=-0.5))
            env.chk("s", f"r1_{j}")
            if ksa < 4:
                return
            env.emit("v", lambda e: e.tensor_copy(mcA[:], pst[:, 0:TT]))
            env.selfsync("v")
            env.emit("v", lambda e: e.tensor_tensor(xp[:, :, 2:2 + TT], xb[:].bitcast(F32), bc8(mcA[:]), ALU.subtract))
            env.chk("v", f"u1_{j}")
            env.wait(f"r1_{j}", who="v")
            env.selfsync("v")
            for dc in range(NDC):
                env.emit("v", lambda e, dc=dc: e.scalar_tensor_tensor(
                    xp[:, dc, 2:2 + TT], xp[:, dc, 2:2 + TT], s_oma[:, dc:dc + 1], tmA[:],
                    ALU.mult, ALU.mult))
            env.selfsync("v")
            env.chk("v", f"xp_{j}")
            if j == kdbt:
                dbgdump(env, "xp_2", xp[:], TT + 4)
                dbgdump(env, "m_2", mcA[:], -1, wlabel=f"u1_{j}")
                dbgdump(env, "r_2", tmA[:], -1, wlabel=f"r1_{j}")
                dbgdump(env, "sq_2", sqb[:].bitcast(F32), TT, wlabel=f"sqx_{j}")
                dbgdump(env, "xb_2", xb[:].bitcast(F32), TT, wlabel=f"d_x{j}")
            if j == 1:
                env.emit("v", lambda e: e.tensor_copy(lstash[:], xp[:, :, TT:TT + 2]))
            if halo or (ksa < 6 and ksa != 45):
                return
            if j - 1 >= 2:
                env.wait(f"shi_{j - 1}", who="v")
                env.wait(f"slo_{j - 1}", who="v")
            for dc in range(NDC):
                init = 0.0 if j == 0 else carry[:, dc, 0:1]
                env.emit("v", lambda e, dc=dc, init=init: e.tensor_tensor_scan(
                    sS[:, dc, :], s_a[:, dc:dc + 1].broadcast_to([P, TT]),
                    xp[:, dc, 2:2 + TT], init, ALU.mult, ALU.add))
            env.selfsync("v")
            env.emit("v", lambda e: e.tensor_copy(carry[:], sS[:, :, TT - 1:TT]))
            env.chk("v", f"scan_{j}")
            if j == kdbt:
                dbgdump(env, "scan_2", sS[:], TT)

        def stageA2(env, j):
            if j < 2 or j == NT_A - 1 or ksa < 6:
                return
            env.wait(f"scan_{j}", who="s")
            if has_mix(j - 1):
                env.wait(f"wmm_{j - 1}_7", who="s")
            env.emit("s", lambda e: e.copy(shi[:], sS[:]))
            env.chk("s", f"shi_{j}")
        def stageA2b(env, j):
            if j < 2 or j == NT_A - 1 or ksa < 6:
                return
            env.wait(f"shi_{j}", who="v")
            if has_mix(j - 1):
                env.wait(f"wmm_{j - 1}_7", who="v")
            env.emit("v", lambda e: e.tensor_tensor(slo[:], sS[:], shi[:], ALU.subtract))
            env.chk("v", f"slo_{j}")

        def has_mix(m):
            return m == 2 or (3 <= m <= NT_A - 2 and (m - 3) < nb)

        def has_b(q):
            return q >= 2 and q - 2 < nb

        def cvb(m):
            return convb2 if m % 2 == 0 else convb

        def mix_chunk(env, m, oc):
            bank = pmm[oc % 2]
            if oc == 0:
                env.wait(f"shi_{m}", who="t")
                env.wait(f"slo_{m}", who="t")
            if oc >= 2:
                env.wait(f"mevac_{m}_{oc - 2}", who="t")
            elif has_mix(m - 1):
                env.wait(f"mevac_{m - 1}_{6 + oc}", who="t")
            for kc in range(NDC):
                env.emit("t", lambda e, kc=kc, oc=oc, bank=bank: e.matmul(
                    bank[:, 0:TT], wh[:, kc, oc, :], shi[:, kc, :], start=(kc == 0), stop=False))
                env.emit("t", lambda e, kc=kc, oc=oc, bank=bank: e.matmul(
                    bank[:, 0:TT], wh[:, kc, oc, :], slo[:, kc, :], start=False, stop=False))
            for kc in range(NDC):
                env.emit("t", lambda e, kc=kc, oc=oc, bank=bank: e.matmul(
                    bank[:, 0:TT], wl[:, kc, oc, :], shi[:, kc, :], start=False,
                    stop=(kc == NDC - 1)))
            env.chk("t", f"wmm_{m}_{oc}")
            if m == kdbt and oc == 2:
                dbgdump(env, "shi_2", shi[:].bitcast(F32), 128, wlabel=f"wmm_{m}_{oc}")
                dbgdump(env, "slo_2", slo[:].bitcast(F32), 128, wlabel=f"wmm_{m}_{oc}")
            env.wait(f"wmm_{m}_{oc}", who="s")
            if oc == 0 and m >= 4 and (m - 2) - 2 < nb:
                env.wait(f"x2_{m - 2}", who="s")
            env.emit("s", lambda e, oc=oc, bank=bank, m=m: e.activation(
                cvb(m)[:, oc, :], bank[:, 0:TT], AF.Identity, bias=biasmix[:, oc:oc + 1]))
            env.chk("s", f"mevac_{m}_{oc}")
            if m == kdbt and oc == 7:
                dbgdump(env, "ymm_2", cvb(m)[:], TT, wlabel=f"mevac_{m}_{oc}")

        def stageB_conv(env, j):
            xp = xpad[j % 2]
            xpn = xpad[(j + 1) % 2]
            cv = cvb(j)
            env.emit("v", lambda e: e.tensor_copy(xp[:, :, 0:2], lstash[:]))
            env.emit("v", lambda e: e.tensor_copy(xp[:, :, 2 + TT:4 + TT], xpn[:, :, 2:4]))
            env.selfsync("v")
            env.emit("v", lambda e: e.tensor_copy(lstash[:], xp[:, :, TT:TT + 2]))
            env.selfsync("v")
            env.wait(f"mevac_{j}_7", who="v")
            for k in range(5):
                if k:
                    env.selfsync("v")
                for dc in range(NDC):
                    env.emit("v", lambda e, dc=dc, k=k: e.scalar_tensor_tensor(
                        cv[:, dc, :], xp[:, dc, k:k + TT], cwp[:, dc, k:k + 1],
                        cv[:, dc, :], ALU.mult, ALU.add))
            env.selfsync("v")
            env.chk("v", f"conv_{j}")

        def stageB_head(env, j):
            f = j - 2
            m = j + 1
            emb = has_mix(m)
            xb = bufx[j % 3]
            cv = cvb(j)
            xv = x2b[j % 2]

            if j == kdbt:
                dbgdump(env, "y_2", cv[:], TT, wlabel=f"conv_{j}")

            env.wait(f"conv_{j}", who="s")
            if has_b(j - 1):
                env.wait(f"stat3_{j - 1}", who="s")
            else:
                env.wait(f"stat1_{j + 1}", who="s")
            env.emit("s", lambda e: e.activation(sqb[:], cv[:], AF.Square))
            env.chk("s", f"sqy_{j}")
            env.wait(f"conv_{j}", who="d")
            if has_b(j - 1):
                env.wait(f"stat3_{j - 1}", who="d")
            env.emit("d", lambda e: e.dma_start(out=statr[:], in_=cv[:].bitcast(F32R)))
            env.chk("dst", f"dcy_{j}", dma=True, via="d")
            pst = pstat[0]
            env.wait(f"sqy_{j}", who="t")
            env.wait(f"dcy_{j}", who="t")
            env.wait(f"xp_{j + 1}", who="t")
            env.wait(f"tm1_{j + 1}", who="t")
            for kc in range(NDC):
                env.emit("t", lambda e, kc=kc: e.matmul(
                    pst[:, 0:TT], s_invd[:], statr[:, kc, :],
                    start=(kc == 0), stop=(kc == NDC - 1)))
            for kc in range(NDC):
                env.emit("t", lambda e, kc=kc: e.matmul(
                    pst[:, TT:2 * TT], s_invd[:], sqb[:, kc, :],
                    start=(kc == 0), stop=(kc == NDC - 1)))
            env.chk("t", f"stat2_{j}")
            env.wait(f"stat2_{j}", who="s")
            if has_b(j - 1):
                env.wait(f"hp_{j - 1}", who="s")
            env.emit("s", lambda e: e.activation(tmB[:], pst[:, 0:TT], AF.Square))
            env.chk("s", f"tm2_{j}")
            env.wait(f"tm2_{j}", who="v")
            env.emit("v", lambda e: e.tensor_tensor(tmB[:], pst[:, TT:2 * TT], tmB[:], ALU.subtract))
            env.chk("v", f"wc2_{j}")
            env.wait(f"wc2_{j}", who="s")
            env.emit("s", lambda e: e.activation(tmB[:], tmB[:], AF.Ln, bias=epsb[:]))
            env.selfsync("s")
            env.emit("s", lambda e: e.activation(tmB[:], tmB[:], AF.Exp, scale=-0.5))
            env.chk("s", f"r2_{j}")
            env.emit("v", lambda e: e.tensor_copy(mcB[:], pst[:, 0:TT]))
            env.selfsync("v")
            env.emit("v", lambda e: e.tensor_tensor(cv[:], cv[:], bc8(mcB[:]), ALU.subtract))
            env.wait(f"r2_{j}", who="v")
            env.selfsync("v")
            env.emit("v", lambda e: e.tensor_tensor(cv[:], cv[:], bc8(tmB[:]), ALU.mult))
            env.chk("v", f"t2_{j}")
            if emb:
                mix_chunk(env, m, 0)
                mix_chunk(env, m, 1)
            env.wait(f"t2_{j}", who="g")
            if has_b(j - 2):
                env.wait(f"hp_{j - 2}", who="g")
                env.wait(f"dcx2_{j - 2}", who="g")
            env.selfsync("g")
            env.emit("g", lambda e: e.tensor_tensor(xv[:], cv[:], xb[:].bitcast(F32), ALU.add))
            env.chk("g", f"x2_{j}")

            if j == kdbt:
                dbgdump(env, "x2_2", xv[:], TT)

            if emb:
                mix_chunk(env, m, 2)

        def stageB_head2a(env, j):
            f = j - 2
            xv = x2b[j % 2]

            env.wait(f"x2_{j}", who="s")
            if has_b(j - 1):
                env.wait(f"stat4_{j - 1}", who="s")
            else:
                env.wait(f"stat1_{j + 2}", who="s")
            env.emit("s", lambda e: e.activation(sqb[:], xv[:], AF.Square))
            env.chk("s", f"sqx2_{j}")
            env.wait(f"x2_{j}", who="d")
            if has_b(j - 1):
                env.wait(f"stat4_{j - 1}", who="d")
            env.emit("d", lambda e: e.dma_start(out=statr[:], in_=xv[:].bitcast(F32R)))
            env.chk("dst", f"dcx2_{j}", dma=True, via="d")
            pst3 = pstat[1]
            env.wait(f"sqx2_{j}", who="t")
            env.wait(f"dcx2_{j}", who="t")
            if f > 0:
                env.wait(f"tm4_{j - 1}", who="t")
                env.wait(f"t4_{j - 1}", who="t")
            else:
                env.wait(f"xp_{j}", who="t")
                env.wait(f"tm1_{j}", who="t")
            for kc in range(NDC):
                env.emit("t", lambda e, kc=kc: e.matmul(
                    pst3[:, 0:TT], s_invd[:], statr[:, kc, :],
                    start=(kc == 0), stop=(kc == NDC - 1)))
            for kc in range(NDC):
                env.emit("t", lambda e, kc=kc: e.matmul(
                    pst3[:, TT:2 * TT], s_invd[:], sqb[:, kc, :],
                    start=(kc == 0), stop=(kc == NDC - 1)))
            env.chk("t", f"stat3_{j}")
            env.wait(f"stat3_{j}", who="s")
            env.emit("s", lambda e: e.activation(tmB[:], pst3[:, 0:TT], AF.Square))
            env.chk("s", f"tm3_{j}")
            env.wait(f"tm3_{j}", who="v")
            env.emit("v", lambda e: e.tensor_tensor(tmB[:], pst3[:, TT:2 * TT], tmB[:], ALU.subtract))
            env.chk("v", f"wc3_{j}")
            env.wait(f"wc3_{j}", who="s")
            env.emit("s", lambda e: e.activation(tmB[:], tmB[:], AF.Ln, bias=epsb[:]))
            env.selfsync("s")
            env.emit("s", lambda e: e.activation(tmB[:], tmB[:], AF.Exp, scale=-0.5))
            env.chk("s", f"r3_{j}")
            env.wait(f"r3_{j}", who="v")
            if has_b(j - 1):
                env.wait(f"hl_{j - 1}", who="v")
            env.emit("v", lambda e: e.tensor_tensor(hpb[:], xv[:], bc8(tmB[:]), ALU.mult))
            env.chk("v", f"hp_{j}")
            env.wait(f"hp_{j}", who="s")
            if f > 0:
                env.wait(f"ffn_{j - 1}_7", who="s")
            env.emit("s", lambda e: e.copy(hh[:], hpb[:]))
            env.chk("s", f"hh_{j}")
            env.wait(f"hh_{j}", who="g")
            if f > 0:
                env.wait(f"ffn_{j - 1}_7", who="g")
            env.emit("g", lambda e: e.tensor_tensor(hl[:], hpb[:], hh[:], ALU.subtract))
            env.chk("g", f"hl_{j}")

        def stageB_head2b(env, j):
            f = j - 2
            m = j + 2
            emb = has_mix(m)

            if emb:
                mix_chunk(env, m, 3)
                mix_chunk(env, m, 4)

            env.wait(f"hh_{j}", who="t")
            env.wait(f"hl_{j}", who="t")
            if f > 0:
                env.wait(f"outer_{j - 1}", who="t")
            for kc in range(NDC):
                env.emit("t", lambda e, kc=kc: e.matmul(
                    ppk[0:40, 0:TT], sph[:, kc, :], hh[:, kc, :], start=(kc == 0), stop=False))
                env.emit("t", lambda e, kc=kc: e.matmul(
                    ppk[0:40, 0:TT], sph[:, kc, :], hl[:, kc, :], start=False, stop=False))
            for kc in range(NDC):
                env.emit("t", lambda e, kc=kc: e.matmul(
                    ppk[0:40, 0:TT], spl[:, kc, :], hh[:, kc, :], start=False,
                    stop=(kc == NDC - 1)))
            env.chk("t", f"pack_{j}")

            env.wait(f"pack_{j}", who="s")
            if f > 0:
                env.wait(f"gq8_{j - 1}", who="s")
            env.emit("s", lambda e: e.activation(e8b[:], ppk[0:8, 0:TT], AF.Exp, bias=s_rb[:]))
            env.chk("s", f"e8_{j}")
            if emb:
                mix_chunk(env, m, 5)
            env.wait(f"e8_{j}", who="t")
            env.emit("t", lambda e: e.matmul(pzb[:, 0:TT], s_onesz[:], e8b[:], start=True, stop=True))
            env.chk("t", f"zbc_{j}")
            env.wait(f"zbc_{j}", who="s")
            if f > 0:
                env.wait(f"gr_{j - 1}", who="s")
            env.emit("s", lambda e: e.activation(lnzb[:], pzb[0:8, 0:TT], AF.Ln))
            env.chk("s", f"lnz_{j}")
            env.wait(f"lnz_{j}", who="v")
            if f > 0:
                env.wait(f"gq8_{j - 1}", who="v")
                env.wait(f"reps_{j - 1}", who="v")
            env.emit("v", lambda e: e.scalar_tensor_tensor(
                gb[0][:], ppk[0:8, 0:TT], s_rb[:], lnzb[:], ALU.add, ALU.subtract))
            env.selfsync("v")
            env.emit("v", lambda e: e.tensor_scalar(
                gb[1][:], gb[0][:], rln2t[:], 127.5, ALU.mult, ALU.add))
            env.chk("v", f"ge2p_{j}")

            def rint_blend(src_t, dst):
                env.selfsync("v")
                env.emit("v", lambda e: e.tensor_single_scalar(e8b[:], src_t[:], C52, ALU.add))
                env.emit("v", lambda e: e.tensor_single_scalar(lnzb[:], src_t[:], C23, ALU.add))
                env.selfsync("v")
                env.emit("v", lambda e: e.tensor_single_scalar(e8b[:], e8b[:], C52, ALU.subtract))
                env.emit("v", lambda e: e.tensor_single_scalar(lnzb[:], lnzb[:], C23, ALU.subtract))
                env.selfsync("v")
                env.emit("v", lambda e: e.tensor_scalar(e8b[:], e8b[:], ind8[:], None, ALU.mult))
                env.emit("v", lambda e: e.tensor_scalar(lnzb[:], lnzb[:], omi8[:], None, ALU.mult))
                env.selfsync("v")
                env.emit("v", lambda e: e.tensor_tensor(dst[:], e8b[:], lnzb[:], ALU.add))

            env.wait(f"zbc_{j}", who="v")
            rint_blend(gb[1], gb[2])
            env.selfsync("v")
            env.emit("v", lambda e: e.scalar_tensor_tensor(
                gb[1][:], gb[2][:], nln2t[:], gb[0][:], ALU.mult, ALU.add))
            env.chk("v", f"gs_{j}")
            env.wait(f"gs_{j}", who="s")
            env.emit("s", lambda e: e.activation(gb[0][:], gb[1][:], AF.Exp, bias=b151[:]))
            env.chk("s", f"gv32_{j}")
            if emb:
                mix_chunk(env, m, 6)
            env.wait(f"gv32_{j}", who="v")
            rint_blend(gb[0], gb[1])
            env.selfsync("v")
            env.emit("v", lambda e: e.tensor_single_scalar(gb[1][:], gb[1][:], float(2.0 ** -16), ALU.mult))
            env.selfsync("v")
            rint_blend(gb[1], gb[0])
            env.chk("v", f"gq8_{j}")
            env.wait(f"gq8_{j}", who="s")
            env.emit("s", lambda e: e.activation(lnzb[:], gb[2][:], AF.Exp, scale=ln2t[:], bias=bm135[:]))
            env.chk("s", f"gu_{j}")
            if emb:
                mix_chunk(env, m, 7)
            env.wait(f"gu_{j}", who="v")
            env.selfsync("v")
            env.emit("v", lambda e: e.tensor_tensor(gb[1][:], gb[0][:], lnzb[:], ALU.mult))
            env.chk("v", f"gr_{j}")
            env.emit("v", lambda e: e.tensor_copy(gb[2][:], ppk[32:40, 0:TT]))
            env.chk("v", f"mide_{j}")

            env.wait(f"gr_{j}", who="t")
            env.wait(f"mide_{j}", who="t")
            env.emit("t", lambda e: e.matmul(ppk[0:64, TT:2 * TT], s_brep[:], gb[1][:], start=True, stop=True))
            env.emit("t", lambda e: e.matmul(pzb[0:64, TT:2 * TT], s_rrep[:], gb[2][:], start=True, stop=True))
            env.chk("t", f"reps_{j}")
            env.wait(f"reps_{j}", who="v")
            if f > 0:
                env.wait(f"ol_{j - 1}", who="v")
            env.emit("v", lambda e: e.tensor_copy(outerb[:], ppk[0:64, TT:2 * TT]))
            env.selfsync("v")
            env.emit("v", lambda e: e.tensor_tensor(outerb[:], outerb[:], pzb[0:64, TT:2 * TT], ALU.mult))
            env.chk("v", f"outer_{j}")
            env.wait(f"outer_{j}", who="s")
            if f > 0:
                env.wait(f"ffn_{j - 1}_7", who="s")
            env.emit("s", lambda e: e.copy(oh[:], outerb[:]))
            env.chk("s", f"oh_{j}")
            env.wait(f"oh_{j}", who="g")
            env.emit("g", lambda e: e.tensor_tensor(ol[:], outerb[:], oh[:], ALU.subtract))
            env.chk("g", f"ol_{j}")


        def stageB_tail(env, j):
            f = j - 2
            for oc in range(NDC):
                bank = pmm[2 + oc % 2]
                if oc == 0 and has_b(j - 1):
                    env.wait(f"dout_{j - 1}", who="s")
                if oc >= 2:
                    env.wait(f"ffnevac_{j}_{oc - 2}", who="t")
                elif f > 0:
                    env.wait(f"ffnevac_{j - 1}_{6 + oc}", who="t")
                for kc in range(NDC):
                    env.emit("t", lambda e, kc=kc, oc=oc, bank=bank: e.matmul(
                        bank[:, 0:TT], bh[:, kc, oc, :], hh[:, kc, :], start=(kc == 0), stop=False))
                    env.emit("t", lambda e, kc=kc, oc=oc, bank=bank: e.matmul(
                        bank[:, 0:TT], bh[:, kc, oc, :], hl[:, kc, :], start=False, stop=False))
                for kc in range(NDC):
                    env.emit("t", lambda e, kc=kc, oc=oc, bank=bank: e.matmul(
                        bank[:, 0:TT], bl[:, kc, oc, :], hh[:, kc, :], start=False, stop=False))
                if oc == 0:
                    env.wait(f"oh_{j}", who="t")
                    env.wait(f"ol_{j}", who="t")
                env.emit("t", lambda e, oc=oc, bank=bank: e.matmul(
                    bank[:, 0:TT], lbh[:, oc, :], oh[:], start=False, stop=False))
                env.emit("t", lambda e, oc=oc, bank=bank: e.matmul(
                    bank[:, 0:TT], lbh[:, oc, :], ol[:], start=False, stop=False))
                env.emit("t", lambda e, oc=oc, bank=bank: e.matmul(
                    bank[:, 0:TT], lbl[:, oc, :], oh[:], start=False, stop=True))
                env.chk("t", f"ffn_{j}_{oc}")
                env.wait(f"ffn_{j}_{oc}", who="s")
                env.emit("s", lambda e, oc=oc, bank=bank: e.activation(
                    hfb[:, oc, :], bank[:, 0:TT], AF.Identity, bias=s_bb[:, oc:oc + 1]))
                env.chk("s", f"ffnevac_{j}_{oc}")

            if j == kdbt:
                dbgdump(env, "f_2", hfb[:], TT, wlabel=f"ffnevac_{j}_{NDC-1}")
            env.wait(f"stat3_{j}", who="s")
            if j + 3 < min(na, NT_A):
                env.wait(f"stat1_{j + 3}", who="s")
            env.selfsync("s")
            env.emit("s", lambda e: e.activation(sqb[:], hfb[:], AF.Square))
            env.chk("s", f"sqf_{j}")
            env.wait(f"ffnevac_{j}_{NDC - 1}", who="d")
            if has_b(j + 1):
                env.wait(f"stat2_{j + 1}", who="d")
            else:
                env.wait(f"stat3_{j}", who="d")
            env.emit("d", lambda e: e.dma_start(out=statr[:], in_=hfb[:].bitcast(F32R)))
            env.chk("dst", f"dcf_{j}", dma=True, via="d")
            pst4 = pstat[1]
            env.wait(f"sqf_{j}", who="t")
            env.wait(f"dcf_{j}", who="t")
            env.wait(f"wc3_{j}", who="t")
            env.wait(f"tm3_{j}", who="t")
            for kc in range(NDC):
                env.emit("t", lambda e, kc=kc: e.matmul(
                    pst4[:, 0:TT], s_invd[:], statr[:, kc, :],
                    start=(kc == 0), stop=(kc == NDC - 1)))
            for kc in range(NDC):
                env.emit("t", lambda e, kc=kc: e.matmul(
                    pst4[:, TT:2 * TT], s_invd[:], sqb[:, kc, :],
                    start=(kc == 0), stop=(kc == NDC - 1)))
            env.chk("t", f"stat4_{j}")
            env.wait(f"stat4_{j}", who="s")
            env.emit("s", lambda e: e.activation(tmB[:], pst4[:, 0:TT], AF.Square))
            env.chk("s", f"tm4_{j}")

        def stageB_ln4(env, j):
            f = j - 2
            pst4 = pstat[1]
            env.wait(f"tm4_{j}", who="v")
            env.emit("v", lambda e: e.tensor_tensor(tmB[:], pst4[:, TT:2 * TT], tmB[:], ALU.subtract))
            env.chk("v", f"wc4_{j}")
            env.wait(f"wc4_{j}", who="s")
            env.emit("s", lambda e: e.activation(tmB[:], tmB[:], AF.Ln, bias=epsb[:]))
            env.selfsync("s")
            env.emit("s", lambda e: e.activation(tmB[:], tmB[:], AF.Exp, scale=-0.5))
            env.chk("s", f"r4_{j}")
            env.emit("v", lambda e: e.tensor_copy(mcB[:], pst4[:, 0:TT]))
            env.selfsync("v")
            env.emit("v", lambda e: e.tensor_tensor(hfb[:], hfb[:], bc8(mcB[:]), ALU.subtract))
            env.wait(f"r4_{j}", who="v")
            env.selfsync("v")
            env.emit("v", lambda e: e.tensor_tensor(hfb[:], hfb[:], bc8(tmB[:]), ALU.mult))
            env.chk("v", f"t4_{j}")
            env.wait(f"t4_{j}", who="g")
            env.emit("g", lambda e, j=j: e.tensor_tensor(hfb[:], hfb[:], x2b[j % 2][:], ALU.add))
            env.chk("g", f"outres_{j}")
        def stageB_dout(env, j):
            f = j - 2
            if j == 15:
                dbgdump(env, "wh_2", wh[:, :, 2, :].bitcast(F32), 64, wlabel=f"outres_{j}")
            env.wait(f"outres_{j}", who="d")
            env.emit("d", lambda e, f=f: e.dma_start(
                out=dout[:, f * TT:(f + 1) * TT].rearrange("(dc p) t -> p dc t", p=P),
                in_=hfb[:]))
            env.chk(f"o{j % 2}", f"dout_{j}", dma=True, via="d")

        def dbgdump(env, label, buf, n3, wlabel=None):
            if kdbg != label:
                return
            env.wait(wlabel or label, who="d")
            dst = ddbg[:, 0, 0:TT] if n3 == -1 else ddbg[:, :, 0:n3]
            env.emit("d", lambda e: e.dma_start(out=dst, in_=buf))
            env.chk("dc", "dbgd", dma=True, via="d")

        def script(env):
            prep(env)
            last_j = min(na, NT_A) - 1
            for j in range(min(na, NT_A)):
                stageA(env, j)
                if has_b(j - 3):
                    stageB_tail(env, j - 3)
                stageA2(env, j)
                if has_b(j - 1):
                    stageB_conv(env, j - 1)
                if has_b(j - 3):
                    stageB_ln4(env, j - 3)
                stageA2b(env, j)
                if j == 2 and ksa >= 6:
                    for oc in range(NDC):
                        mix_chunk(env, 2, oc)
                if has_b(j - 2):
                    stageB_head2a(env, j - 2)
                if has_b(j - 1):
                    stageB_head(env, j - 1)
                if j == 3 and ksa >= 6 and has_mix(3):
                    for oc in range(3, NDC):
                        mix_chunk(env, 3, oc)
                if has_b(j - 2):
                    stageB_head2b(env, j - 2)
                if has_b(j - 3):
                    stageB_dout(env, j - 3)
            for q in (last_j - 2, last_j - 1):
                if has_b(q):
                    if q == last_j - 1:
                        stageB_head2a(env, q)
                        stageB_head2b(env, q)
                    stageB_tail(env, q)
                    stageB_ln4(env, q)
                    stageB_dout(env, q)
            if env.active == "d":
                env.eng.wait_ge(sems["o0"], env.cnt["o0"])
                env.eng.wait_ge(sems["o1"], env.cnt["o1"])

        @block.sync
        def _(sync):
            script(Env("d", sync))

        @block.tensor
        def _(tensor):
            script(Env("t", tensor))

        @block.vector
        def _(vector):
            script(Env("v", vector))

        @block.scalar
        def _(scalar):
            script(Env("s", scalar))

        @block.gpsimd
        def _(gpsimd):
            script(Env("g", gpsimd))

    return nc


_CACHED = {}


def kernel(L0_attn_x, L0_attn_decay_logit, L0_attn_W, L0_attn_b, L0_attn_conv_w,
           L0_attn_conv_b, L0_ffn_base_w, L0_ffn_base_b, L0_ffn_lora_A,
           L0_ffn_lora_B, L0_ffn_router_w, L0_ffn_router_b):
    x = np.ascontiguousarray(np.asarray(L0_attn_x, dtype=np.float32))
    B, T, Dm = x.shape

    def fm(w):
        w = np.asarray(w, dtype=np.float32)
        return np.ascontiguousarray(w.reshape(NDC, P, -1).transpose(1, 0, 2))

    wt = fm(np.asarray(L0_attn_W, dtype=np.float32).T)
    bwt = fm(np.asarray(L0_ffn_base_w, dtype=np.float32).T)
    rt = fm(np.asarray(L0_ffn_router_w, dtype=np.float32).T)
    la = fm(np.asarray(L0_ffn_lora_A, dtype=np.float32))
    lb = np.ascontiguousarray(
        np.asarray(L0_ffn_lora_B, dtype=np.float32).reshape(64, NDC, P))
    cw = fm(np.asarray(L0_attn_conv_w, dtype=np.float32)[:, 0, :])

    def pv(v):
        return np.ascontiguousarray(np.asarray(v, dtype=np.float32).reshape(NDC, P).T)

    dec = pv(L0_attn_decay_logit)
    ba = pv(L0_attn_b)
    cb = pv(L0_attn_conv_b)
    bb = pv(L0_ffn_base_b)
    rb = np.asarray(L0_ffn_router_b, dtype=np.float32).reshape(8, 1)
    invd = np.full((P, P), 1.0 / D, dtype=np.float32)
    onesz = np.ones((8, P), dtype=np.float32)
    brep = np.zeros((8, 64), dtype=np.float32)
    rrep = np.zeros((8, 64), dtype=np.float32)
    for e in range(8):
        for r in range(8):
            brep[e, e * 8 + r] = 1.0
            rrep[r, e * 8 + r] = 1.0

    in_maps = []
    for c in range(8):
        b, q = divmod(c, 4)
        t0 = q * CH
        xin = np.zeros((ROWS, D), dtype=np.float32)
        lo = max(0, t0 - KAP)
        hi = min(T, t0 + CH + TT)
        xin[KAP - (t0 - lo): KAP + (hi - t0)] = x[b, lo:hi]
        in_maps.append({
            "x": np.ascontiguousarray(xin.T), "wt": wt, "bwt": bwt, "rt": rt,
            "la": la, "lb": lb, "cw": cw, "dec": dec, "ba": ba, "cb": cb,
            "bb": bb, "rb": rb, "invd": invd, "invdf": invd, "onesz": onesz,
            "brep": brep, "rrep": rrep,
        })

    import os as _os
    if "nc" not in _CACHED:
        _CACHED["nc"] = build_nc(nb=int(_os.environ.get("KNB", "16")), na=int(_os.environ.get("KNA", str(NT_A))), ksa=int(_os.environ.get("KSA", "9")), kpr=int(_os.environ.get("KPR", "0")), kdbg=_os.environ.get("KDBG", ""), kdbt=int(_os.environ.get("KDBT", "2")))
    res = run_bass_kernel_spmd(_CACHED["nc"], in_maps, core_ids=list(range(8)))

    out = np.empty((B, T, Dm), dtype=np.float32)
    for c in range(8):
        b, q = divmod(c, 4)
        out[b, q * CH:(q + 1) * CH] = np.asarray(res.results[c]["out"]).T
    return out


# version 20 (final)
# speedup vs baseline: 1.2559x; 1.0172x over previous
import numpy as np
from contextlib import ExitStack

import concourse.bass as bass
import concourse.mybir as mybir
from concourse.bass_utils import run_bass_kernel_spmd

F32 = mybir.dt.float32
F16 = mybir.dt.float16
F32R = mybir.dt.float32r
AF = mybir.ActivationFunctionType
ALU = mybir.AluOpType

P = 128
NDC = 8
D = 1024
TT = 256
KAP = 512
CH = 4096
NT_A = KAP // TT + CH // TT + 1
ROWS = KAP + CH + TT
EPS = 1e-5
LN2C = float(np.log(2.0))
C52 = float(2.0 ** 52)
C23 = float(2.0 ** 23)


def build_nc(nb=16, na=NT_A, ksa=9, kpr=0, kdbg="", kdbt=2):
    nc = bass.Bass("TRN2", target_bir_lowering=False, dynamic_dma_scratch_size=8192)

    dx = nc.dram_tensor("x", [D, ROWS], F32R, kind="ExternalInput").ap()
    dwt = nc.dram_tensor("wt", [P, NDC, D], F32, kind="ExternalInput").ap()
    dbwt = nc.dram_tensor("bwt", [P, NDC, D], F32, kind="ExternalInput").ap()
    drt = nc.dram_tensor("rt", [P, NDC, 8], F32, kind="ExternalInput").ap()
    dla = nc.dram_tensor("la", [P, NDC, 8], F32, kind="ExternalInput").ap()
    dlb = nc.dram_tensor("lb", [64, NDC, P], F32, kind="ExternalInput").ap()
    dcw = nc.dram_tensor("cw", [P, NDC, 5], F32, kind="ExternalInput").ap()
    ddec = nc.dram_tensor("dec", [P, NDC], F32, kind="ExternalInput").ap()
    dba = nc.dram_tensor("ba", [P, NDC], F32, kind="ExternalInput").ap()
    dcb = nc.dram_tensor("cb", [P, NDC], F32, kind="ExternalInput").ap()
    dbb = nc.dram_tensor("bb", [P, NDC], F32, kind="ExternalInput").ap()
    drb = nc.dram_tensor("rb", [8, 1], F32, kind="ExternalInput").ap()
    dinvd = nc.dram_tensor("invd", [P, P], F32R, kind="ExternalInput").ap()
    dinvdf = nc.dram_tensor("invdf", [P, P], F32, kind="ExternalInput").ap()
    donesz = nc.dram_tensor("onesz", [8, P], F32, kind="ExternalInput").ap()
    dbrep = nc.dram_tensor("brep", [8, 64], F32, kind="ExternalInput").ap()
    drrep = nc.dram_tensor("rrep", [8, 64], F32, kind="ExternalInput").ap()
    dout = nc.dram_tensor("out", [D, CH], F32, kind="ExternalOutput").ap()
    ddbg = nc.dram_tensor("dbg", [P, NDC, TT + 4], F32, kind="ExternalOutput").ap()

    es = ExitStack()

    def sb(name, shape, dt=F32):
        return es.enter_context(nc.sbuf_tensor(name, shape, dt))

    def psf(name):
        return es.enter_context(nc.psum_tensor(name, [P, 512], F32))

    with es:
        wh = sb("wh", [P, NDC, NDC, P], F16)
        wl = sb("wl", [P, NDC, NDC, P], F16)
        bh = sb("bh", [P, NDC, NDC, P], F16)
        bl = sb("bl", [P, NDC, NDC, P], F16)
        sph = sb("sph", [P, NDC, 40], F16)
        spl = sb("spl", [P, NDC, 40], F16)
        lbh = sb("lbh", [64, NDC, P], F16)
        lbl = sb("lbl", [64, NDC, P], F16)
        cwp = sb("cwp", [P, NDC, 5])
        s_a = sb("s_a", [P, NDC])
        s_oma = sb("s_oma", [P, NDC])
        s_ioma = sb("s_ioma", [P, NDC])
        biasmix = sb("biasmix", [P, NDC])
        s_cb = sb("s_cb", [P, NDC])
        s_bb = sb("s_bb", [P, NDC])
        s_rb = sb("s_rb", [8, 1])
        s_invd = sb("s_invd", [P, P], F32R)
        s_onesz = sb("s_onesz", [8, P])
        s_brep = sb("s_brep", [8, 64])
        s_rrep = sb("s_rrep", [8, 64])
        epsb = sb("epsb", [P, 1])
        b151 = sb("b151", [8, 1])
        bm135 = sb("bm135", [8, 1])
        bufx = [sb(f"bufx{i}", [P, NDC, TT], F32R) for i in range(3)]
        xpad = [sb(f"xpad{i}", [P, NDC, TT + 4]) for i in range(2)]
        lstash = sb("lstash", [P, NDC, 2])
        sS = sb("sS", [P, NDC, TT])
        carry = sb("carry", [P, NDC, 1])
        shi = sb("shi0", [P, NDC, TT], F16)
        slo = sb("slo0", [P, NDC, TT], F16)
        sqb = sb("sqb", [P, NDC, TT], F32R)
        statr = sb("statr", [P, NDC, TT], F32R)
        convb = sb("convb", [P, NDC, TT])
        convb2 = sb("convb2", [P, NDC, TT])
        x2b = [sb("x2b0", [P, NDC, TT]), sb("x2b1", [P, NDC, TT])]
        hfb = sb("hfb", [P, NDC, TT])
        hpb = sb("hpb", [P, NDC, TT])
        hh = sb("hh", [P, NDC, TT], F16)
        hl = sb("hl", [P, NDC, TT], F16)
        tmA = sb("tmA", [P, TT])
        tmB = sb("tmB", [P, TT])
        mcA = sb("mcA", [P, TT])
        mcB = sb("mcB", [P, TT])
        e8b = sb("e8b", [8, TT])
        lnzb = sb("lnzb", [8, TT])
        gb = [sb(f"gb{i}", [8, TT]) for i in range(3)]
        ind8 = sb("ind8", [8, 1])
        omi8 = sb("omi8", [8, 1])
        ln2t = sb("ln2t", [8, 1])
        nln2t = sb("nln2t", [8, 1])
        rln2t = sb("rln2t", [8, 1])
        outerb = sb("outerb", [64, TT])
        oh = sb("oh", [64, TT], F16)
        ol = sb("ol", [64, TT], F16)
        wsc = [convb[:, :, 0:P], convb[:, :, P:2 * P]]
        s_invdf = x2b[0][:, 0, 0:P].rearrange("p t -> p t")
        s_dec = x2b[0][:, 1, 0:NDC]
        s_cw = x2b[0][:, 2, 0:40].rearrange("p (dc k) -> p dc k", k=5)
        pstat = [psf("pstat0"), psf("pstat1")]
        pmm = [psf(f"pmm{i}") for i in range(4)]
        ppk = psf("ppk")
        pzb = psf("pzb")

        sems = {k: es.enter_context(nc.semaphore(f"sem_{k}"))
                for k in ["v", "s", "t", "g", "dc", "dwA", "dwB",
                          "dx0", "dx1", "dx2", "dst", "o0", "o1"]}
        block = es.enter_context(nc.Block())

        def bc8(ap2d):
            return ap2d.unsqueeze(1).broadcast_to([P, NDC, TT])

        class Env:
            def __init__(self, active, eng):
                self.active = active
                self.eng = eng
                self.cnt = {k: 0 for k in sems}
                self.cp = {}
                self.last = None

            def emit(self, engine, fn):
                if self.active == engine:
                    self.last = fn(self.eng)

            def chk(self, engine, label, dma=False, via=None):
                self.cnt[engine] += 16 if dma else 1
                assert label not in self.cp, label
                self.cp[label] = (engine, self.cnt[engine])
                owner = via if via is not None else engine
                if self.active == owner:
                    self.last.then_inc(sems[engine], 16 if dma else 1)

            def wait(self, label, who=None):
                if who is not None and self.active != who:
                    return
                src, val = self.cp[label]
                if src == self.active:
                    return
                self.eng.wait_ge(sems[src], val)

            def nop(self, engine):
                if self.active == engine:
                    self.eng.engine_nop()

            def selfsync(self, engine):
                if self.active == engine:
                    self.eng.drain()

        def prep(env):
            def dma(dst, src, label, sem="dc"):
                env.emit("d", lambda e: e.dma_start(out=dst, in_=src))
                env.chk(sem, label, dma=True, via="d")

            dma(s_invd[:], dinvd, "d_invd")
            dma(s_invdf, dinvdf, "d_invdf")
            dma(s_dec, ddec, "d_dec")
            dma(s_cw, dcw, "d_cw")
            dma(biasmix[:], dba, "d_ba")
            dma(s_cb[:], dcb, "d_cb")
            dma(s_bb[:], dbb, "d_bb")
            dma(s_rb[:], drb, "d_rb")
            dma(s_onesz[:], donesz, "d_onesz")
            dma(s_brep[:], dbrep, "d_brep")
            dma(s_rrep[:], drrep, "d_rrep")
            env.cp["d_consts"] = ("dc", env.cnt["dc"])

            env.wait("d_consts", who="s")
            env.emit("s", lambda e: e.activation(s_a[:], s_dec, AF.Sigmoid))
            env.chk("s", "a_done")

            env.emit("v", lambda e: e.memset(ind8[:], 1.0))
            env.emit("v", lambda e: e.drain())
            env.emit("v", lambda e: e.tensor_single_scalar(ind8[:], ind8[:], float(2.0 ** -40), ALU.add))
            env.emit("v", lambda e: e.drain())
            env.emit("v", lambda e: e.tensor_single_scalar(ind8[:], ind8[:], 1.0, ALU.subtract))
            env.emit("v", lambda e: e.drain())
            env.emit("v", lambda e: e.tensor_single_scalar(ind8[:], ind8[:], float(2.0 ** 40), ALU.mult))
            env.emit("v", lambda e: e.drain())
            env.emit("v", lambda e: e.tensor_scalar(omi8[:], ind8[:], -1.0, 1.0, ALU.mult, ALU.add))
            env.emit("v", lambda e: e.memset(epsb[:], EPS))
            env.emit("v", lambda e: e.memset(carry[:], 0.0))
            env.emit("v", lambda e: e.memset(ln2t[:], 2.0))
            env.emit("v", lambda e: e.drain())
            env.wait("a_done", who="v")
            env.emit("v", lambda e: e.tensor_scalar(s_oma[:], s_a[:], -1.0, 1.0, ALU.mult, ALU.add))
            env.selfsync("v")
            env.emit("v", lambda e: e.reciprocal(s_ioma[:], s_oma[:]))
            env.selfsync("v")
            env.wait("d_consts", who="v")
            for k in range(5):
                env.emit("v", lambda e, k=k: e.tensor_tensor(cwp[:, :, k], s_cw[:, :, k], s_ioma[:], ALU.mult))
            env.wait("d_consts", who="v")
            env.wait("d_consts", who="v")
            env.emit("v", lambda e: e.tensor_tensor(biasmix[:], biasmix[:], s_cb[:], ALU.add))
            env.chk("v", "vconsts")
            env.wait("vconsts", who="s")
            env.emit("s", lambda e: e.activation(ln2t[:], ln2t[:], AF.Ln))
            env.chk("s", "ln2done")
            env.wait("ln2done", who="v")
            env.emit("v", lambda e: e.tensor_single_scalar(b151[:], ln2t[:], 151.0, ALU.mult))
            env.emit("v", lambda e: e.tensor_single_scalar(bm135[:], ln2t[:], -135.0, ALU.mult))
            env.emit("v", lambda e: e.tensor_single_scalar(nln2t[:], ln2t[:], -1.0, ALU.mult))
            env.emit("v", lambda e: e.drain())
            env.emit("v", lambda e: e.reciprocal(rln2t[:], ln2t[:]))
            env.chk("v", "ln2fam")

            for wi, (src, dsth, dstl, center) in enumerate(
                    [] if kpr else [(dwt, wh, wl, False), (dbwt, bh, bl, True)]):
                for oc in range(NDC):
                    scr = wsc[oc % 2]
                    pwi, poc = (wi, oc - 2) if oc >= 2 else (wi - 1, NDC - 2 + oc)
                    if pwi >= 0:
                        env.wait(f"w{pwi}l_{poc}", who="d")
                    dma(scr, src[:, :, oc * P:(oc + 1) * P], f"d_w{wi}_{oc}", sem=('dwA', 'dwB')[oc % 2])
                    if center:
                        env.wait(f"d_w{wi}_{oc}", who="t")
                        if oc >= 1:
                            env.wait(f"w{wi}c_{oc - 1}", who="t")
                        env.wait("d_consts", who="t")
                        for kc in range(NDC):
                            env.emit("t", lambda e, kc=kc, scr=scr: e.matmul(
                                pstat[0][:, 0:P], s_invdf,
                                scr[:, kc, :],
                                start=(kc == 0), stop=(kc == NDC - 1)))
                        env.chk("t", f"w{wi}m_{oc}")
                        env.wait(f"w{wi}m_{oc}", who="v")
                        for kc in range(NDC):
                            env.emit("v", lambda e, kc=kc, scr=scr: e.tensor_tensor(
                                scr[:, kc, :], scr[:, kc, :],
                                pstat[0][:, 0:P], ALU.subtract))
                        env.selfsync("v")
                        env.chk("v", f"w{wi}c_{oc}")
                        env.wait(f"w{wi}c_{oc}", who="s")
                    else:
                        env.wait(f"d_w{wi}_{oc}", who="s")
                    env.emit("s", lambda e, oc=oc, scr=scr, dsth=dsth: e.copy(dsth[:, :, oc, :], scr))
                    env.chk("s", f"w{wi}h_{oc}")
                    env.wait(f"w{wi}h_{oc}", who="v")
                    env.emit("v", lambda e, oc=oc, scr=scr, dsth=dsth, dstl=dstl: e.tensor_tensor(
                        dstl[:, :, oc, :], scr, dsth[:, :, oc, :], ALU.subtract))
                    env.chk("v", f"w{wi}l_{oc}")

            if kpr:
                env.cp["prep_s"] = env.cp["a_done"]
                env.cp["prep_t"] = ("t", 0)
                env.emit("v", lambda e: e.engine_nop()); env.chk("v", "prep_v")
                env.emit("g", lambda e: e.engine_nop()); env.chk("g", "prep_g")
                for k in ["v", "s", "t", "g"]:
                    env.wait(f"prep_{k}")
                dma(bufx[0][:], dx[:, 0:TT].rearrange("(dc p) t -> p dc t", p=P), "d_x0", sem="dx0")
                return
            env.wait("w1l_6", who="d")
            dma(wsc[0][:, :, 0:8], drt, "d_rt", sem="dwA")
            env.wait("w1l_7", who="d")
            dma(wsc[0][:, :, 8:16], dla, "d_la", sem="dwB")
            env.wait("d_rt", who="t")
            env.wait("d_la", who="t")
            env.wait("w1c_7", who="t")
            for kc in range(NDC):
                env.emit("t", lambda e, kc=kc: e.matmul(
                    pstat[0][:, 0:16], s_invdf, wsc[0][:, kc, 0:16],
                    start=(kc == 0), stop=(kc == NDC - 1)))
            env.chk("t", "rtm")
            env.wait("rtm", who="v")
            for kc in range(NDC):
                env.emit("v", lambda e, kc=kc: e.tensor_tensor(
                    wsc[0][:, kc, 0:16], wsc[0][:, kc, 0:16], pstat[0][:, 0:16], ALU.subtract))
            env.chk("v", "rtc")
            env.emit("v", lambda e: e.memset(sph[:], 0.0))
            env.emit("v", lambda e: e.memset(spl[:], 0.0))
            env.chk("v", "spz")
            env.wait("rtc", who="s")
            env.wait("spz", who="s")
            env.emit("s", lambda e: e.copy(sph[:, :, 0:8], wsc[0][:, :, 0:8]))
            env.emit("s", lambda e: e.copy(sph[:, :, 32:40], wsc[0][:, :, 8:16]))
            env.chk("s", "rth")
            env.wait("rth", who="v")
            env.emit("v", lambda e: e.tensor_tensor(spl[:, :, 0:8], wsc[0][:, :, 0:8], sph[:, :, 0:8], ALU.subtract))
            env.emit("v", lambda e: e.tensor_tensor(spl[:, :, 32:40], wsc[0][:, :, 8:16], sph[:, :, 32:40], ALU.subtract))
            env.chk("v", "rtl")

            env.wait("rtl", who="d")
            lsc = convb[0:64, :, P:2 * P]
            dma(lsc, dlb, "d_lb", sem="dwB")
            env.wait("d_lb", who="v")
            env.emit("v", lambda e: e.tensor_single_scalar(lsc, lsc, 2.0, ALU.mult))
            env.selfsync("v")
            env.chk("v", "lb2")
            env.wait("lb2", who="s")
            env.emit("s", lambda e: e.copy(lbh[:], lsc))
            env.chk("s", "lbh")
            env.wait("lbh", who="v")
            env.emit("v", lambda e: e.tensor_tensor(lbl[:], lsc, lbh[:], ALU.subtract))
            env.chk("v", "lbl")

            dma(bufx[0][:], dx[:, 0:TT].rearrange("(dc p) t -> p dc t", p=P), "d_x0", sem="dx0")

            env.emit("v", lambda e: e.engine_nop()); env.chk("v", "prep_v")
            env.cp["prep_s"] = env.cp["lbh"]
            env.cp["prep_t"] = env.cp["rtm"]
            env.emit("g", lambda e: e.engine_nop()); env.chk("g", "prep_g")
            for k in ["v", "s", "t", "g"]:
                env.wait(f"prep_{k}")

        def stageA(env, j):
            halo = (j == NT_A - 1)
            xb = bufx[j % 3]
            xp = xpad[j % 2]
            pst = pstat[0]
            if j + 1 < NT_A:
                if j >= 2:
                    env.wait(f"stat1_{j - 2}", who="d")
                    env.wait(f"xp_{j - 2}", who="d")
                    if 2 <= j - 2 <= NT_A - 2 and (j - 2) - 2 < nb:
                        env.wait(f"x2_{j - 2}", who="d")
                env.emit("d", lambda e, j=j: e.dma_start(
                    out=bufx[(j + 1) % 3][:],
                    in_=dx[:, (j + 1) * TT:(j + 2) * TT].rearrange("(dc p) t -> p dc t", p=P)))
                env.chk(f"dx{(j + 1) % 3}", f"d_x{j + 1}", dma=True, via="d")
            env.wait(f"d_x{j}", who="s")
            if j >= 1:
                env.wait(f"xp_{j - 1}", who="s")
            if j >= 4 and has_b(j - 2):
                env.wait(f"stat2_{j - 2}", who="s")
            env.emit("s", lambda e: e.activation(sqb[:], xb[:].bitcast(F32), AF.Square))
            env.chk("s", f"sqx_{j}")
            if ksa < 2:
                return
            env.wait(f"d_x{j}", who="t")
            env.wait(f"sqx_{j}", who="t")
            if j >= 4 and (j - 2) - 2 < nb:
                env.wait(f"tm2_{j - 2}", who="t")
                env.wait(f"t2_{j - 2}", who="t")
            elif j >= 2:
                env.wait(f"xp_{j - 2}", who="t")
            for kc in range(NDC):
                env.emit("t", lambda e, kc=kc: e.matmul(
                    pst[:, 0:TT], s_invd[:], xb[:, kc, :],
                    start=(kc == 0), stop=(kc == NDC - 1)))
            for kc in range(NDC):
                env.emit("t", lambda e, kc=kc: e.matmul(
                    pst[:, TT:2 * TT], s_invd[:], sqb[:, kc, :],
                    start=(kc == 0), stop=(kc == NDC - 1)))
            env.chk("t", f"stat1_{j}")
            if ksa < 3:
                return
            env.wait(f"stat1_{j}", who="s")
            env.emit("s", lambda e: e.activation(tmA[:], pst[:, 0:TT], AF.Square))
            env.chk("s", f"tm1_{j}")
            env.wait(f"tm1_{j}", who="v")
            env.emit("v", lambda e: e.tensor_tensor(tmA[:], pst[:, TT:2 * TT], tmA[:], ALU.subtract))
            env.chk("v", f"wc1_{j}")
            env.wait(f"wc1_{j}", who="s")
            env.emit("s", lambda e: e.activation(tmA[:], tmA[:], AF.Ln, bias=epsb[:]))
            env.selfsync("s")
            env.emit("s", lambda e: e.activation(tmA[:], tmA[:], AF.Exp, scale=-0.5))
            env.chk("s", f"r1_{j}")
            if ksa < 4:
                return
            env.emit("v", lambda e: e.tensor_copy(mcA[:], pst[:, 0:TT]))
            env.selfsync("v")
            env.emit("v", lambda e: e.tensor_tensor(xp[:, :, 2:2 + TT], xb[:].bitcast(F32), bc8(mcA[:]), ALU.subtract))
            env.chk("v", f"u1_{j}")
            env.wait(f"r1_{j}", who="v")
            env.selfsync("v")
            for dc in range(NDC):
                env.emit("v", lambda e, dc=dc: e.scalar_tensor_tensor(
                    xp[:, dc, 2:2 + TT], xp[:, dc, 2:2 + TT], s_oma[:, dc:dc + 1], tmA[:],
                    ALU.mult, ALU.mult))
            env.selfsync("v")
            env.chk("v", f"xp_{j}")
            if j == kdbt:
                dbgdump(env, "xp_2", xp[:], TT + 4)
                dbgdump(env, "m_2", mcA[:], -1, wlabel=f"u1_{j}")
                dbgdump(env, "r_2", tmA[:], -1, wlabel=f"r1_{j}")
                dbgdump(env, "sq_2", sqb[:].bitcast(F32), TT, wlabel=f"sqx_{j}")
                dbgdump(env, "xb_2", xb[:].bitcast(F32), TT, wlabel=f"d_x{j}")
            if j == 1:
                env.emit("v", lambda e: e.tensor_copy(lstash[:], xp[:, :, TT:TT + 2]))
            if halo or (ksa < 6 and ksa != 45):
                return
            if j - 1 >= 2:
                env.wait(f"shi_{j - 1}", who="v")
                env.wait(f"slo_{j - 1}", who="v")
            for dc in range(NDC):
                init = 0.0 if j == 0 else carry[:, dc, 0:1]
                env.emit("v", lambda e, dc=dc, init=init: e.tensor_tensor_scan(
                    sS[:, dc, :], s_a[:, dc:dc + 1].broadcast_to([P, TT]),
                    xp[:, dc, 2:2 + TT], init, ALU.mult, ALU.add))
            env.selfsync("v")
            env.emit("v", lambda e: e.tensor_copy(carry[:], sS[:, :, TT - 1:TT]))
            env.chk("v", f"scan_{j}")
            if j == kdbt:
                dbgdump(env, "scan_2", sS[:], TT)

        def stageA2(env, j):
            if j < 2 or j == NT_A - 1 or ksa < 6:
                return
            env.wait(f"scan_{j}", who="s")
            if has_mix(j - 1):
                env.wait(f"wmm_{j - 1}_7", who="s")
            env.emit("s", lambda e: e.copy(shi[:], sS[:]))
            env.chk("s", f"shi_{j}")
        def stageA2b(env, j):
            if j < 2 or j == NT_A - 1 or ksa < 6:
                return
            env.wait(f"shi_{j}", who="v")
            if has_mix(j - 1):
                env.wait(f"wmm_{j - 1}_7", who="v")
            env.emit("v", lambda e: e.tensor_tensor(slo[:], sS[:], shi[:], ALU.subtract))
            env.chk("v", f"slo_{j}")

        def has_mix(m):
            return m == 2 or (3 <= m <= NT_A - 2 and (m - 3) < nb)

        def has_b(q):
            return q >= 2 and q - 2 < nb

        def cvb(m):
            return convb2 if m % 2 == 0 else convb

        def mix_chunk(env, m, oc):
            bank = pmm[oc % 2]
            if oc == 0:
                env.wait(f"shi_{m}", who="t")
                env.wait(f"slo_{m}", who="t")
            if oc >= 2:
                env.wait(f"mevac_{m}_{oc - 2}", who="t")
            elif has_mix(m - 1):
                env.wait(f"mevac_{m - 1}_{6 + oc}", who="t")
            for kc in range(NDC):
                env.emit("t", lambda e, kc=kc, oc=oc, bank=bank: e.matmul(
                    bank[:, 0:TT], wh[:, kc, oc, :], shi[:, kc, :], start=(kc == 0), stop=False))
                env.emit("t", lambda e, kc=kc, oc=oc, bank=bank: e.matmul(
                    bank[:, 0:TT], wh[:, kc, oc, :], slo[:, kc, :], start=False, stop=False))
            for kc in range(NDC):
                env.emit("t", lambda e, kc=kc, oc=oc, bank=bank: e.matmul(
                    bank[:, 0:TT], wl[:, kc, oc, :], shi[:, kc, :], start=False,
                    stop=(kc == NDC - 1)))
            env.chk("t", f"wmm_{m}_{oc}")
            if m == kdbt and oc == 2:
                dbgdump(env, "shi_2", shi[:].bitcast(F32), 128, wlabel=f"wmm_{m}_{oc}")
                dbgdump(env, "slo_2", slo[:].bitcast(F32), 128, wlabel=f"wmm_{m}_{oc}")
            env.wait(f"wmm_{m}_{oc}", who="s")
            if oc == 0 and m >= 4 and (m - 2) - 2 < nb:
                env.wait(f"x2_{m - 2}", who="s")
            env.emit("s", lambda e, oc=oc, bank=bank, m=m: e.activation(
                cvb(m)[:, oc, :], bank[:, 0:TT], AF.Identity, bias=biasmix[:, oc:oc + 1]))
            env.chk("s", f"mevac_{m}_{oc}")
            if m == kdbt and oc == 7:
                dbgdump(env, "ymm_2", cvb(m)[:], TT, wlabel=f"mevac_{m}_{oc}")

        def stageB_conv(env, j):
            xp = xpad[j % 2]
            xpn = xpad[(j + 1) % 2]
            cv = cvb(j)
            env.emit("v", lambda e: e.tensor_copy(xp[:, :, 0:2], lstash[:]))
            env.emit("v", lambda e: e.tensor_copy(xp[:, :, 2 + TT:4 + TT], xpn[:, :, 2:4]))
            env.selfsync("v")
            env.emit("v", lambda e: e.tensor_copy(lstash[:], xp[:, :, TT:TT + 2]))
            env.selfsync("v")
            env.wait(f"mevac_{j}_7", who="v")
            for k in range(5):
                if k:
                    env.selfsync("v")
                for dc in range(NDC):
                    env.emit("v", lambda e, dc=dc, k=k: e.scalar_tensor_tensor(
                        cv[:, dc, :], xp[:, dc, k:k + TT], cwp[:, dc, k:k + 1],
                        cv[:, dc, :], ALU.mult, ALU.add))
            env.selfsync("v")
            env.chk("v", f"conv_{j}")

        def stageB_head(env, j):
            f = j - 2
            m = j + 1
            emb = has_mix(m)
            xb = bufx[j % 3]
            cv = cvb(j)
            xv = x2b[j % 2]

            if j == kdbt:
                dbgdump(env, "y_2", cv[:], TT, wlabel=f"conv_{j}")

            env.wait(f"conv_{j}", who="s")
            if has_b(j - 1):
                env.wait(f"stat3_{j - 1}", who="s")
            else:
                env.wait(f"stat1_{j + 1}", who="s")
            env.emit("s", lambda e: e.activation(sqb[:], cv[:], AF.Square))
            env.chk("s", f"sqy_{j}")
            env.wait(f"conv_{j}", who="d")
            if has_b(j - 1):
                env.wait(f"stat3_{j - 1}", who="d")
            env.emit("d", lambda e: e.dma_start(out=statr[:], in_=cv[:].bitcast(F32R)))
            env.chk("dst", f"dcy_{j}", dma=True, via="d")
            pst = pstat[0]
            env.wait(f"sqy_{j}", who="t")
            env.wait(f"dcy_{j}", who="t")
            env.wait(f"xp_{j + 1}", who="t")
            env.wait(f"tm1_{j + 1}", who="t")
            for kc in range(NDC):
                env.emit("t", lambda e, kc=kc: e.matmul(
                    pst[:, 0:TT], s_invd[:], statr[:, kc, :],
                    start=(kc == 0), stop=(kc == NDC - 1)))
            for kc in range(NDC):
                env.emit("t", lambda e, kc=kc: e.matmul(
                    pst[:, TT:2 * TT], s_invd[:], sqb[:, kc, :],
                    start=(kc == 0), stop=(kc == NDC - 1)))
            env.chk("t", f"stat2_{j}")
            env.wait(f"stat2_{j}", who="s")
            if has_b(j - 1):
                env.wait(f"hp_{j - 1}", who="s")
            env.emit("s", lambda e: e.activation(tmB[:], pst[:, 0:TT], AF.Square))
            env.chk("s", f"tm2_{j}")
            env.wait(f"tm2_{j}", who="v")
            env.emit("v", lambda e: e.tensor_tensor(tmB[:], pst[:, TT:2 * TT], tmB[:], ALU.subtract))
            env.chk("v", f"wc2_{j}")
            env.wait(f"wc2_{j}", who="s")
            env.emit("s", lambda e: e.activation(tmB[:], tmB[:], AF.Ln, bias=epsb[:]))
            env.selfsync("s")
            env.emit("s", lambda e: e.activation(tmB[:], tmB[:], AF.Exp, scale=-0.5))
            env.chk("s", f"r2_{j}")
            env.emit("v", lambda e: e.tensor_copy(mcB[:], pst[:, 0:TT]))
            env.selfsync("v")
            env.emit("v", lambda e: e.tensor_tensor(cv[:], cv[:], bc8(mcB[:]), ALU.subtract))
            env.wait(f"r2_{j}", who="v")
            env.selfsync("v")
            env.emit("v", lambda e: e.tensor_tensor(cv[:], cv[:], bc8(tmB[:]), ALU.mult))
            env.chk("v", f"t2_{j}")
            if emb:
                mix_chunk(env, m, 0)
                mix_chunk(env, m, 1)
            env.wait(f"t2_{j}", who="g")
            if has_b(j - 2):
                env.wait(f"hp_{j - 2}", who="g")
                env.wait(f"dcx2_{j - 2}", who="g")
            env.selfsync("g")
            env.emit("g", lambda e: e.tensor_tensor(xv[:], cv[:], xb[:].bitcast(F32), ALU.add))
            env.chk("g", f"x2_{j}")

            if j == kdbt:
                dbgdump(env, "x2_2", xv[:], TT)

            if emb:
                mix_chunk(env, m, 2)

        def stageB_head2a(env, j):
            f = j - 2
            xv = x2b[j % 2]

            env.wait(f"x2_{j}", who="s")
            if has_b(j - 1):
                env.wait(f"stat4_{j - 1}", who="s")
            else:
                env.wait(f"stat1_{j + 2}", who="s")
            env.emit("s", lambda e: e.activation(sqb[:], xv[:], AF.Square))
            env.chk("s", f"sqx2_{j}")
            env.wait(f"x2_{j}", who="d")
            if has_b(j - 1):
                env.wait(f"stat4_{j - 1}", who="d")
            env.emit("d", lambda e: e.dma_start(out=statr[:], in_=xv[:].bitcast(F32R)))
            env.chk("dst", f"dcx2_{j}", dma=True, via="d")
            pst3 = pstat[1]
            env.wait(f"sqx2_{j}", who="t")
            env.wait(f"dcx2_{j}", who="t")
            if f > 0:
                env.wait(f"tm4_{j - 1}", who="t")
                env.wait(f"t4_{j - 1}", who="t")
            else:
                env.wait(f"xp_{j}", who="t")
                env.wait(f"tm1_{j}", who="t")
            for kc in range(NDC):
                env.emit("t", lambda e, kc=kc: e.matmul(
                    pst3[:, 0:TT], s_invd[:], statr[:, kc, :],
                    start=(kc == 0), stop=(kc == NDC - 1)))
            for kc in range(NDC):
                env.emit("t", lambda e, kc=kc: e.matmul(
                    pst3[:, TT:2 * TT], s_invd[:], sqb[:, kc, :],
                    start=(kc == 0), stop=(kc == NDC - 1)))
            env.chk("t", f"stat3_{j}")
            env.wait(f"stat3_{j}", who="s")
            env.emit("s", lambda e: e.activation(tmB[:], pst3[:, 0:TT], AF.Square))
            env.chk("s", f"tm3_{j}")
            env.wait(f"tm3_{j}", who="v")
            env.emit("v", lambda e: e.tensor_tensor(tmB[:], pst3[:, TT:2 * TT], tmB[:], ALU.subtract))
            env.chk("v", f"wc3_{j}")
            env.wait(f"wc3_{j}", who="s")
            env.emit("s", lambda e: e.activation(tmB[:], tmB[:], AF.Ln, bias=epsb[:]))
            env.selfsync("s")
            env.emit("s", lambda e: e.activation(tmB[:], tmB[:], AF.Exp, scale=-0.5))
            env.chk("s", f"r3_{j}")
            env.wait(f"r3_{j}", who="v")
            if has_b(j - 1):
                env.wait(f"hl_{j - 1}", who="v")
            env.emit("v", lambda e: e.tensor_tensor(hpb[:], xv[:], bc8(tmB[:]), ALU.mult))
            env.chk("v", f"hp_{j}")
            env.wait(f"hp_{j}", who="s")
            if f > 0:
                env.wait(f"ffn_{j - 1}_7", who="s")
            env.emit("s", lambda e: e.copy(hh[:], hpb[:]))
            env.chk("s", f"hh_{j}")
            env.wait(f"hh_{j}", who="g")
            if f > 0:
                env.wait(f"ffn_{j - 1}_7", who="g")
            env.emit("g", lambda e: e.tensor_tensor(hl[:], hpb[:], hh[:], ALU.subtract))
            env.chk("g", f"hl_{j}")

        def stageB_head2b(env, j):
            f = j - 2
            m = j + 2
            emb = has_mix(m)

            if emb:
                mix_chunk(env, m, 3)
                mix_chunk(env, m, 4)

            env.wait(f"hh_{j}", who="t")
            env.wait(f"hl_{j}", who="t")
            if f > 0:
                env.wait(f"outer_{j - 1}", who="t")
            for kc in range(NDC):
                env.emit("t", lambda e, kc=kc: e.matmul(
                    ppk[0:40, 0:TT], sph[:, kc, :], hh[:, kc, :], start=(kc == 0), stop=False))
                env.emit("t", lambda e, kc=kc: e.matmul(
                    ppk[0:40, 0:TT], sph[:, kc, :], hl[:, kc, :], start=False, stop=False))
            for kc in range(NDC):
                env.emit("t", lambda e, kc=kc: e.matmul(
                    ppk[0:40, 0:TT], spl[:, kc, :], hh[:, kc, :], start=False,
                    stop=(kc == NDC - 1)))
            env.chk("t", f"pack_{j}")

            env.wait(f"pack_{j}", who="s")
            if f > 0:
                env.wait(f"gq8_{j - 1}", who="s")
            env.emit("s", lambda e: e.activation(e8b[:], ppk[0:8, 0:TT], AF.Exp, bias=s_rb[:]))
            env.chk("s", f"e8_{j}")
            if emb:
                mix_chunk(env, m, 5)
            env.wait(f"e8_{j}", who="t")
            env.emit("t", lambda e: e.matmul(pzb[:, 0:TT], s_onesz[:], e8b[:], start=True, stop=True))
            env.chk("t", f"zbc_{j}")
            env.wait(f"zbc_{j}", who="s")
            if f > 0:
                env.wait(f"gr_{j - 1}", who="s")
            env.emit("s", lambda e: e.activation(lnzb[:], pzb[0:8, 0:TT], AF.Ln))
            env.chk("s", f"lnz_{j}")
            env.wait(f"lnz_{j}", who="v")
            if f > 0:
                env.wait(f"gq8_{j - 1}", who="v")
                env.wait(f"reps_{j - 1}", who="v")
            env.emit("v", lambda e: e.scalar_tensor_tensor(
                gb[0][:], ppk[0:8, 0:TT], s_rb[:], lnzb[:], ALU.add, ALU.subtract))
            env.selfsync("v")
            env.emit("v", lambda e: e.tensor_scalar(
                gb[1][:], gb[0][:], rln2t[:], 127.5, ALU.mult, ALU.add))
            env.chk("v", f"ge2p_{j}")

            def rint_blend(src_t, dst):
                env.selfsync("v")
                env.emit("v", lambda e: e.tensor_single_scalar(e8b[:], src_t[:], C52, ALU.add))
                env.emit("v", lambda e: e.tensor_single_scalar(lnzb[:], src_t[:], C23, ALU.add))
                env.selfsync("v")
                env.emit("v", lambda e: e.tensor_single_scalar(e8b[:], e8b[:], C52, ALU.subtract))
                env.emit("v", lambda e: e.tensor_single_scalar(lnzb[:], lnzb[:], C23, ALU.subtract))
                env.selfsync("v")
                env.emit("v", lambda e: e.tensor_scalar(e8b[:], e8b[:], ind8[:], None, ALU.mult))
                env.emit("v", lambda e: e.tensor_scalar(lnzb[:], lnzb[:], omi8[:], None, ALU.mult))
                env.selfsync("v")
                env.emit("v", lambda e: e.tensor_tensor(dst[:], e8b[:], lnzb[:], ALU.add))

            env.wait(f"zbc_{j}", who="v")
            rint_blend(gb[1], gb[2])
            env.selfsync("v")
            env.emit("v", lambda e: e.scalar_tensor_tensor(
                gb[1][:], gb[2][:], nln2t[:], gb[0][:], ALU.mult, ALU.add))
            env.chk("v", f"gs_{j}")
            env.wait(f"gs_{j}", who="s")
            env.emit("s", lambda e: e.activation(gb[0][:], gb[1][:], AF.Exp, bias=b151[:]))
            env.chk("s", f"gv32_{j}")
            if emb:
                mix_chunk(env, m, 6)
            env.wait(f"gv32_{j}", who="v")
            rint_blend(gb[0], gb[1])
            env.selfsync("v")
            env.emit("v", lambda e: e.tensor_single_scalar(gb[1][:], gb[1][:], float(2.0 ** -16), ALU.mult))
            env.selfsync("v")
            rint_blend(gb[1], gb[0])
            env.chk("v", f"gq8_{j}")
            env.wait(f"gq8_{j}", who="s")
            env.emit("s", lambda e: e.activation(lnzb[:], gb[2][:], AF.Exp, scale=ln2t[:], bias=bm135[:]))
            env.chk("s", f"gu_{j}")
            if emb:
                mix_chunk(env, m, 7)
            env.wait(f"gu_{j}", who="v")
            env.selfsync("v")
            env.emit("v", lambda e: e.tensor_tensor(gb[1][:], gb[0][:], lnzb[:], ALU.mult))
            env.chk("v", f"gr_{j}")
            env.emit("v", lambda e: e.tensor_copy(gb[2][:], ppk[32:40, 0:TT]))
            env.chk("v", f"mide_{j}")

            env.wait(f"gr_{j}", who="t")
            env.wait(f"mide_{j}", who="t")
            env.emit("t", lambda e: e.matmul(ppk[0:64, TT:2 * TT], s_brep[:], gb[1][:], start=True, stop=True))
            env.emit("t", lambda e: e.matmul(pzb[0:64, TT:2 * TT], s_rrep[:], gb[2][:], start=True, stop=True))
            env.chk("t", f"reps_{j}")
            env.wait(f"reps_{j}", who="v")
            if f > 0:
                env.wait(f"ol_{j - 1}", who="v")
            env.emit("v", lambda e: e.tensor_copy(outerb[:], ppk[0:64, TT:2 * TT]))
            env.selfsync("v")
            env.emit("v", lambda e: e.tensor_tensor(outerb[:], outerb[:], pzb[0:64, TT:2 * TT], ALU.mult))
            env.chk("v", f"outer_{j}")
            env.wait(f"outer_{j}", who="s")
            if f > 0:
                env.wait(f"ffn_{j - 1}_7", who="s")
            env.emit("s", lambda e: e.copy(oh[:], outerb[:]))
            env.chk("s", f"oh_{j}")
            env.wait(f"oh_{j}", who="g")
            env.emit("g", lambda e: e.tensor_tensor(ol[:], outerb[:], oh[:], ALU.subtract))
            env.chk("g", f"ol_{j}")


        def stageB_tail(env, j):
            f = j - 2
            for oc in range(NDC):
                bank = pmm[2 + oc % 2]
                if oc == 0 and has_b(j - 1):
                    env.wait(f"dout_{j - 1}", who="s")
                if oc >= 2:
                    env.wait(f"ffnevac_{j}_{oc - 2}", who="t")
                elif f > 0:
                    env.wait(f"ffnevac_{j - 1}_{6 + oc}", who="t")
                for kc in range(NDC):
                    env.emit("t", lambda e, kc=kc, oc=oc, bank=bank: e.matmul(
                        bank[:, 0:TT], bh[:, kc, oc, :], hh[:, kc, :], start=(kc == 0), stop=False))
                    env.emit("t", lambda e, kc=kc, oc=oc, bank=bank: e.matmul(
                        bank[:, 0:TT], bh[:, kc, oc, :], hl[:, kc, :], start=False, stop=False))
                for kc in range(NDC):
                    env.emit("t", lambda e, kc=kc, oc=oc, bank=bank: e.matmul(
                        bank[:, 0:TT], bl[:, kc, oc, :], hh[:, kc, :], start=False, stop=False))
                if oc == 0:
                    env.wait(f"oh_{j}", who="t")
                    env.wait(f"ol_{j}", who="t")
                env.emit("t", lambda e, oc=oc, bank=bank: e.matmul(
                    bank[:, 0:TT], lbh[:, oc, :], oh[:], start=False, stop=False))
                env.emit("t", lambda e, oc=oc, bank=bank: e.matmul(
                    bank[:, 0:TT], lbh[:, oc, :], ol[:], start=False, stop=False))
                env.emit("t", lambda e, oc=oc, bank=bank: e.matmul(
                    bank[:, 0:TT], lbl[:, oc, :], oh[:], start=False, stop=True))
                env.chk("t", f"ffn_{j}_{oc}")
                env.wait(f"ffn_{j}_{oc}", who="s")
                env.emit("s", lambda e, oc=oc, bank=bank: e.activation(
                    hfb[:, oc, :], bank[:, 0:TT], AF.Identity, bias=s_bb[:, oc:oc + 1]))
                env.chk("s", f"ffnevac_{j}_{oc}")
                if oc == 0:
                    env.wait(f"stat3_{j}", who="s")
                    if j + 3 < min(na, NT_A):
                        env.wait(f"stat1_{j + 3}", who="s")
                env.selfsync("s")
                env.emit("s", lambda e, oc=oc: e.activation(
                    sqb[:, oc, :], hfb[:, oc, :], AF.Square))
                env.chk("s", f"sqf_{j}_{oc}")
                if oc == 0:
                    if has_b(j + 1):
                        env.wait(f"stat2_{j + 1}", who="d")
                    else:
                        env.wait(f"stat3_{j}", who="d")
                env.wait(f"ffnevac_{j}_{oc}", who="d")
                env.emit("d", lambda e, oc=oc: e.dma_start(
                    out=statr[:, oc, :], in_=hfb[:, oc, :].bitcast(F32R)))
                env.chk("dst", f"dcf_{j}_{oc}", dma=True, via="d")

            if j == kdbt:
                dbgdump(env, "f_2", hfb[:], TT, wlabel=f"ffnevac_{j}_{NDC-1}")
            pst4 = pstat[1]
            env.wait(f"sqf_{j}_{NDC - 1}", who="t")
            env.wait(f"dcf_{j}_{NDC - 1}", who="t")
            env.wait(f"wc3_{j}", who="t")
            env.wait(f"tm3_{j}", who="t")
            for kc in range(NDC):
                env.emit("t", lambda e, kc=kc: e.matmul(
                    pst4[:, 0:TT], s_invd[:], statr[:, kc, :],
                    start=(kc == 0), stop=(kc == NDC - 1)))
            for kc in range(NDC):
                env.emit("t", lambda e, kc=kc: e.matmul(
                    pst4[:, TT:2 * TT], s_invd[:], sqb[:, kc, :],
                    start=(kc == 0), stop=(kc == NDC - 1)))
            env.chk("t", f"stat4_{j}")
            env.wait(f"stat4_{j}", who="s")
            env.emit("s", lambda e: e.activation(tmB[:], pst4[:, 0:TT], AF.Square))
            env.chk("s", f"tm4_{j}")

        def stageB_ln4(env, j):
            f = j - 2
            pst4 = pstat[1]
            env.wait(f"tm4_{j}", who="v")
            env.emit("v", lambda e: e.tensor_tensor(tmB[:], pst4[:, TT:2 * TT], tmB[:], ALU.subtract))
            env.chk("v", f"wc4_{j}")
            env.wait(f"wc4_{j}", who="s")
            env.emit("s", lambda e: e.activation(tmB[:], tmB[:], AF.Ln, bias=epsb[:]))
            env.selfsync("s")
            env.emit("s", lambda e: e.activation(tmB[:], tmB[:], AF.Exp, scale=-0.5))
            env.chk("s", f"r4_{j}")
            env.emit("v", lambda e: e.tensor_copy(mcB[:], pst4[:, 0:TT]))
            env.selfsync("v")
            env.emit("v", lambda e: e.tensor_tensor(hfb[:], hfb[:], bc8(mcB[:]), ALU.subtract))
            env.wait(f"r4_{j}", who="v")
            env.selfsync("v")
            env.emit("v", lambda e: e.tensor_tensor(hfb[:], hfb[:], bc8(tmB[:]), ALU.mult))
            env.chk("v", f"t4_{j}")
            env.wait(f"t4_{j}", who="g")
            env.emit("g", lambda e, j=j: e.tensor_tensor(hfb[:], hfb[:], x2b[j % 2][:], ALU.add))
            env.chk("g", f"outres_{j}")
        def stageB_dout(env, j):
            f = j - 2
            if j == 15:
                dbgdump(env, "wh_2", wh[:, :, 2, :].bitcast(F32), 64, wlabel=f"outres_{j}")
            env.wait(f"outres_{j}", who="d")
            env.emit("d", lambda e, f=f: e.dma_start(
                out=dout[:, f * TT:(f + 1) * TT].rearrange("(dc p) t -> p dc t", p=P),
                in_=hfb[:]))
            env.chk(f"o{j % 2}", f"dout_{j}", dma=True, via="d")

        def dbgdump(env, label, buf, n3, wlabel=None):
            if kdbg != label:
                return
            env.wait(wlabel or label, who="d")
            dst = ddbg[:, 0, 0:TT] if n3 == -1 else ddbg[:, :, 0:n3]
            env.emit("d", lambda e: e.dma_start(out=dst, in_=buf))
            env.chk("dc", "dbgd", dma=True, via="d")

        def script(env):
            prep(env)
            last_j = min(na, NT_A) - 1
            for j in range(min(na, NT_A)):
                stageA(env, j)
                if has_b(j - 3):
                    stageB_tail(env, j - 3)
                stageA2(env, j)
                if has_b(j - 1):
                    stageB_conv(env, j - 1)
                if has_b(j - 3):
                    stageB_ln4(env, j - 3)
                stageA2b(env, j)
                if j == 2 and ksa >= 6:
                    for oc in range(NDC):
                        mix_chunk(env, 2, oc)
                if has_b(j - 2):
                    stageB_head2a(env, j - 2)
                if has_b(j - 1):
                    stageB_head(env, j - 1)
                if j == 3 and ksa >= 6 and has_mix(3):
                    for oc in range(3, NDC):
                        mix_chunk(env, 3, oc)
                if has_b(j - 2):
                    stageB_head2b(env, j - 2)
                if has_b(j - 3):
                    stageB_dout(env, j - 3)
            for q in (last_j - 2, last_j - 1):
                if has_b(q):
                    if q == last_j - 1:
                        stageB_head2a(env, q)
                        stageB_head2b(env, q)
                    stageB_tail(env, q)
                    stageB_ln4(env, q)
                    stageB_dout(env, q)
            if env.active == "d":
                env.eng.wait_ge(sems["o0"], env.cnt["o0"])
                env.eng.wait_ge(sems["o1"], env.cnt["o1"])

        @block.sync
        def _(sync):
            script(Env("d", sync))

        @block.tensor
        def _(tensor):
            script(Env("t", tensor))

        @block.vector
        def _(vector):
            script(Env("v", vector))

        @block.scalar
        def _(scalar):
            script(Env("s", scalar))

        @block.gpsimd
        def _(gpsimd):
            script(Env("g", gpsimd))

    return nc


_CACHED = {}


def kernel(L0_attn_x, L0_attn_decay_logit, L0_attn_W, L0_attn_b, L0_attn_conv_w,
           L0_attn_conv_b, L0_ffn_base_w, L0_ffn_base_b, L0_ffn_lora_A,
           L0_ffn_lora_B, L0_ffn_router_w, L0_ffn_router_b):
    x = np.ascontiguousarray(np.asarray(L0_attn_x, dtype=np.float32))
    B, T, Dm = x.shape

    def fm(w):
        w = np.asarray(w, dtype=np.float32)
        return np.ascontiguousarray(w.reshape(NDC, P, -1).transpose(1, 0, 2))

    wt = fm(np.asarray(L0_attn_W, dtype=np.float32).T)
    bwt = fm(np.asarray(L0_ffn_base_w, dtype=np.float32).T)
    rt = fm(np.asarray(L0_ffn_router_w, dtype=np.float32).T)
    la = fm(np.asarray(L0_ffn_lora_A, dtype=np.float32))
    lb = np.ascontiguousarray(
        np.asarray(L0_ffn_lora_B, dtype=np.float32).reshape(64, NDC, P))
    cw = fm(np.asarray(L0_attn_conv_w, dtype=np.float32)[:, 0, :])

    def pv(v):
        return np.ascontiguousarray(np.asarray(v, dtype=np.float32).reshape(NDC, P).T)

    dec = pv(L0_attn_decay_logit)
    ba = pv(L0_attn_b)
    cb = pv(L0_attn_conv_b)
    bb = pv(L0_ffn_base_b)
    rb = np.asarray(L0_ffn_router_b, dtype=np.float32).reshape(8, 1)
    invd = np.full((P, P), 1.0 / D, dtype=np.float32)
    onesz = np.ones((8, P), dtype=np.float32)
    brep = np.zeros((8, 64), dtype=np.float32)
    rrep = np.zeros((8, 64), dtype=np.float32)
    for e in range(8):
        for r in range(8):
            brep[e, e * 8 + r] = 1.0
            rrep[r, e * 8 + r] = 1.0

    in_maps = []
    for c in range(8):
        b, q = divmod(c, 4)
        t0 = q * CH
        xin = np.zeros((ROWS, D), dtype=np.float32)
        lo = max(0, t0 - KAP)
        hi = min(T, t0 + CH + TT)
        xin[KAP - (t0 - lo): KAP + (hi - t0)] = x[b, lo:hi]
        in_maps.append({
            "x": np.ascontiguousarray(xin.T), "wt": wt, "bwt": bwt, "rt": rt,
            "la": la, "lb": lb, "cw": cw, "dec": dec, "ba": ba, "cb": cb,
            "bb": bb, "rb": rb, "invd": invd, "invdf": invd, "onesz": onesz,
            "brep": brep, "rrep": rrep,
        })

    import os as _os
    if "nc" not in _CACHED:
        _CACHED["nc"] = build_nc(nb=int(_os.environ.get("KNB", "16")), na=int(_os.environ.get("KNA", str(NT_A))), ksa=int(_os.environ.get("KSA", "9")), kpr=int(_os.environ.get("KPR", "0")), kdbg=_os.environ.get("KDBG", ""), kdbt=int(_os.environ.get("KDBT", "2")))
    res = run_bass_kernel_spmd(_CACHED["nc"], in_maps, core_ids=list(range(8)))

    out = np.empty((B, T, Dm), dtype=np.float32)
    for c in range(8):
        b, q = divmod(c, 4)
        out[b, q * CH:(q + 1) * CH] = np.asarray(res.results[c]["out"]).T
    return out
